# version 02
# baseline (speedup 1.0000x reference)
import os
import numpy as np
import ml_dtypes
import concourse.bass as bass
import concourse.mybir as mybir
from concourse.bass_utils import run_bass_kernel_spmd
from contextlib import ExitStack

F32 = mybir.dt.float32
BF16 = mybir.dt.bfloat16
I32 = mybir.dt.int32
ALU = mybir.AluOpType
AF = mybir.ActivationFunctionType
AX = mybir.AxisListType

D = 1024
S_BUF = 4096
T_OWN = 2048
NT_BUF = 32
NT_OWN = 16
I0 = 2816
F1 = 3072
EXP_SHIFT = -20.0
KILL = -10000.0

_CACHE = {}
KPH = int(os.environ.get("KPH", "99"))
KP2 = int(os.environ.get("KP2", "99"))
KP3 = int(os.environ.get("KP3", "99"))
KP4 = int(os.environ.get("KP4", "99"))
KP6 = int(os.environ.get("KP6", "99"))
KP5 = int(os.environ.get("KP5", "99"))


class Sem:
    __slots__ = ("h", "n")

    def __init__(self, handle):
        self.h = handle
        self.n = 0

    def inc(self, instr, k=1):
        instr.then_inc(self.h, k)
        self.n += k
        return self.n


class VDma:
    __slots__ = ("sems", "toks", "marks")

    def __init__(self, sems):
        self.sems = sems
        self.toks = []
        self.marks = {}

    def start(self, eng, out, in_):
        s = self.sems[len(self.toks) % len(self.sems)]
        eng.wait_ge(s.h, s.n)
        s.inc(eng.dma_start(out=out, in_=in_), 16)
        self.toks.append((s, s.n))
        return len(self.toks)

    def pos(self):
        return len(self.toks)

    def wait(self, eng, upto=None):
        v = len(self.toks) if upto is None else upto
        key = id(eng)
        lo = max(self.marks.get(key, 0), v - len(self.sems))
        for (s, n) in self.toks[lo:v]:
            eng.wait_ge(s.h, n)
        if v > self.marks.get(key, 0):
            self.marks[key] = v


class Slot:
    __slots__ = ("cons", "teng")

    def __init__(self, teng=None):
        self.cons = []
        self.teng = teng

    def acquire(self, eng):
        for sm, v in self.cons:
            eng.wait_ge(sm.h, v)
        self.cons = []

    def consumed(self, sm):
        self.cons.append((sm, sm.n))
        if self.teng is not None:
            self.teng.wait_ge(sm.h, sm.n)


def build():
    nc = bass.Bass("TRN2", target_bir_lowering=False, debug=False,
                   detect_race_conditions=False)

    def inp(name, shape, dt=F32):
        return nc.dram_tensor(name, shape, dt, kind="ExternalInput").ap()

    xin = inp("xin", [S_BUF, D])
    Wq = inp("Wq", [D, D]); Wk = inp("Wk", [D, D]); Wv = inp("Wv", [D, D])
    Wo = inp("Wo", [D, D])
    w2blk = inp("w2blk", [128, 128])
    bq_pp = inp("bq_pp", [128, 8]); bk_pp = inp("bk_pp", [128, 8])
    bo_b = inp("bo_b", [128, D], BF16)
    bv_b = inp("bv_b", [128, D], BF16)
    wgate0 = inp("wgate0", [I0, D]); wup0 = inp("wup0", [I0, D])
    wdown0 = inp("wdown0", [D, I0])
    wrms_mix_pp = inp("wrms_mix_pp", [128, 8])
    wrms_ffn_pp = inp("wrms_ffn_pp", [128, 8])
    Wg1 = inp("Wg1", [D, D]); Wv1 = inp("Wv1", [D, D]); Wd1 = inp("Wd1", [D, D])
    bg_pp = inp("bg_pp", [128, 8]); bv1_pp = inp("bv1_pp", [128, 8])
    bd_pp = inp("bd_pp", [128, 8])
    Wgate1 = inp("Wgate1", [D, F1]); Wup1 = inp("Wup1", [D, F1])
    Wout1 = inp("Wout1", [F1, D])
    wgi = inp("wgi", [F1, D], I32); wui = inp("wui", [F1, D], I32)
    wdi = inp("wdi", [D, F1], I32)
    ident_bf_i = inp("ident_bf", [128, 128], BF16)
    ident_f32_i = inp("ident_f32", [128, 128])
    shift_id_i = inp("shift_id", [64, 128], BF16)
    onesf_i = inp("onesf", [128, 128])
    trimask_i = inp("trimask", [128, 4 * 512], BF16)
    bias_tbl_i = inp("bias_tbl", [128, NT_BUF])
    sel_i = inp("sel", [128, 64])

    yout = nc.dram_tensor("yout", [T_OWN, D], F32, kind="ExternalOutput").ap()
    dbg = nc.dram_tensor("dbg", [128, 8192], F32, kind="ExternalOutput").ap()

    wgq_t = nc.dram_tensor("wgq_t", [8, 128, I0], BF16).ap()
    wuq_t = nc.dram_tensor("wuq_t", [8, 128, I0], BF16).ap()
    wdq_t = nc.dram_tensor("wdq_t", [22, 128, D], BF16).ap()
    wgiT = nc.dram_tensor("wgiT", [8, 128, F1], BF16).ap()
    wuiT = nc.dram_tensor("wuiT", [8, 128, F1], BF16).ap()
    wdiT = nc.dram_tensor("wdiT", [24, 128, D], BF16).ap()
    xn1_sp = nc.dram_tensor("xn1_sp", [8, 128, T_OWN], BF16).ap()
    xnm_sp = nc.dram_tensor("xnm_sp", [8, 128, T_OWN], BF16).ap()
    xq_sp = nc.dram_tensor("xq_sp", [NT_OWN, 128, D], BF16).ap()
    cin_b = nc.dram_tensor("cin_b", [128, 8], F32).ap()
    cout_b = nc.dram_tensor("cout_b", [8 * 128, 8], F32).ap()

    ctx = ExitStack()
    ctx.__enter__()

    def sbt(name, shape, dt):
        return ctx.enter_context(nc.sbuf_tensor(name, shape, dt))

    def pst(name, shape):
        return ctx.enter_context(nc.psum_tensor(name, shape, F32))

    def mksem(name):
        return Sem(ctx.enter_context(nc.semaphore(name)))

    resid = sbt("resid", [128, NT_OWN * D], BF16)
    acc = sbt("acc", [128, NT_OWN * D], BF16)
    arena = sbt("arena", [128, 24704], BF16)
    kg = arena[:, 0:8192]
    vg = arena[:, 8192:16512]
    qcg = arena[:, 16512:20608]
    ylat = arena[:, 20608:24704]
    bigA = arena[:, 0:12288]
    bigB = arena[:, 12288:24576]
    acum = arena[:, 0:16384]
    wbuf = sbt("wbuf", [128, 2 * 8 * 512], BF16)
    xnt4 = sbt("xnt4", [128, 8 * 512], BF16)
    arenaF = sbt("arenaF", [128, 4096], F32)
    xtile = arenaF[:, 0:2048]
    wtile = arenaF[:, 0:1024]
    a_c = arenaF[:, 2048:4096]
    h_c = arenaF[:, 0:2048]
    fs1 = sbt("fs1", [128, D], F32)
    bstage = sbt("bstage", [128, 4 * D], BF16)
    pTb = sbt("pTb", [128, 2 * 1024], BF16)
    stats = sbt("stats", [128, 2 * NT_BUF], F32)
    smc = sbt("smc", [128, 128], F32)
    rden = sbt("rden", [128, 1024], F32)
    consts = sbt("consts", [128, 4 * 512], BF16)
    cbias = sbt("cbias", [128, NT_BUF], F32)
    csel = sbt("csel", [128, 64], F32)
    cid_bf = sbt("cid_bf", [128, 128], BF16)
    cid_f = sbt("cid_f", [128, 128], F32)
    cshift = sbt("cshift", [64, 128], BF16)
    conesf = sbt("conesf", [128, 128], F32)
    cw2 = sbt("cw2", [128, 128], BF16)
    cbq = sbt("cbq", [128, 8], F32)
    cbk = sbt("cbk", [128, 8], F32)
    cbo = sbt("cbo", [128, D], BF16)
    cbv = sbt("cbv", [128, D], BF16)
    cwrm = sbt("cwrm", [128, 8], F32)
    cwrf = sbt("cwrf", [128, 8], F32)
    cbg = sbt("cbg", [128, 8], F32)
    cbv1 = sbt("cbv1", [128, 8], F32)
    cbd = sbt("cbd", [128, 8], F32)
    carry = sbt("carry", [128, 8], F32)
    crecv = sbt("crecv", [128, 64], F32)
    cin_sb = sbt("cin_sb", [128, 8], F32)
    xs_c = sbt("xs_c", [128, T_OWN], BF16)
    qs_u = sbt("qs_u", [128, 1024], F32)
    qs_a = sbt("qs_a", [128, 1024], F32)
    qs_b = sbt("qs_b", [128, 1024], F32)
    qs_s = sbt("qs_s", [128, 1024], F32)
    qs_S = sbt("qs_S", [128, 3 * 64], F32)
    junk = fs1

    psS = pst("psS", [128, 1024])
    psS2 = pst("psS2", [128, 1024])
    psY0 = pst("psY0", [128, 512])
    psY1 = pst("psY1", [128, 512])
    psA1 = ctx.enter_context(nc.psum_tensor("psA1", [128, 1024], BF16))
    psA2 = ctx.enter_context(nc.psum_tensor("psA2", [128, 1024], BF16))

    sLD = mksem("sLD"); sGD = mksem("sGD")
    sST = mksem("sST"); sMM = mksem("sMM"); sV = mksem("sV")
    sSA = mksem("sSA"); sG = mksem("sG"); sCC = mksem("sCC")
    PSY = [mksem(f"pS{i}") for i in range(16)]
    PG = [mksem(f"pG{i}") for i in range(6)]
    PT = [mksem(f"pT{i}") for i in range(4)]
    ALL_SEMS = tuple([sLD, sGD, sST, sMM, sV, sSA, sG, sCC] + PSY + PG + PT)

    block = ctx.enter_context(nc.Block())
    V, SCL, T, SY, G = nc.vector, nc.scalar, nc.tensor, nc.sync, nc.gpsimd
    dS = VDma(PSY)
    dG = VDma(PG)
    dT = VDma(PT)


    slS = Slot(nc.tensor); slS2 = Slot(nc.tensor)
    slY0 = Slot(nc.tensor); slY1 = Slot(nc.tensor)

    @block.sync
    def _(_sync):
        def barrier():
            for e, s in ((V, sV), (SCL, sSA), (T, sMM), (SY, sLD), (G, sG)):
                s.inc(e.nop())
            for e in (V, SCL, T, SY, G):
                for s2 in ALL_SEMS:
                    e.wait_ge(s2.h, s2.n)

        def vflush():
            sV.inc(V.tensor_copy(smc[:, 126:127], smc[:, 125:126]))

        def sflush():
            sSA.inc(SCL.copy(smc[:, 124:125], smc[:, 123:124]))

        def vwait(s, n=None):
            V.wait_ge(s.h, s.n if n is None else n)

        def swait(s, n=None):
            SCL.wait_ge(s.h, s.n if n is None else n)

        def twait(s, n=None):
            T.wait_ge(s.h, s.n if n is None else n)

        def ywait(s, n=None):
            SY.wait_ge(s.h, s.n if n is None else n)

        def gwait(s, n=None):
            G.wait_ge(s.h, s.n if n is None else n)

        def mm_acc(pt, pairs, tp=None, slot=None, first=True, last=True):
            if first:
                twait(sV)
                twait(sSA)
            if slot is not None and first:
                slot.acquire(T)
            n = len(pairs)
            ins = None
            for i, (lh, rh) in enumerate(pairs):
                ins = T.matmul(pt, lh, rh, start=(first and i == 0),
                               stop=(last and i == n - 1), tile_position=tp)
            sMM.inc(ins)
            return sMM.n

        trqA = []

        def transpose128(src_ap, dst_ap, ncols=128):
            k = len(trqA)
            pool = psA1 if k % 2 == 0 else psA2
            pos = (k // 2) % 4
            pt = pool[:, pos * 128:pos * 128 + 128]
            twait(sV)
            twait(sSA)
            if k >= 8:
                twait(sV, trqA[-8])
            sMM.inc(T.transpose(pt, src_ap, cid_bf[:]))
            vwait(sMM)
            sV.inc(V.tensor_copy(dst_ap, pt))
            trqA.append(sV.n)

        def quant_chain(src_ap, dst_ap, n):
            nb = n // 16
            blk = src_ap.rearrange("p (b k) -> p b k", k=16)
            S_ = qs_S[:, 0:nb]
            R_ = qs_S[:, 64:64 + nb]
            C_ = qs_S[:, 128:128 + nb]
            sV.inc(V.tensor_reduce(S_, blk, AX.X, ALU.max,
                                   apply_absolute_value=True))
            vwait(sV)
            sV.inc(V.reciprocal(R_, S_))
            vwait(sV)
            V.tensor_scalar(R_, R_, 6.0, 1e12, ALU.mult, ALU.min)
            V.tensor_scalar(C_, S_, 1.0 / 6.0, 1e-12, ALU.mult, ALU.max)
            V.scalar_tensor_tensor(qs_u[:, 0:n], src_ap, -1.0, src_ap,
                                   ALU.mult, ALU.max)
            V.tensor_tensor(qs_u[:, 0:n].rearrange("p (b k) -> p b k", k=16),
                            qs_u[:, 0:n].rearrange("p (b k) -> p b k", k=16),
                            R_.unsqueeze(2).broadcast_to([128, nb, 16]),
                            ALU.mult)
            swait(sV)
            sSA.inc(SCL.activation(qs_s[:, 0:n], src_ap, AF.Sign))
            V.tensor_single_scalar(qs_a[:, 0:n], qs_u[:, 0:n], 0.25,
                                   ALU.is_gt)
            for c_ in (0.75, 1.25, 1.75):
                V.scalar_tensor_tensor(qs_a[:, 0:n], qs_u[:, 0:n], c_,
                                       qs_a[:, 0:n], ALU.is_gt, ALU.add)
            V.tensor_single_scalar(qs_b[:, 0:n], qs_u[:, 0:n], 2.5, ALU.is_gt)
            for c_ in (3.5, 5.0, 5.0):
                V.scalar_tensor_tensor(qs_b[:, 0:n], qs_u[:, 0:n], c_,
                                       qs_b[:, 0:n], ALU.is_gt, ALU.add)
            V.scalar_tensor_tensor(qs_b[:, 0:n], qs_a[:, 0:n], 0.5,
                                   qs_b[:, 0:n], ALU.mult, ALU.add)
            vwait(sSA)
            V.tensor_tensor(qs_s[:, 0:n].rearrange("p (b k) -> p b k", k=16),
                            qs_s[:, 0:n].rearrange("p (b k) -> p b k", k=16),
                            C_.unsqueeze(2).broadcast_to([128, nb, 16]),
                            ALU.mult)
            sV.inc(V.tensor_tensor(dst_ap, qs_b[:, 0:n], qs_s[:, 0:n],
                                   ALU.mult))

        def task_fp4(dst_hbm, src_hbm, r, width):
            for off in range(0, width, 1024):
                nn = min(1024, width - off)
                ywait(sV)
                ywait(sSA)
                dS.start(SY,
                         out=wtile[:, 0:nn],
                         in_=src_hbm[r * 128:(r + 1) * 128, off:off + nn])
                dS.wait(V)
                dS.wait(SCL)
                vwait(sMM)
                quant_chain(wtile[:, 0:nn], wbuf[:, off:off + nn], nn)
            task_state["free_v"] = sV.n
            dS.wait(V, task_state["free_l"])
            nblk = width // 128
            for b_ in range(nblk):
                transpose128(wbuf[:, b_ * 128:(b_ + 1) * 128],
                             wbuf[:, 2816 + b_ * 128:2816 + (b_ + 1) * 128])
            vflush()
            ywait(sV)
            dS.start(SY,
                     out=dst_hbm[:, :, r * 128:(r + 1) * 128]
                     .rearrange("c p n -> p c n"),
                     in_=wbuf[:, 2816:2816 + nblk * 128]
                     .rearrange("p (c n) -> p c n", n=128))
            task_state["free_l"] = dS.pos()

        def task_int(dst_hbm, src_hbm, r, width):
            gwait(sV)
            dS.wait(G)
            dS.wait(G, task_state["free_l"])
            gwait(sMM)
            dG.start(G, out=wbuf[:, 0:width],
                                in_=src_hbm[r * 128:(r + 1) * 128, :])
            tv = dG.pos()
            nblk = width // 128
            dS.wait(V, task_state["free_l"])
            for b_ in range(nblk):
                dG.wait(T, tv)
                transpose128(wbuf[:, b_ * 128:(b_ + 1) * 128],
                             wbuf[:, 3072 + b_ * 128:3072 + (b_ + 1) * 128])
            vflush()
            ywait(sV)
            dS.start(SY,
                     out=dst_hbm[:, :, r * 128:(r + 1) * 128]
                     .rearrange("c p n -> p c n"),
                     in_=wbuf[:, 3072:3072 + nblk * 128]
                     .rearrange("p (c n) -> p c n", n=128))
            task_state["free_l"] = dS.pos()

        task_state = {"free_v": 0, "free_l": 0}
        tasks = []
        for r in range(22):
            tasks.append(("f", wgq_t, wgate0, r, 1024))
        for r in range(22):
            tasks.append(("f", wuq_t, wup0, r, 1024))
        for r in range(8):
            tasks.append(("f", wdq_t, wdown0, r, 2816))
        for r in range(24):
            tasks.append(("i", wgiT, wgi, r, 1024))
        for r in range(24):
            tasks.append(("i", wuiT, wui, r, 1024))
        for r in range(8):
            tasks.append(("i", wdiT, wdi, r, 3072))
        tasks.reverse()

        def pump(nt):
            for _ in range(nt):
                if not tasks:
                    return
                kind, dh_, sh_, r_, w_ = tasks.pop()
                (task_fp4 if kind == "f" else task_int)(dh_, sh_, r_, w_)


        for dst, src in ((consts, trimask_i), (cbias, bias_tbl_i),
                         (csel, sel_i), (cid_bf, ident_bf_i),
                         (cid_f, ident_f32_i), (cshift, shift_id_i),
                         (conesf, onesf_i), (cbq, bq_pp), (cbk, bk_pp),
                         (cbo, bo_b), (cbv, bv_b), (cwrm, wrms_mix_pp),
                         (cwrf, wrms_ffn_pp), (cbg, bg_pp),
                         (cbv1, bv1_pp), (cbd, bd_pp)):
            dS.start(SY, out=dst[:], in_=src)
        dG.start(G, out=cw2[:], in_=w2blk)
        dS.wait(V)
        dG.wait(V)
        sV.inc(V.memset(vg[:].rearrange("p (j h c) -> p j h c", h=8, c=65)
                        [:, :, :, 64:65], 1.0))
        sV.inc(V.memset(smc[:, 120:128], 0.0))
        if int(os.environ.get("DBGF", "0")) == 8:
            sV.inc(V.memset(smc[:, 0:120], 0.0))
            sV.inc(V.memset(stats[:, 0:64], 0.0))
            sV.inc(V.memset(qs_u[:, 0:48], 0.0))

        def bailout():
            dS.wait(V)
            for tt in range(NT_OWN):
                dT.wait(V)
                sV.inc(V.tensor_copy(xtile[:, 0:D],
                                     acc[:, tt * D:(tt + 1) * D]))
                vflush()
                ywait(sV)
                dT.start(SY, out=yout[tt * 128:(tt + 1) * 128, :],
                         in_=xtile[:, 0:D])
            dT.wait(SY)

        pa_done = {}
        for bt in range(NT_BUF):
            sl = bt % 2
            buf = xtile[:, sl * D:sl * D + D]
            if sl in pa_done:
                ywait(sV, pa_done[sl][0])
                ywait(sSA, pa_done[sl][1])
            dS.start(SY, out=buf,
                                 in_=xin[bt * 128:(bt + 1) * 128, :])
            dS.wait(V)
            dS.wait(SCL)
            sV.inc(V.reduce_sum(smc[:, bt:bt + 1], buf, AX.X))
            swait(sSA)
            sSA.inc(SCL.activation(junk[:], buf, AF.Square,
                                   accum_out=smc[:, 32 + bt:32 + bt + 1]))
            pa_done[sl] = (sV.n, sSA.n)
        vwait(sV)
        vwait(sSA)
        sV.inc(V.tensor_scalar(stats[:, 0:32], smc[:, 0:32], 1.0 / D, None,
                               ALU.mult))
        vwait(sV)
        sV.inc(V.tensor_scalar(smc[:, 64:96], smc[:, 32:64], 1.0 / D, 1e-5,
                               ALU.mult, ALU.add))
        vwait(sV)
        sV.inc(V.tensor_tensor(smc[:, 0:32], stats[:, 0:32], stats[:, 0:32],
                               ALU.mult))
        vwait(sV)
        sV.inc(V.tensor_sub(smc[:, 32:64], smc[:, 64:96], smc[:, 0:32]))
        vwait(sV)
        sV.inc(V.reciprocal(smc[:, 64:96], smc[:, 32:64]))
        vwait(sV)
        vflush()
        vwait(sV)
        swait(sV)
        sSA.inc(SCL.activation(stats[:, 32:64], smc[:, 64:96], AF.Sqrt))
        sflush()
        vwait(sSA)
        swait(sSA)
        pc_done = {}
        xq_st = {}
        _otorder = list(range(NT_OWN))
        if os.environ.get("KREV"):
            _otorder = _otorder[::-1]
        for ot in _otorder:
            bt = 16 + ot
            sl = ot % 2
            buf = xtile[:, sl * D:sl * D + D]
            if sl in pc_done:
                ywait(sV, pc_done[sl][0])
                ywait(sSA, pc_done[sl][1])
            else:
                ywait(sV)
                ywait(sSA)
            dS.start(SY, out=buf,
                                 in_=xin[bt * 128:(bt + 1) * 128, :])
            dS.wait(V)
            dS.wait(SCL)
            sV.inc(V.tensor_copy(resid[:, ot * D:(ot + 1) * D], buf))
            V.tensor_scalar(fs1[:], buf, stats[:, bt:bt + 1],
                            stats[:, 32 + bt:32 + bt + 1],
                            ALU.subtract, ALU.mult)
            if KP6 >= 1:
                dT.wait(V, xq_st.get(sl, 0))
                quant_chain(fs1[:], bstage[:, sl * D:(sl + 1) * D], D)
                vflush()
                ywait(sV)
                dT.start(SY, out=xq_sp[ot, :, :],
                                     in_=bstage[:, sl * D:(sl + 1) * D])
                xq_st[sl] = dT.pos()
            pc_done[sl] = (sV.n, sSA.n)
        barrier()

        if KPH < 2:
            bailout()
            return
        for g in range(4 if KPH >= 4 else 1):
            dG.start(G, 
                out=wbuf[:, 0:2048].rearrange("p (c n) -> p c n", n=256),
                in_=Wk[:, g * 256:(g + 1) * 256]
                    .rearrange("(c p) n -> p c n", p=128))
            dG.start(G, 
                out=wbuf[:, 2048:4096].rearrange("p (c n) -> p c n", n=256),
                in_=Wv[:, g * 256:(g + 1) * 256]
                    .rearrange("(c p) n -> p c n", p=128))
            dG.start(G, 
                out=wbuf[:, 4096:6144].rearrange("p (c n) -> p c n", n=256),
                in_=Wq[:, g * 256:(g + 1) * 256]
                    .rearrange("(c p) n -> p c n", p=128))
            wk_ap = wbuf[:, 0:2048].rearrange("p (c n) -> p c n", n=256)
            wv_ap = wbuf[:, 2048:4096].rearrange("p (c n) -> p c n", n=256)
            wq_ap = wbuf[:, 4096:6144].rearrange("p (c n) -> p c n", n=256)
            bst = {}
            for t4 in range(8):
                for q in range(4):
                    bt = t4 * 4 + q
                    sl = q % 2
                    buf = xtile[:, sl * D:sl * D + D]
                    if sl in bst:
                        ywait(sV, bst[sl])
                    else:
                        ywait(sV)
                        ywait(sSA)
                    dS.start(SY, 
                        out=buf, in_=xin[bt * 128:(bt + 1) * 128, :])
                    dS.wait(V)
                    vwait(sMM)
                    sV.inc(V.tensor_scalar(
                        bstage[:, sl * D:sl * D + D], buf,
                        stats[:, bt:bt + 1], stats[:, 32 + bt:32 + bt + 1],
                        ALU.subtract, ALU.mult))
                    bst[sl] = sV.n
                    for dc in range(8):
                        transpose128(
                            bstage[:, sl * D + dc * 128:sl * D + dc * 128
                                   + 128],
                            xnt4[:, dc * 512 + q * 128:dc * 512 + q * 128
                                 + 128])
                xnt = xnt4[:].rearrange("p (c n) -> p c n", n=512)
                dG.wait(T)
                for p in range(min(2 if KP2 >= 2 else 0,
                                   KP3 - t4 if KP3 < 99 else 2)):
                    pk, sl_ = (psY0, slY0) if p % 2 == 0 else (psY1, slY1)
                    mm_acc(pk[:, 0:512],
                           [(wk_ap[:, dc, p * 128:(p + 1) * 128],
                             xnt[:, dc, :]) for dc in range(8)], slot=sl_)
                    vwait(sMM)
                    if KP4 >= 1:
                        if KP4 == 2:
                            sV.inc(V.tensor_copy(
                                kg[:, p * S_BUF + t4 * 512:p * S_BUF
                                   + (t4 + 1) * 512], pk[:, 0:512]))
                        elif KP4 == 3:
                            sV.inc(V.tensor_copy(kg[:, 0:512],
                                                 pk[:, 0:512]))
                        else:
                            sV.inc(V.tensor_scalar(
                                kg[:, p * S_BUF + t4 * 512:p * S_BUF
                                   + (t4 + 1) * 512],
                                pk[:, 0:512],
                                cbk[:, g * 2 + p:g * 2 + p + 1],
                                None, ALU.add))
                        sl_.consumed(sV)
                        if KP4 == 4:
                            twait(sV)
                for q in range(4 if KP2 >= 3 else 0):
                    j = t4 * 4 + q
                    pk, sl_ = (psY0, slY0) if q % 2 == 0 else (psY1, slY1)
                    mm_acc(pk[:, 0:256],
                           [(xnt[:, dc, q * 128:(q + 1) * 128],
                             wv_ap[:, dc, :]) for dc in range(8)], slot=sl_)
                    vwait(sMM)
                    dst = vg.rearrange("p (j h c) -> p j h c", h=4, c=65)[
                        :, j, :, 0:64]
                    sV.inc(V.tensor_add(
                        dst,
                        pk[:, 0:256].rearrange("p (h c) -> p h c", c=64),
                        cbv[:, g * 256:(g + 1) * 256]
                            .rearrange("p (h c) -> p h c", c=64)))
                    sl_.consumed(sV)
                if t4 >= 4 and KP2 >= 4:
                    for p in range(2):
                        mm_acc(psY0[:, 0:512],
                               [(wq_ap[:, dc, p * 128:(p + 1) * 128],
                                 xnt[:, dc, :]) for dc in range(8)],
                               slot=slY0)
                        vwait(sMM)
                        sV.inc(V.tensor_scalar(
                            bstage[:, 3072:3072 + 512], psY0[:, 0:512],
                            cbq[:, g * 2 + p:g * 2 + p + 1], None, ALU.add))
                        slY0.consumed(sV)
                        twait(sV)
                        mm_acc(psY1[:, 0:512],
                               [(cw2[:], bstage[:, 3072:3072 + 512])],
                               slot=slY1)
                        vwait(sMM)
                        sV.inc(V.tensor_copy(
                            qcg[:, p * T_OWN + (t4 - 4) * 512:
                                p * T_OWN + (t4 - 3) * 512],
                            psY1[:, 0:512]))
                        slY1.consumed(sV)
            barrier()
            if g == int(os.environ.get("DBGP", "-1")):
                _db_off = {"ylat": 20608, "kg": 0, "kg2": 4096,
                           "qcg": 16512}[os.environ.get("DBGW", "ylat")]
                for half in range(4):
                    dS.wait(V)
                    sV.inc(V.tensor_copy(
                        fs1[:, 0:1024],
                        arena[:, _db_off + half * 1024:_db_off
                              + (half + 1) * 1024]))
                    ywait(sV)
                    dS.start(SY, 
                        out=dbg[:, half * 1024:(half + 1) * 1024],
                        in_=fs1[:, 0:1024])
                dS.wait(SY)
            if KPH < 3:
                bailout()
                return
            vhat = vg.rearrange("p (j h c) -> p j h c", h=4, c=65)
            for c in range(4):
                for p in range(2):
                    jmax = 19 + 4 * c
                    expv = {}
                    avv = {}
                    slY0.acquire(T)
                    slY1.acquire(T)
                    for j in range(jmax + 1):
                        pS, slp = (psS, slS) if j % 2 == 0 else (psS2, slS2)
                        twait(sV)
                        twait(sSA)
                        slp.acquire(T)
                        T.matmul(pS[:, 0:512],
                                 kg[0:64, p * S_BUF + j * 128:
                                    p * S_BUF + (j + 1) * 128],
                                 qcg[0:64, p * T_OWN + c * 512:
                                     p * T_OWN + (c + 1) * 512],
                                 start=True, stop=True, tile_position=(0, 0))
                        sMM.inc(T.matmul(
                            pS[:, 512:1024],
                            kg[64:128, p * S_BUF + j * 128:
                               p * S_BUF + (j + 1) * 128],
                            qcg[64:128, p * T_OWN + c * 512:
                                p * T_OWN + (c + 1) * 512],
                            start=True, stop=True, tile_position=(64, 0)))
                        qkv = sMM.n
                        d_ = j - 16 - 4 * c
                        if 0 <= d_ <= 3:
                            vwait(sMM, qkv)
                            sV.inc(V.tensor_tensor(
                                pS[:].rearrange("p (h n) -> p h n", n=512),
                                pS[:].rearrange("p (h n) -> p h n", n=512),
                                consts[:, d_ * 512:(d_ + 1) * 512]
                                .unsqueeze(1).broadcast_to([128, 2, 512]),
                                ALU.add))
                            swait(sV)
                        else:
                            swait(sMM, qkv)
                        slot = j % 2
                        if j >= 2:
                            swait(sMM, avv[j - 2])
                        else:
                            swait(sMM)
                        sSA.inc(SCL.activation(
                            pTb[:, slot * 1024:(slot + 1) * 1024], pS[:],
                            AF.Exp, bias=cbias[:, j:j + 1], scale=0.125))
                        expv[j] = sSA.n
                        slp.consumed(sSA)
                        twait(sSA)
                        twait(sV)
                        T.matmul(psY0[0:65, 0:512],
                                 vhat[:, j, 2 * p, :].opt(),
                                 pTb[:, slot * 1024:slot * 1024 + 512],
                                 start=(j == 0), stop=(j == jmax))
                        sMM.inc(T.matmul(
                            psY1[0:65, 0:512],
                            vhat[:, j, 2 * p + 1, :].opt(),
                            pTb[:, slot * 1024 + 512:(slot + 1) * 1024],
                            start=(j == 0), stop=(j == jmax)))
                        avv[j] = sMM.n
                    vwait(sMM)
                    sV.inc(V.tensor_copy(rden[64:65, 0:512], psY0[64:65, :]))
                    sV.inc(V.tensor_copy(rden[64:65, 512:1024],
                                         psY1[64:65, :]))
                    vflush()
                    ywait(sV)
                    dS.start(SY, out=fs1[0:1, 0:1024],
                                         in_=rden[64:65, 0:1024])
                    dS.wait(V)
                    sV.inc(V.reciprocal(rden[0:1, 0:1024],
                                        fs1[0:1, 0:1024]))
                    vwait(sV)
                    twait(sV)
                    twait(sSA)
                    slS.acquire(T)
                    T.matmul(psS[:, 0:512], conesf[0:1, :],
                             rden[0:1, 0:512], start=True, stop=True)
                    sMM.inc(T.matmul(psS[:, 512:1024], conesf[0:1, :],
                                     rden[0:1, 512:1024], start=True,
                                     stop=True))
                    vwait(sMM)
                    sV.inc(V.tensor_copy(rden[:, 0:1024], psS[:]))
                    slS.consumed(sV)
                    yco = p * T_OWN + c * 512
                    sV.inc(V.tensor_mul(ylat[0:64, yco:yco + 512],
                                        psY0[0:64, :], rden[0:64, 0:512]))
                    slY0.consumed(sV)
                    vwait(sMM)
                    sV.inc(V.tensor_mul(bstage[0:64, 3072:3072 + 512],
                                        psY1[0:64, :], rden[0:64, 512:1024]))
                    slY1.consumed(sV)
                    twait(sV)
                    twait(sSA)
                    slS.acquire(T)
                    sMM.inc(T.matmul(psS[:, 0:512], cshift[:],
                                     bstage[0:64, 3072:3072 + 512],
                                     start=True, stop=True))
                    vwait(sMM)
                    sV.inc(V.tensor_copy(ylat[64:128, yco:yco + 512],
                                         psS[64:128, 0:512]))
                    slS.consumed(sV)
                    pump(4)
            barrier()
            if g == int(os.environ.get("DBGG", "-1")):
                _db_off = {"ylat": 20608, "kg": 0, "kg2": 4096,
                           "qcg": 16512}[os.environ.get("DBGW", "ylat")]
                for half in range(4):
                    dS.wait(V)
                    sV.inc(V.tensor_copy(
                        fs1[:, 0:1024],
                        arena[:, _db_off + half * 1024:_db_off + (half + 1) * 1024]))
                    ywait(sV)
                    dS.start(SY, 
                        out=dbg[:, half * 1024:(half + 1) * 1024],
                        in_=fs1[:, 0:1024])
                dS.wait(SY)
            gwait(sMM)
            dS.wait(G)
            dG.start(G, 
                out=wbuf[:, 0:2048].rearrange("p (c n) -> p c n", n=1024),
                in_=Wo[g * 256:(g + 1) * 256, :]
                    .rearrange("(c p) n -> p c n", p=128))
            dG.wait(T)
            for tt in range(NT_OWN):
                for dh in range(2):
                    pk, sl_ = (psY0, slY0) if (tt * 2 + dh) % 2 == 0 \
                        else (psY1, slY1)
                    mm_acc(pk[:, 0:512],
                           [(ylat[:, pq * T_OWN + tt * 128:
                                  pq * T_OWN + (tt + 1) * 128],
                             wbuf[:, pq * 1024 + dh * 512:
                                  pq * 1024 + (dh + 1) * 512])
                            for pq in range(2)], slot=sl_)
                    vwait(sMM)
                    dst = acc[:, tt * D + dh * 512:tt * D + (dh + 1) * 512]
                    if g == 0:
                        sV.inc(V.tensor_add(dst, pk[:, 0:512],
                                            cbo[:, dh * 512:(dh + 1) * 512]))
                    else:
                        sV.inc(V.tensor_add(dst, pk[:, 0:512], dst))
                    sl_.consumed(sV)
            barrier()

        if KPH < 5:
            bailout()
            return
        pump(len(tasks))
        barrier()

        if KPH < 6:
            bailout()
            return
        if int(os.environ.get("DBGF", "0")) == 7:
            for t_ in range(4):
                dG.start(G, out=dbg[:, t_ * 1024:(t_ + 1) * 1024],
                         in_=xq_sp[int(os.environ.get("DBGO", "12"))
                                   + t_, :, :])
            gwait(sV)
            dG.start(G, out=dbg[:, 4096:5120], in_=bstage[:, 0:1024])
            dG.start(G, out=dbg[:, 5120:6144], in_=bstage[:, 1024:2048])
            dG.wait(G)
        hq = bigA[:].rearrange("p (t n) -> p t n", n=3072)
        ioffs = [(0, 512), (512, 512), (1024, 512), (1536, 512),
                 (2048, 512), (2560, 256)]
        for sup in range(4):
            ywait(sV)
            ywait(sMM)
            for t_ in range(4):
                dS.start(SY, out=bstage[:, t_ * D:(t_ + 1) * D],
                         in_=xq_sp[sup * 4 + t_, :, :])
            dS.wait(T)
            dS.wait(V)
            for q in range(4):
                for dc in range(8):
                    transpose128(bstage[:, q * D + dc * 128:
                                        q * D + dc * 128 + 128],
                                 xnt4[:, dc * 512 + q * 128:
                                      dc * 512 + q * 128 + 128])
            xnt = xnt4[:].rearrange("p (c n) -> p c n", n=512)
            if (sup == int(os.environ.get("DBGSUP", "0"))
                    and int(os.environ.get("DBGF", "0")) in (4, 6)):
                _dsrc = bstage if os.environ["DBGF"] == "6" else xnt4
                for half in range(4):
                    dS.wait(V)
                    vwait(sV)
                    sV.inc(V.tensor_copy(
                        xtile[:, 0:1024],
                        _dsrc[:, half * 1024:(half + 1) * 1024]))
                    ywait(sV)
                    dS.start(SY, out=dbg[:, half * 1024:(half + 1) * 1024],
                             in_=xtile[:, 0:1024])
                dS.wait(SY)
                dS.wait(V)
                dS.wait(SCL)
                dS.wait(T)
                dS.wait(G)
            wld = {}
            for ic, (io, il) in enumerate(ioffs):
                sl = ic % 2
                ywait(sMM)
                dS.start(SY,
                         out=bigB[:, sl * 4096:sl * 4096 + 8 * il]
                         .rearrange("p (c n) -> p c n", n=il),
                         in_=wgq_t[:, :, io:io + il]
                         .rearrange("c p n -> p c n"))
                dS.start(SY,
                         out=wbuf[:, sl * 4096:sl * 4096 + 8 * il]
                         .rearrange("p (c n) -> p c n", n=il),
                         in_=wuq_t[:, :, io:io + il]
                         .rearrange("c p n -> p c n"))
                wg_ap = bigB[:, sl * 4096:sl * 4096 + 8 * il] \
                    .rearrange("p (c n) -> p c n", n=il)
                wu_ap = wbuf[:, sl * 4096:sl * 4096 + 8 * il] \
                    .rearrange("p (c n) -> p c n", n=il)
                dS.wait(T)
                for q in range(4):
                    slS.acquire(T)
                    mm_acc(psS[:, 0:il],
                           [(xnt[:, dc, q * 128:(q + 1) * 128],
                             wg_ap[:, dc, :]) for dc in range(8)])
                    mm_acc(psS[:, 512:512 + il],
                           [(xnt[:, dc, q * 128:(q + 1) * 128],
                             wu_ap[:, dc, :]) for dc in range(8)])
                    wld[sl] = sMM.n
                    swait(sMM)
                    swait(sV)
                    sSA.inc(SCL.activation(pTb[:, 0:il], psS[:, 0:il],
                                           AF.Silu))
                    slS.consumed(sSA)
                    vwait(sMM)
                    vwait(sSA)
                    _dbf = int(os.environ.get("DBGF", "0"))
                    if _dbf == 2:
                        sV.inc(V.tensor_copy(fs1[:, 0:il], psS[:, 0:il]))
                    elif _dbf == 3:
                        sV.inc(V.tensor_copy(fs1[:, 0:il],
                                             psS[:, 512:512 + il]))
                    else:
                        sV.inc(V.tensor_tensor(fs1[:, 0:il],
                                               psS[:, 512:512 + il],
                                               pTb[:, 0:il], ALU.mult))
                    slS.consumed(sV)
                    vwait(sV)
                    if _dbf in (2, 3):
                        sV.inc(V.tensor_copy(hq[:, q, io:io + il],
                                             fs1[:, 0:il]))
                    else:
                        quant_chain(fs1[:, 0:il], hq[:, q, io:io + il], il)
            if (sup == int(os.environ.get("DBGSUP", "0"))
                    and int(os.environ.get("DBGF", "0")) == 1):
                for half in range(3):
                    dS.wait(V)
                    vwait(sV)
                    sV.inc(V.tensor_copy(
                        xtile[:, 0:1024],
                        bigA[:, int(os.environ.get("DBGQ", "0")) * 3072
                             + half * 1024:
                             int(os.environ.get("DBGQ", "0")) * 3072
                             + (half + 1) * 1024]))
                    ywait(sV)
                    dS.start(SY, out=dbg[:, half * 1024:(half + 1) * 1024],
                             in_=xtile[:, 0:1024])
                dS.wait(SY)
                dS.wait(V)
                dS.wait(SCL)
                dS.wait(T)
                dS.wait(G)
            for dh in range(2):
                ywait(sMM)
                dS.start(SY, 
                    out=bigB[:, 0:22 * 512].rearrange("p (c n) -> p c n",
                                                      n=512),
                    in_=wdq_t[:, :, dh * 512:(dh + 1) * 512]
                        .rearrange("c p n -> p c n"))
                wd_ap = bigB[:, 0:22 * 512].rearrange("p (c n) -> p c n",
                                                      n=512)
                for q in range(4):
                    tt = sup * 4 + q
                    for icc in range(22):
                        transpose128(hq[:, q, icc * 128:(icc + 1) * 128],
                                     xnt4[:, icc * 128:(icc + 1) * 128])
                    twait(sV)
                    dS.wait(T)
                    pk, sl_ = (psY0, slY0) if q % 2 == 1 else (psY1, slY1)
                    mm_acc(pk[:, 0:512],
                           [(xnt4[:, icc * 128:(icc + 1) * 128],
                             wd_ap[:, icc, :]) for icc in range(22)],
                           slot=sl_)
                    vwait(sMM)
                    dst = acc[:, tt * D + dh * 512:tt * D + (dh + 1) * 512]
                    sV.inc(V.tensor_add(dst, pk[:, 0:512], dst))
                    sl_.consumed(sV)
        barrier()

        if KPH < 7:
            bailout()
            return
        for tt in range(NT_OWN):
            r_ = resid[:, tt * D:(tt + 1) * D]
            sV.inc(V.tensor_add(r_, r_, acc[:, tt * D:(tt + 1) * D]))
            vflush()
            vwait(sV)
            swait(sV)
            sV.inc(V.reduce_sum(smc[:, tt:tt + 1], r_, AX.X))
            swait(sSA)
            sSA.inc(SCL.activation(junk[:], r_, AF.Square,
                                   accum_out=smc[:, 32 + tt:32 + tt + 1]))
        vwait(sV)
        sflush()
        vwait(sSA)
        if int(os.environ.get("DBGF", "0")) == 8:
            sV.inc(V.tensor_copy(qs_u[:, 0:48], smc[:, 0:48]))
            vwait(sV)
        sV.inc(V.tensor_scalar(stats[:, 0:16], smc[:, 0:16], 1.0 / D, None,
                               ALU.mult))
        vwait(sV)
        sV.inc(V.tensor_scalar(smc[:, 64:80], smc[:, 32:48], 1.0 / D, 1e-5,
                               ALU.mult, ALU.add))
        vwait(sV)
        sV.inc(V.tensor_tensor(smc[:, 0:16], stats[:, 0:16], stats[:, 0:16],
                               ALU.mult))
        vwait(sV)
        sV.inc(V.tensor_sub(smc[:, 32:48], smc[:, 64:80], smc[:, 0:16]))
        vwait(sV)
        sV.inc(V.reciprocal(smc[:, 64:80], smc[:, 32:48]))
        vwait(sV)
        vflush()
        vwait(sV)
        swait(sV)
        sSA.inc(SCL.activation(stats[:, 32:48], smc[:, 64:80], AF.Sqrt))
        sflush()
        if KP5 < 2:
            bailout()
            return
        vwait(sSA)
        swait(sSA)
        for tt in range(NT_OWN):
            sV.inc(V.tensor_scalar(acc[:, tt * D:(tt + 1) * D],
                                   resid[:, tt * D:(tt + 1) * D],
                                   stats[:, tt:tt + 1],
                                   stats[:, 32 + tt:32 + tt + 1],
                                   ALU.subtract, ALU.mult))
            vflush()
            swait(sV)
            swait(sSA)
            sSA.inc(SCL.activation(junk[:], acc[:, tt * D:(tt + 1) * D],
                                   AF.Square,
                                   accum_out=smc[:, 96 + tt:96 + tt + 1]))
        sflush()
        vwait(sSA)
        sV.inc(V.tensor_scalar(smc[:, 64:80], smc[:, 96:112], 1.0 / D, 1e-6,
                               ALU.mult, ALU.add))
        vwait(sV)
        sV.inc(V.reciprocal(smc[:, 80:96], smc[:, 64:80]))
        vwait(sV)
        vflush()
        vwait(sV)
        swait(sV)
        sSA.inc(SCL.activation(stats[:, 48:64], smc[:, 80:96], AF.Sqrt))
        sflush()
        vwait(sSA)
        for tt in range(NT_OWN):
            for dc in range(8):
                transpose128(acc[:, tt * D + dc * 128:tt * D + dc * 128
                                 + 128],
                             bstage[:, dc * 128:(dc + 1) * 128])
            vflush()
            ywait(sV)
            dS.start(SY,
                out=xn1_sp[:, :, tt * 128:(tt + 1) * 128]
                    .rearrange("c p n -> p c n"),
                in_=bstage[:, 0:1024].rearrange("p (c n) -> p c n", n=128))
            vwait(sMM)
            sV.inc(V.tensor_scalar(bstage[:, 1024:2048],
                                   acc[:, tt * D:(tt + 1) * D],
                                   stats[:, 48 + tt:48 + tt + 1], None,
                                   ALU.mult))
            dT.wait(V)
            for dc in range(8):
                transpose128(bstage[:, 1024 + dc * 128:1024 + dc * 128
                                    + 128], pTb[:, 0:128])
                sV.inc(V.tensor_scalar(
                    bstage[:, 2048 + dc * 128:2048 + (dc + 1) * 128],
                    pTb[:, 0:128], cwrm[:, dc:dc + 1], None, ALU.mult))
            vflush()
            ywait(sV)
            dT.start(SY,
                out=xnm_sp[:, :, tt * 128:(tt + 1) * 128]
                    .rearrange("c p n -> p c n"),
                in_=bstage[:, 2048:3072].rearrange("p (c n) -> p c n",
                                                   n=128))
        barrier()

        if int(os.environ.get("DBGF", "0")) == 8:
            vwait(sSA)
            sV.inc(V.tensor_copy(xtile[:, 0:64], stats[:, 0:64]))
            sV.inc(V.tensor_copy(xtile[:, 64:192], smc[:, 0:128]))
            sV.inc(V.tensor_copy(xtile[:, 192:240], qs_u[:, 0:48]))
            vflush()
            ywait(sV)
            dS.start(SY, out=dbg[:, 0:240], in_=xtile[:, 0:240])
            dS.wait(SY)
        if KP5 < 3:
            bailout()
            return
        gvd_ld = {}
        for cc in range(8):
            sl = cc % 2
            if sl in gvd_ld:
                gwait(sMM, gvd_ld[sl])
            wga = wbuf[:, sl * 4096:sl * 4096 + 8 * 384] \
                .rearrange("p (c n) -> p c n", n=384)
            dG.start(G, 
                out=wga[:, :, 0:128],
                in_=Wg1[:, cc * 128:(cc + 1) * 128]
                    .rearrange("(c p) n -> p c n", p=128))
            dG.start(G, 
                out=wga[:, :, 128:256],
                in_=Wv1[:, cc * 128:(cc + 1) * 128]
                    .rearrange("(c p) n -> p c n", p=128))
            dG.start(G, 
                out=wga[:, :, 256:384],
                in_=Wd1[:, cc * 128:(cc + 1) * 128]
                    .rearrange("(c p) n -> p c n", p=128))
            dG.wait(T)
            for st4 in range(4):
                ywait(sMM)
                dS.start(SY, 
                    out=xnt4[:].rearrange("p (c n) -> p c n", n=512),
                    in_=xnm_sp[:, :, st4 * 512:(st4 + 1) * 512]
                        .rearrange("c p n -> p c n"))
                xnt = xnt4[:].rearrange("p (c n) -> p c n", n=512)
                dS.wait(T)
                slS.acquire(T)
                slS2.acquire(T)
                mm_acc(psS[:, 0:512],
                       [(wga[:, dc, 0:128], xnt[:, dc, :])
                        for dc in range(8)])
                mm_acc(psS[:, 512:1024],
                       [(wga[:, dc, 128:256], xnt[:, dc, :])
                        for dc in range(8)])
                mm_acc(psS2[:, 0:512],
                       [(wga[:, dc, 256:384], xnt[:, dc, :])
                        for dc in range(8)])
                gvd_ld[sl] = sMM.n
                swait(sMM)
                swait(sV)
                sSA.inc(SCL.activation(pTb[:, 0:512], psS[:, 0:512],
                                       AF.Sigmoid, bias=cbg[:, cc:cc + 1]))
                sSA.inc(SCL.activation(pTb[:, 512:1024], psS[:, 512:1024],
                                       AF.Tanh, bias=cbv1[:, cc:cc + 1]))
                sSA.inc(SCL.activation(fs1[:, 0:512], psS2[:, 0:512],
                                       AF.Sigmoid, bias=cbd[:, cc:cc + 1]))
                slS.consumed(sSA)
                slS2.consumed(sSA)
                vwait(sSA)
                sV.inc(V.tensor_mul(xs_c[:, st4 * 512:(st4 + 1) * 512],
                                    pTb[:, 0:512], pTb[:, 512:1024]))
                sV.inc(V.tensor_scalar(a_c[:, st4 * 512:(st4 + 1) * 512],
                                       fs1[:, 0:512], 0.998, 0.001,
                                       ALU.mult, ALU.add))
            vwait(sMM)
            sV.inc(V.tensor_tensor_scan(h_c[:], a_c[:], xs_c[:], 0.0,
                                        ALU.mult, ALU.add))
            vwait(sV)
            sV.inc(V.tensor_copy(carry[:, cc:cc + 1], h_c[:, 2047:2048]))
            sV.inc(V.tensor_tensor_scan(
                acum[:, cc * T_OWN:(cc + 1) * T_OWN], a_c[:], a_c[:], 1.0,
                ALU.mult, ALU.bypass))
            vwait(sV)
            sV.inc(V.tensor_copy(xs_c[:], h_c[:]))
            for tt in range(NT_OWN):
                transpose128(xs_c[:, tt * 128:(tt + 1) * 128], pTb[:, 0:128])
                dst = acc[:, tt * D + cc * 128:tt * D + cc * 128 + 128]
                sV.inc(V.tensor_add(dst, pTb[:, 0:128], dst))
        barrier()
        if KP5 < 4:
            bailout()
            return
        dG.start(G, out=cin_b, in_=carry[:])
        dG.wait(G)
        sCC.inc(G.collective_compute(
            "AllGather", ALU.bypass, replica_groups=[list(range(8))],
            ins=[cin_b.opt()], outs=[cout_b.opt()]))
        gwait(sCC)
        dG.start(G, 
            out=crecv[:].rearrange("p (r c) -> p r c", c=8),
            in_=cout_b.rearrange("(r p) c -> p r c", p=128))
        dG.wait(V)
        sV.inc(V.tensor_mul(crecv[:], crecv[:], csel[:]))
        vwait(sV)
        sV.inc(V.tensor_reduce(cin_sb[:], crecv[:]
                               .rearrange("p (r c) -> p c r", c=8),
                               AX.X, ALU.add))
        vwait(sV)
        for cc in range(8):
            vwait(sMM)
            sV.inc(V.tensor_scalar(xs_c[:],
                                   acum[:, cc * T_OWN:(cc + 1) * T_OWN],
                                   cin_sb[:, cc:cc + 1], None, ALU.mult))
            for tt in range(NT_OWN):
                transpose128(xs_c[:, tt * 128:(tt + 1) * 128], pTb[:, 0:128])
                dst = acc[:, tt * D + cc * 128:tt * D + cc * 128 + 128]
                sV.inc(V.tensor_add(dst, pTb[:, 0:128], dst))
        barrier()
        if KP5 < 5:
            bailout()
            return
        for tt in range(NT_OWN):
            swait(sV)
            swait(sSA)
            sSA.inc(SCL.activation(junk[:], acc[:, tt * D:(tt + 1) * D],
                                   AF.Square, accum_out=smc[:, tt:tt + 1]))
        sflush()
        vwait(sSA)
        sV.inc(V.tensor_scalar(smc[:, 32:48], smc[:, 0:16], 1.0 / D, 1e-6,
                               ALU.mult, ALU.add))
        vwait(sV)
        sV.inc(V.reciprocal(smc[:, 64:80], smc[:, 32:48]))
        vwait(sV)
        vflush()
        vwait(sV)
        swait(sV)
        sSA.inc(SCL.activation(stats[:, 0:16], smc[:, 64:80], AF.Sqrt))
        sflush()
        barrier()

        def emit_ffn(variant):
            ffn_ev = {}
            prod_v = [0]
            for sup in range(4):
                if variant == "A":
                    for q in range(4):
                        tt = sup * 4 + q
                        vwait(sMM)
                        sV.inc(V.tensor_scalar(
                            bstage[:, 0:D], acc[:, tt * D:(tt + 1) * D],
                            stats[:, tt:tt + 1], None, ALU.mult))
                        for dc in range(8):
                            transpose128(bstage[:, dc * 128:(dc + 1) * 128],
                                         pTb[:, 0:128])
                            sV.inc(V.tensor_scalar(
                                xnt4[:, dc * 512 + q * 128:
                                     dc * 512 + q * 128 + 128],
                                pTb[:, 0:128], cwrf[:, dc:dc + 1], None,
                                ALU.mult))
                else:
                    ywait(sMM)
                    dS.start(SY, 
                        out=xnt4[:].rearrange("p (c n) -> p c n", n=512),
                        in_=xn1_sp[:, :, sup * 512:(sup + 1) * 512]
                            .rearrange("c p n -> p c n"))
                xnt = xnt4[:].rearrange("p (c n) -> p c n", n=512)
                for fh in range(2):
                    if variant == "A":
                        gwait(sMM)
                    else:
                        ywait(sMM)
                    for f12 in range(12):
                        f = fh * 12 + f12
                        if variant == "A":
                            dG.start(G, 
                                out=bigB[:, f12 * 1024:(f12 + 1) * 1024],
                                in_=Wout1[f * 128:(f + 1) * 128, :])
                        else:
                            dS.start(SY, 
                                out=bigB[:, f12 * 1024:(f12 + 1) * 1024],
                                in_=wdiT[f, :, :])
                    for f12 in range(12):
                        f = fh * 12 + f12
                        sl = f12 % 2
                        wgu = wbuf[:, sl * 4096:sl * 4096 + 8 * 256] \
                            .rearrange("p (c n) -> p c n", n=256)
                        if variant == "A":
                            gwait(sMM)
                            dG.start(G, 
                                out=wgu[:, :, 0:128],
                                in_=Wgate1[:, f * 128:(f + 1) * 128]
                                    .rearrange("(c p) n -> p c n", p=128))
                            dG.start(G, 
                                out=wgu[:, :, 128:256],
                                in_=Wup1[:, f * 128:(f + 1) * 128]
                                    .rearrange("(c p) n -> p c n", p=128))
                            dG.wait(T)
                        else:
                            ywait(sMM)
                            dS.start(SY, 
                                out=wgu[:, :, 0:128],
                                in_=wgiT[:, :, f * 128:(f + 1) * 128]
                                    .rearrange("c p n -> p c n"))
                            dS.start(SY, 
                                out=wgu[:, :, 128:256],
                                in_=wuiT[:, :, f * 128:(f + 1) * 128]
                                    .rearrange("c p n -> p c n"))
                            dS.wait(T)
                        twait(sV, prod_v[0])
                        slS.acquire(T)
                        mm_acc(psS[:, 0:512],
                               [(wgu[:, dc, 0:128], xnt[:, dc, :])
                                for dc in range(8)])
                        mm_acc(psS[:, 512:1024],
                               [(wgu[:, dc, 128:256], xnt[:, dc, :])
                                for dc in range(8)])
                        swait(sMM)
                        swait(sV)
                        sSA.inc(SCL.activation(
                            pTb[:, 0:512], psS[:, 0:512], AF.Silu,
                            scale=(1.0 if variant == "A" else 1.0 / 127.0)))
                        slS.consumed(sSA)
                        vwait(sMM)
                        vwait(sSA)
                        sV.inc(V.scalar_tensor_tensor(
                            bigA[:, f12 * 512:(f12 + 1) * 512],
                            psS[:, 512:1024],
                            (1.0 if variant == "A" else 1.0 / 16129.0),
                            pTb[:, 0:512], ALU.mult, ALU.mult))
                        slS.consumed(sV)
                        prod_v[0] = sV.n
                    prod_all = sV.n
                    if (int(os.environ.get("DBGF", "0")) == 12
                            and variant == "A" and sup == 0
                            and fh == int(os.environ.get("DBGFH", "0"))):
                        dG.wait(T)
                        dS.wait(T)
                        gwait(sMM)
                        dG.start(G, out=dbg[:, 0:8192], in_=bigB[:, 0:8192])
                        dG.wait(V)
                        dG.wait(T)
                    if (int(os.environ.get("DBGF", "0")) in (10, 11)
                            and variant == ("B" if int(os.environ.get(
                                "DBGF", "0")) == 10 else "A") and sup == 0
                            and fh == int(os.environ.get("DBGFH", "0"))):
                        vflush()
                        gwait(sV)
                        dG.start(G, out=dbg[:, 0:6144], in_=bigA[:, 0:6144])
                        dG.wait(V)
                        dG.wait(T)
                    dS.wait(T)
                    dG.wait(T)
                    for tt4 in range(4):
                        tt = sup * 4 + tt4
                        for dh in range(2):
                            pk, sl_ = (psY0, slY0) \
                                if (tt4 * 2 + dh) % 2 == 0 else (psY1, slY1)
                            twait(sV, prod_all)
                            mm_acc(pk[:, 0:512],
                                   [(bigA[:, f12 * 512 + tt4 * 128:
                                          f12 * 512 + tt4 * 128 + 128],
                                     bigB[:, f12 * 1024 + dh * 512:
                                          f12 * 1024 + (dh + 1) * 512])
                                    for f12 in range(12)], slot=sl_)
                            vwait(sMM)
                            if variant == "A":
                                dst = acc[:, tt * D + dh * 512:
                                          tt * D + (dh + 1) * 512]
                                sV.inc(V.tensor_add(dst, pk[:, 0:512], dst))
                            else:
                                dst = arenaF[:, tt4 * D + dh * 512:
                                             tt4 * D + (dh + 1) * 512]
                                if fh == 0:
                                    dT.wait(V)
                                    sV.inc(V.tensor_add(
                                        dst, pk[:, 0:512],
                                        acc[:, tt * D + dh * 512:
                                            tt * D + (dh + 1) * 512]))
                                else:
                                    sV.inc(V.tensor_add(dst, pk[:, 0:512],
                                                        dst))
                                    sV.inc(V.tensor_add(
                                        dst, dst,
                                        resid[:, tt * D + dh * 512:
                                              tt * D + (dh + 1) * 512]))
                                    if dh == 1:
                                        vflush()
                                        ywait(sV)
                                        dT.start(SY,
                                            out=yout[tt * 128:
                                                     (tt + 1) * 128, :],
                                            in_=arenaF[:, tt4 * D:
                                                       (tt4 + 1) * D])
                            sl_.consumed(sV)

        if KPH < 8:
            bailout()
            return
        emit_ffn("A")
        barrier()
        if int(os.environ.get("DBGF", "0")) == 13:
            dG.start(G, out=dbg[:, 0:8192], in_=acc[:, 0:8192])
            dG.wait(V)
            dG.wait(T)
        if KPH < 9:
            bailout()
            return
        emit_ffn("B")
        dT.wait(SY)
        dT.wait(G)
        dT.wait(V)
        dT.wait(SCL)
        dT.wait(T)

    ctx.close()
    return nc


def _prep_inputs(inputs):
    x = np.asarray(inputs["L0_attn_x"])
    f32 = np.float32
    bf16 = ml_dtypes.bfloat16

    def pp(v):
        return np.ascontiguousarray(np.asarray(v, dtype=f32).reshape(8, 128).T)

    def bcast(v):
        return np.ascontiguousarray(
            np.broadcast_to(np.asarray(v, dtype=f32)[None, :], (128, D)))

    wc = np.asarray(inputs["L0_attn_Wc"], dtype=f32)
    w2 = np.zeros((128, 128), f32)
    w2[0:64, 0:64] = wc
    w2[64:128, 64:128] = wc

    ident = np.eye(128, dtype=f32)
    shift_id = np.zeros((64, 128), f32)
    shift_id[np.arange(64), 64 + np.arange(64)] = 1.0

    tm = np.zeros((128, 4, 512), f32)
    for d_ in range(4):
        for kr in range(128):
            lim = d_ * 128 + kr
            tm[kr, d_, :lim] = KILL
    tm = np.ascontiguousarray(tm.reshape(128, 4 * 512))

    base = dict(
        Wq=np.ascontiguousarray(inputs["L0_attn_Wq"], f32),
        Wk=np.ascontiguousarray(inputs["L0_attn_Wk"], f32),
        Wv=np.ascontiguousarray(inputs["L0_attn_Wv"], f32),
        Wo=np.ascontiguousarray(inputs["L0_attn_Wo"], f32),
        w2blk=w2,
        bq_pp=pp(inputs["L0_attn_bq"]),
        bk_pp=pp(inputs["L0_attn_bk"]),
        bo_b=bcast(inputs["L0_attn_bo"]).astype(bf16),
        bv_b=bcast(inputs["L0_attn_bv"]).astype(bf16),
        wgate0=np.ascontiguousarray(inputs["L0_ffn_w_gate"], f32),
        wup0=np.ascontiguousarray(inputs["L0_ffn_w_up"], f32),
        wdown0=np.ascontiguousarray(inputs["L0_ffn_w_down"], f32),
        wrms_mix_pp=pp(inputs["L1_attn_w_rms_mix"]),
        wrms_ffn_pp=pp(inputs["L1_attn_w_rms_ffn"]),
        Wg1=np.ascontiguousarray(inputs["L1_attn_Wg"], f32),
        Wv1=np.ascontiguousarray(inputs["L1_attn_Wv"], f32),
        Wd1=np.ascontiguousarray(inputs["L1_attn_Wd"], f32),
        bg_pp=pp(inputs["L1_attn_bg"]),
        bv1_pp=pp(inputs["L1_attn_bv"]),
        bd_pp=pp(inputs["L1_attn_bd"]),
        Wgate1=np.ascontiguousarray(inputs["L1_attn_W_gate"], f32),
        Wup1=np.ascontiguousarray(inputs["L1_attn_W_up"], f32),
        Wout1=np.ascontiguousarray(inputs["L1_attn_W_out"], f32),
        wgi=np.ascontiguousarray(inputs["L1_ffn_w_gate"], np.int32),
        wui=np.ascontiguousarray(inputs["L1_ffn_w_up"], np.int32),
        wdi=np.ascontiguousarray(inputs["L1_ffn_w_down"], np.int32),
        ident_bf=ident.astype(bf16),
        ident_f32=ident,
        shift_id=shift_id.astype(bf16),
        onesf=np.ones((128, 128), f32),
        trimask=tm.astype(bf16),
    )

    in_maps = []
    for c in range(8):
        b, h = c // 2, c % 2
        if h == 0:
            xb = np.concatenate([np.zeros((T_OWN, D), f32),
                                 np.asarray(x[b, 0:T_OWN], f32)], axis=0)
        else:
            xb = np.ascontiguousarray(np.asarray(x[b], f32))
        bias = np.full((128, NT_BUF), EXP_SHIFT, f32)
        if h == 0:
            bias[:, 0:16] = KILL + EXP_SHIFT
        sel = np.zeros((128, 64), f32)
        if h == 1:
            partner = c - 1
            sel[:, partner * 8:(partner + 1) * 8] = 1.0
        m = dict(base)
        m.update(xin=xb, bias_tbl=bias, sel=sel)
        in_maps.append(m)
    return in_maps


def kernel(**inputs):
    if "nc" not in _CACHE:
        _CACHE["nc"] = build()
    nc = _CACHE["nc"]
    in_maps = _prep_inputs(inputs)
    res = run_bass_kernel_spmd(nc, in_maps, core_ids=list(range(8)))
    out = np.empty((4, S_BUF, D), np.float32)
    for c in range(8):
        b, h = c // 2, c % 2
        out[b, h * T_OWN:(h + 1) * T_OWN, :] = res.results[c]["yout"]
    return out


# version 04
# speedup vs baseline: 1.0419x; 1.0419x over previous
import os
import numpy as np
import ml_dtypes
import concourse.bass as bass
import concourse.mybir as mybir
from concourse.bass_utils import run_bass_kernel_spmd
from contextlib import ExitStack

F32 = mybir.dt.float32
BF16 = mybir.dt.bfloat16
I32 = mybir.dt.int32
ALU = mybir.AluOpType
AF = mybir.ActivationFunctionType
AX = mybir.AxisListType

D = 1024
S_BUF = 4096
T_OWN = 2048
NT_BUF = 32
NT_OWN = 16
I0 = 2816
F1 = 3072
EXP_SHIFT = -20.0
KILL = -10000.0

_CACHE = {}
KPH = int(os.environ.get("KPH", "99"))
KP2 = int(os.environ.get("KP2", "99"))
KP3 = int(os.environ.get("KP3", "99"))
KP4 = int(os.environ.get("KP4", "99"))
KP6 = int(os.environ.get("KP6", "99"))
KP5 = int(os.environ.get("KP5", "99"))


class Sem:
    __slots__ = ("h", "n")

    def __init__(self, handle):
        self.h = handle
        self.n = 0

    def inc(self, instr, k=1):
        instr.then_inc(self.h, k)
        self.n += k
        return self.n


class VDma:
    __slots__ = ("sems", "toks", "marks")

    def __init__(self, sems):
        self.sems = sems
        self.toks = []
        self.marks = {}

    def start(self, eng, out, in_):
        s = self.sems[len(self.toks) % len(self.sems)]
        eng.wait_ge(s.h, s.n)
        s.inc(eng.dma_start(out=out, in_=in_), 16)
        self.toks.append((s, s.n))
        return len(self.toks)

    def pos(self):
        return len(self.toks)

    def wait(self, eng, upto=None):
        v = len(self.toks) if upto is None else upto
        key = id(eng)
        lo = max(self.marks.get(key, 0), v - len(self.sems))
        for (s, n) in self.toks[lo:v]:
            eng.wait_ge(s.h, n)
        if v > self.marks.get(key, 0):
            self.marks[key] = v


class Slot:
    __slots__ = ("cons", "teng")

    def __init__(self, teng=None):
        self.cons = []
        self.teng = teng

    def acquire(self, eng):
        for sm, v in self.cons:
            eng.wait_ge(sm.h, v)
        self.cons = []

    def consumed(self, sm):
        self.cons.append((sm, sm.n))
        if self.teng is not None:
            self.teng.wait_ge(sm.h, sm.n)


def build():
    nc = bass.Bass("TRN2", target_bir_lowering=False, debug=False,
                   detect_race_conditions=False)

    def inp(name, shape, dt=F32):
        return nc.dram_tensor(name, shape, dt, kind="ExternalInput").ap()

    xin = inp("xin", [S_BUF, D])
    Wq = inp("Wq", [D, D]); Wk = inp("Wk", [D, D]); Wv = inp("Wv", [D, D])
    Wo = inp("Wo", [D, D])
    w2blk = inp("w2blk", [128, 128])
    bq_pp = inp("bq_pp", [128, 8]); bk_pp = inp("bk_pp", [128, 8])
    bo_b = inp("bo_b", [128, D], BF16)
    bv_b = inp("bv_b", [128, D], BF16)
    wgate0 = inp("wgate0", [I0, D]); wup0 = inp("wup0", [I0, D])
    wdown0 = inp("wdown0", [D, I0])
    wrms_mix_pp = inp("wrms_mix_pp", [128, 8])
    wrms_ffn_pp = inp("wrms_ffn_pp", [128, 8])
    Wg1 = inp("Wg1", [D, D]); Wv1 = inp("Wv1", [D, D]); Wd1 = inp("Wd1", [D, D])
    bg_pp = inp("bg_pp", [128, 8]); bv1_pp = inp("bv1_pp", [128, 8])
    bd_pp = inp("bd_pp", [128, 8])
    Wgate1 = inp("Wgate1", [D, F1]); Wup1 = inp("Wup1", [D, F1])
    Wout1 = inp("Wout1", [F1, D])
    wgi = inp("wgi", [F1, D], I32); wui = inp("wui", [F1, D], I32)
    wdi = inp("wdi", [D, F1], I32)
    ident_bf_i = inp("ident_bf", [128, 128], BF16)
    ident_f32_i = inp("ident_f32", [128, 128])
    shift_id_i = inp("shift_id", [64, 128], BF16)
    onesf_i = inp("onesf", [128, 128])
    trimask_i = inp("trimask", [128, 4 * 512], BF16)
    bias_tbl_i = inp("bias_tbl", [128, NT_BUF])
    sel_i = inp("sel", [128, 64])

    yout = nc.dram_tensor("yout", [T_OWN, D], F32, kind="ExternalOutput").ap()
    dbg = nc.dram_tensor("dbg", [128, 8192], F32, kind="ExternalOutput").ap()

    wgq_t = nc.dram_tensor("wgq_t", [8, 128, I0], BF16).ap()
    wuq_t = nc.dram_tensor("wuq_t", [8, 128, I0], BF16).ap()
    wdq_t = nc.dram_tensor("wdq_t", [22, 128, D], BF16).ap()
    wgiT = nc.dram_tensor("wgiT", [8, 128, F1], BF16).ap()
    wuiT = nc.dram_tensor("wuiT", [8, 128, F1], BF16).ap()
    wdiT = nc.dram_tensor("wdiT", [24, 128, D], BF16).ap()
    xn1_sp = nc.dram_tensor("xn1_sp", [8, 128, T_OWN], BF16).ap()
    xnm_sp = nc.dram_tensor("xnm_sp", [8, 128, T_OWN], BF16).ap()
    xq_sp = nc.dram_tensor("xq_sp", [NT_OWN, 128, D], BF16).ap()
    cin_b = nc.dram_tensor("cin_b", [128, 8], F32).ap()
    cout_b = nc.dram_tensor("cout_b", [8 * 128, 8], F32).ap()

    ctx = ExitStack()
    ctx.__enter__()

    def sbt(name, shape, dt):
        return ctx.enter_context(nc.sbuf_tensor(name, shape, dt))

    def pst(name, shape):
        return ctx.enter_context(nc.psum_tensor(name, shape, F32))

    def mksem(name):
        return Sem(ctx.enter_context(nc.semaphore(name)))

    resid = sbt("resid", [128, NT_OWN * D], BF16)
    acc = sbt("acc", [128, NT_OWN * D], BF16)
    arena = sbt("arena", [128, 24704], BF16)
    kg = arena[:, 0:8192]
    vg = arena[:, 8192:16512]
    qcg = arena[:, 16512:20608]
    ylat = arena[:, 20608:24704]
    bigA = arena[:, 0:12288]
    bigB = arena[:, 12288:24576]
    acum = arena[:, 0:16384]
    wbuf = sbt("wbuf", [128, 2 * 8 * 512], BF16)
    xnt4 = sbt("xnt4", [128, 8 * 512], BF16)
    arenaF = sbt("arenaF", [128, 4096], F32)
    xtile = arenaF[:, 0:2048]
    wtile = arenaF[:, 0:1024]
    a_c = arenaF[:, 2048:4096]
    h_c = arenaF[:, 0:2048]
    fs1 = sbt("fs1", [128, D], F32)
    bstage = sbt("bstage", [128, 4 * D], BF16)
    pTb = sbt("pTb", [128, 2 * 1024], BF16)
    stats = sbt("stats", [128, 2 * NT_BUF], F32)
    smc = sbt("smc", [128, 128], F32)
    rden = sbt("rden", [128, 1024], F32)
    consts = sbt("consts", [128, 4 * 512], BF16)
    cbias = sbt("cbias", [128, NT_BUF], F32)
    csel = sbt("csel", [128, 64], F32)
    cid_bf = sbt("cid_bf", [128, 128], BF16)
    cid_f = sbt("cid_f", [128, 128], F32)
    cshift = sbt("cshift", [64, 128], BF16)
    conesf = sbt("conesf", [128, 128], F32)
    cw2 = sbt("cw2", [128, 128], BF16)
    cbq = sbt("cbq", [128, 8], F32)
    cbk = sbt("cbk", [128, 8], F32)
    cbo = sbt("cbo", [128, D], BF16)
    cbv = sbt("cbv", [128, D], BF16)
    cwrm = sbt("cwrm", [128, 8], F32)
    cwrf = sbt("cwrf", [128, 8], F32)
    cbg = sbt("cbg", [128, 8], F32)
    cbv1 = sbt("cbv1", [128, 8], F32)
    cbd = sbt("cbd", [128, 8], F32)
    carry = sbt("carry", [128, 8], F32)
    crecv = sbt("crecv", [128, 64], F32)
    cin_sb = sbt("cin_sb", [128, 8], F32)
    xs_c = sbt("xs_c", [128, T_OWN], BF16)
    qs_u = sbt("qs_u", [128, 1024], F32)
    qs_a = sbt("qs_a", [128, 1024], F32)
    qs_b = sbt("qs_b", [128, 1024], F32)
    qs_s = sbt("qs_s", [128, 1024], F32)
    qs_S = sbt("qs_S", [128, 3 * 64], F32)
    junk = fs1

    psS = pst("psS", [128, 1024])
    psS2 = pst("psS2", [128, 1024])
    psY0 = pst("psY0", [128, 512])
    psY1 = pst("psY1", [128, 512])
    psA1 = ctx.enter_context(nc.psum_tensor("psA1", [128, 1024], BF16))
    psA2 = ctx.enter_context(nc.psum_tensor("psA2", [128, 1024], BF16))

    sLD = mksem("sLD"); sGD = mksem("sGD")
    sST = mksem("sST"); sMM = mksem("sMM"); sV = mksem("sV")
    sSA = mksem("sSA"); sG = mksem("sG"); sCC = mksem("sCC")
    PSY = [mksem(f"pS{i}") for i in range(16)]
    PG = [mksem(f"pG{i}") for i in range(6)]
    PT = [mksem(f"pT{i}") for i in range(4)]
    ALL_SEMS = tuple([sLD, sGD, sST, sMM, sV, sSA, sG, sCC] + PSY + PG + PT)

    block = ctx.enter_context(nc.Block())
    V, SCL, T, SY, G = nc.vector, nc.scalar, nc.tensor, nc.sync, nc.gpsimd
    dS = VDma(PSY)
    dG = VDma(PG)
    dT = VDma(PT)


    slS = Slot(nc.tensor); slS2 = Slot(nc.tensor)
    slY0 = Slot(nc.tensor); slY1 = Slot(nc.tensor)

    @block.sync
    def _(_sync):
        def barrier():
            for e, s in ((V, sV), (SCL, sSA), (T, sMM), (SY, sLD), (G, sG)):
                s.inc(e.nop())
            for e in (V, SCL, T, SY, G):
                for s2 in ALL_SEMS:
                    e.wait_ge(s2.h, s2.n)

        def vflush():
            sV.inc(V.tensor_copy(smc[:, 126:127], smc[:, 125:126]))

        def sflush():
            sSA.inc(SCL.copy(smc[:, 124:125], smc[:, 123:124]))

        def vwait(s, n=None):
            V.wait_ge(s.h, s.n if n is None else n)

        def swait(s, n=None):
            SCL.wait_ge(s.h, s.n if n is None else n)

        def twait(s, n=None):
            T.wait_ge(s.h, s.n if n is None else n)

        def ywait(s, n=None):
            SY.wait_ge(s.h, s.n if n is None else n)

        def gwait(s, n=None):
            G.wait_ge(s.h, s.n if n is None else n)

        def mm_acc(pt, pairs, tp=None, slot=None, first=True, last=True):
            if first:
                twait(sV)
                twait(sSA)
            if slot is not None and first:
                slot.acquire(T)
            n = len(pairs)
            ins = None
            for i, (lh, rh) in enumerate(pairs):
                ins = T.matmul(pt, lh, rh, start=(first and i == 0),
                               stop=(last and i == n - 1), tile_position=tp)
            sMM.inc(ins)
            return sMM.n

        trqA = []

        def transpose128(src_ap, dst_ap, ncols=128, coarse=True):
            k = len(trqA)
            pool = psA1 if k % 2 == 0 else psA2
            pos = (k // 2) % 4
            pt = pool[:, pos * 128:pos * 128 + 128]
            if coarse:
                twait(sV)
                twait(sSA)
            elif trqA:
                twait(sV, trqA[-1])
            sMM.inc(T.transpose(pt, src_ap, cid_bf[:]))
            vwait(sMM)
            sV.inc(V.tensor_copy(dst_ap, pt))
            trqA.append(sV.n)

        def quant_chain(src_ap, dst_ap, n):
            nb = n // 16
            blk = src_ap.rearrange("p (b k) -> p b k", k=16)
            S_ = qs_S[:, 0:nb]
            R_ = qs_S[:, 64:64 + nb]
            C_ = qs_S[:, 128:128 + nb]
            sV.inc(V.tensor_reduce(S_, blk, AX.X, ALU.max,
                                   apply_absolute_value=True))
            vwait(sV)
            sV.inc(V.reciprocal(R_, S_))
            vwait(sV)
            V.tensor_scalar(R_, R_, 6.0, 1e12, ALU.mult, ALU.min)
            V.tensor_scalar(C_, S_, 1.0 / 6.0, 1e-12, ALU.mult, ALU.max)
            V.scalar_tensor_tensor(qs_u[:, 0:n], src_ap, -1.0, src_ap,
                                   ALU.mult, ALU.max)
            V.tensor_tensor(qs_u[:, 0:n].rearrange("p (b k) -> p b k", k=16),
                            qs_u[:, 0:n].rearrange("p (b k) -> p b k", k=16),
                            R_.unsqueeze(2).broadcast_to([128, nb, 16]),
                            ALU.mult)
            swait(sV)
            sSA.inc(SCL.activation(qs_s[:, 0:n], src_ap, AF.Sign))
            V.tensor_single_scalar(qs_a[:, 0:n], qs_u[:, 0:n], 0.25,
                                   ALU.is_gt)
            for c_ in (0.75, 1.25, 1.75):
                V.scalar_tensor_tensor(qs_a[:, 0:n], qs_u[:, 0:n], c_,
                                       qs_a[:, 0:n], ALU.is_gt, ALU.add)
            V.tensor_single_scalar(qs_b[:, 0:n], qs_u[:, 0:n], 2.5, ALU.is_gt)
            for c_ in (3.5, 5.0, 5.0):
                V.scalar_tensor_tensor(qs_b[:, 0:n], qs_u[:, 0:n], c_,
                                       qs_b[:, 0:n], ALU.is_gt, ALU.add)
            V.scalar_tensor_tensor(qs_b[:, 0:n], qs_a[:, 0:n], 0.5,
                                   qs_b[:, 0:n], ALU.mult, ALU.add)
            vwait(sSA)
            V.tensor_tensor(qs_s[:, 0:n].rearrange("p (b k) -> p b k", k=16),
                            qs_s[:, 0:n].rearrange("p (b k) -> p b k", k=16),
                            C_.unsqueeze(2).broadcast_to([128, nb, 16]),
                            ALU.mult)
            sV.inc(V.tensor_tensor(dst_ap, qs_b[:, 0:n], qs_s[:, 0:n],
                                   ALU.mult))

        def task_fp4(dst_hbm, src_hbm, r, width):
            for off in range(0, width, 1024):
                nn = min(1024, width - off)
                ywait(sV)
                ywait(sSA)
                dS.start(SY,
                         out=wtile[:, 0:nn],
                         in_=src_hbm[r * 128:(r + 1) * 128, off:off + nn])
                dS.wait(V)
                dS.wait(SCL)
                vwait(sMM)
                quant_chain(wtile[:, 0:nn], wbuf[:, off:off + nn], nn)
            task_state["free_v"] = sV.n
            dS.wait(V, task_state["free_l"])
            nblk = width // 128
            for b_ in range(nblk):
                transpose128(wbuf[:, b_ * 128:(b_ + 1) * 128],
                             wbuf[:, 2816 + b_ * 128:2816 + (b_ + 1) * 128],
                             coarse=(b_ == 0))
            vflush()
            ywait(sV)
            dS.start(SY,
                     out=dst_hbm[:, :, r * 128:(r + 1) * 128]
                     .rearrange("c p n -> p c n"),
                     in_=wbuf[:, 2816:2816 + nblk * 128]
                     .rearrange("p (c n) -> p c n", n=128))
            task_state["free_l"] = dS.pos()

        def task_int(dst_hbm, src_hbm, r, width):
            gwait(sV)
            dS.wait(G)
            dS.wait(G, task_state["free_l"])
            gwait(sMM)
            dG.start(G, out=wbuf[:, 0:width],
                                in_=src_hbm[r * 128:(r + 1) * 128, :])
            tv = dG.pos()
            nblk = width // 128
            dS.wait(V, task_state["free_l"])
            for b_ in range(nblk):
                dG.wait(T, tv)
                transpose128(wbuf[:, b_ * 128:(b_ + 1) * 128],
                             wbuf[:, 3072 + b_ * 128:3072 + (b_ + 1) * 128],
                             coarse=(b_ == 0))
            vflush()
            ywait(sV)
            dS.start(SY,
                     out=dst_hbm[:, :, r * 128:(r + 1) * 128]
                     .rearrange("c p n -> p c n"),
                     in_=wbuf[:, 3072:3072 + nblk * 128]
                     .rearrange("p (c n) -> p c n", n=128))
            task_state["free_l"] = dS.pos()

        task_state = {"free_v": 0, "free_l": 0}
        tasks = []
        for r in range(22):
            tasks.append(("f", wgq_t, wgate0, r, 1024))
        for r in range(22):
            tasks.append(("f", wuq_t, wup0, r, 1024))
        for r in range(8):
            tasks.append(("f", wdq_t, wdown0, r, 2816))
        for r in range(24):
            tasks.append(("i", wgiT, wgi, r, 1024))
        for r in range(24):
            tasks.append(("i", wuiT, wui, r, 1024))
        for r in range(8):
            tasks.append(("i", wdiT, wdi, r, 3072))
        tasks.reverse()

        def pump(nt):
            for _ in range(nt):
                if not tasks:
                    return
                kind, dh_, sh_, r_, w_ = tasks.pop()
                (task_fp4 if kind == "f" else task_int)(dh_, sh_, r_, w_)


        for dst, src in ((consts, trimask_i), (cbias, bias_tbl_i),
                         (csel, sel_i), (cid_bf, ident_bf_i),
                         (cid_f, ident_f32_i), (cshift, shift_id_i),
                         (conesf, onesf_i), (cbq, bq_pp), (cbk, bk_pp),
                         (cbo, bo_b), (cbv, bv_b), (cwrm, wrms_mix_pp),
                         (cwrf, wrms_ffn_pp), (cbg, bg_pp),
                         (cbv1, bv1_pp), (cbd, bd_pp)):
            dS.start(SY, out=dst[:], in_=src)
        dG.start(G, out=cw2[:], in_=w2blk)
        dS.wait(V)
        dG.wait(V)
        sV.inc(V.memset(vg[:].rearrange("p (j h c) -> p j h c", h=8, c=65)
                        [:, :, :, 64:65], 1.0))
        sV.inc(V.memset(smc[:, 120:128], 0.0))
        if int(os.environ.get("DBGF", "0")) == 8:
            sV.inc(V.memset(smc[:, 0:120], 0.0))
            sV.inc(V.memset(stats[:, 0:64], 0.0))
            sV.inc(V.memset(qs_u[:, 0:48], 0.0))

        def bailout():
            dS.wait(V)
            for tt in range(NT_OWN):
                dT.wait(V)
                sV.inc(V.tensor_copy(xtile[:, 0:D],
                                     acc[:, tt * D:(tt + 1) * D]))
                vflush()
                ywait(sV)
                dT.start(SY, out=yout[tt * 128:(tt + 1) * 128, :],
                         in_=xtile[:, 0:D])
            dT.wait(SY)

        pa_done = {}
        for bt in range(NT_BUF):
            sl = bt % 2
            buf = xtile[:, sl * D:sl * D + D]
            if sl in pa_done:
                ywait(sV, pa_done[sl][0])
                ywait(sSA, pa_done[sl][1])
            dS.start(SY, out=buf,
                                 in_=xin[bt * 128:(bt + 1) * 128, :])
            dS.wait(V)
            dS.wait(SCL)
            sV.inc(V.reduce_sum(smc[:, bt:bt + 1], buf, AX.X))
            swait(sSA)
            sSA.inc(SCL.activation(junk[:], buf, AF.Square,
                                   accum_out=smc[:, 32 + bt:32 + bt + 1]))
            pa_done[sl] = (sV.n, sSA.n)
        vwait(sV)
        vwait(sSA)
        sV.inc(V.tensor_scalar(stats[:, 0:32], smc[:, 0:32], 1.0 / D, None,
                               ALU.mult))
        vwait(sV)
        sV.inc(V.tensor_scalar(smc[:, 64:96], smc[:, 32:64], 1.0 / D, 1e-5,
                               ALU.mult, ALU.add))
        vwait(sV)
        sV.inc(V.tensor_tensor(smc[:, 0:32], stats[:, 0:32], stats[:, 0:32],
                               ALU.mult))
        vwait(sV)
        sV.inc(V.tensor_sub(smc[:, 32:64], smc[:, 64:96], smc[:, 0:32]))
        vwait(sV)
        sV.inc(V.reciprocal(smc[:, 64:96], smc[:, 32:64]))
        vwait(sV)
        vflush()
        vwait(sV)
        swait(sV)
        sSA.inc(SCL.activation(stats[:, 32:64], smc[:, 64:96], AF.Sqrt))
        sflush()
        vwait(sSA)
        swait(sSA)
        pc_done = {}
        xq_st = {}
        _otorder = list(range(NT_OWN))
        if os.environ.get("KREV"):
            _otorder = _otorder[::-1]
        for ot in _otorder:
            bt = 16 + ot
            sl = ot % 2
            buf = xtile[:, sl * D:sl * D + D]
            if sl in pc_done:
                ywait(sV, pc_done[sl][0])
                ywait(sSA, pc_done[sl][1])
            else:
                ywait(sV)
                ywait(sSA)
            dS.start(SY, out=buf,
                                 in_=xin[bt * 128:(bt + 1) * 128, :])
            dS.wait(V)
            dS.wait(SCL)
            sV.inc(V.tensor_copy(resid[:, ot * D:(ot + 1) * D], buf))
            V.tensor_scalar(fs1[:], buf, stats[:, bt:bt + 1],
                            stats[:, 32 + bt:32 + bt + 1],
                            ALU.subtract, ALU.mult)
            if KP6 >= 1:
                dT.wait(V, xq_st.get(sl, 0))
                quant_chain(fs1[:], bstage[:, sl * D:(sl + 1) * D], D)
                vflush()
                ywait(sV)
                dT.start(SY, out=xq_sp[ot, :, :],
                                     in_=bstage[:, sl * D:(sl + 1) * D])
                xq_st[sl] = dT.pos()
            pc_done[sl] = (sV.n, sSA.n)
        barrier()

        if KPH < 2:
            bailout()
            return
        for g in range(4 if KPH >= 4 else 1):
            dG.start(G, 
                out=wbuf[:, 0:2048].rearrange("p (c n) -> p c n", n=256),
                in_=Wk[:, g * 256:(g + 1) * 256]
                    .rearrange("(c p) n -> p c n", p=128))
            dG.start(G, 
                out=wbuf[:, 2048:4096].rearrange("p (c n) -> p c n", n=256),
                in_=Wv[:, g * 256:(g + 1) * 256]
                    .rearrange("(c p) n -> p c n", p=128))
            dG.start(G, 
                out=wbuf[:, 4096:6144].rearrange("p (c n) -> p c n", n=256),
                in_=Wq[:, g * 256:(g + 1) * 256]
                    .rearrange("(c p) n -> p c n", p=128))
            wk_ap = wbuf[:, 0:2048].rearrange("p (c n) -> p c n", n=256)
            wv_ap = wbuf[:, 2048:4096].rearrange("p (c n) -> p c n", n=256)
            wq_ap = wbuf[:, 4096:6144].rearrange("p (c n) -> p c n", n=256)
            bst = {}
            for t4 in range(8):
                for q in range(4):
                    bt = t4 * 4 + q
                    sl = q % 2
                    buf = xtile[:, sl * D:sl * D + D]
                    if sl in bst:
                        ywait(sV, bst[sl])
                    else:
                        ywait(sV)
                        ywait(sSA)
                    dS.start(SY, 
                        out=buf, in_=xin[bt * 128:(bt + 1) * 128, :])
                    dS.wait(V)
                    vwait(sMM)
                    sV.inc(V.tensor_scalar(
                        bstage[:, sl * D:sl * D + D], buf,
                        stats[:, bt:bt + 1], stats[:, 32 + bt:32 + bt + 1],
                        ALU.subtract, ALU.mult))
                    bst[sl] = sV.n
                    for dc in range(8):
                        transpose128(
                            bstage[:, sl * D + dc * 128:sl * D + dc * 128
                                   + 128],
                            xnt4[:, dc * 512 + q * 128:dc * 512 + q * 128
                                 + 128], coarse=(dc == 0))
                xnt = xnt4[:].rearrange("p (c n) -> p c n", n=512)
                dG.wait(T)
                for p in range(min(2 if KP2 >= 2 else 0,
                                   KP3 - t4 if KP3 < 99 else 2)):
                    pk, sl_ = (psY0, slY0) if p % 2 == 0 else (psY1, slY1)
                    mm_acc(pk[:, 0:512],
                           [(wk_ap[:, dc, p * 128:(p + 1) * 128],
                             xnt[:, dc, :]) for dc in range(8)], slot=sl_)
                    vwait(sMM)
                    if KP4 >= 1:
                        if KP4 == 2:
                            sV.inc(V.tensor_copy(
                                kg[:, p * S_BUF + t4 * 512:p * S_BUF
                                   + (t4 + 1) * 512], pk[:, 0:512]))
                        elif KP4 == 3:
                            sV.inc(V.tensor_copy(kg[:, 0:512],
                                                 pk[:, 0:512]))
                        else:
                            sV.inc(V.tensor_scalar(
                                kg[:, p * S_BUF + t4 * 512:p * S_BUF
                                   + (t4 + 1) * 512],
                                pk[:, 0:512],
                                cbk[:, g * 2 + p:g * 2 + p + 1],
                                None, ALU.add))
                        sl_.consumed(sV)
                        if KP4 == 4:
                            twait(sV)
                for q in range(4 if KP2 >= 3 else 0):
                    j = t4 * 4 + q
                    pk, sl_ = (psY0, slY0) if q % 2 == 0 else (psY1, slY1)
                    mm_acc(pk[:, 0:256],
                           [(xnt[:, dc, q * 128:(q + 1) * 128],
                             wv_ap[:, dc, :]) for dc in range(8)], slot=sl_)
                    vwait(sMM)
                    dst = vg.rearrange("p (j h c) -> p j h c", h=4, c=65)[
                        :, j, :, 0:64]
                    sV.inc(V.tensor_add(
                        dst,
                        pk[:, 0:256].rearrange("p (h c) -> p h c", c=64),
                        cbv[:, g * 256:(g + 1) * 256]
                            .rearrange("p (h c) -> p h c", c=64)))
                    sl_.consumed(sV)
                if t4 >= 4 and KP2 >= 4:
                    for p in range(2):
                        mm_acc(psY0[:, 0:512],
                               [(wq_ap[:, dc, p * 128:(p + 1) * 128],
                                 xnt[:, dc, :]) for dc in range(8)],
                               slot=slY0)
                        vwait(sMM)
                        sV.inc(V.tensor_scalar(
                            bstage[:, 3072:3072 + 512], psY0[:, 0:512],
                            cbq[:, g * 2 + p:g * 2 + p + 1], None, ALU.add))
                        slY0.consumed(sV)
                        twait(sV)
                        mm_acc(psY1[:, 0:512],
                               [(cw2[:], bstage[:, 3072:3072 + 512])],
                               slot=slY1)
                        vwait(sMM)
                        sV.inc(V.tensor_copy(
                            qcg[:, p * T_OWN + (t4 - 4) * 512:
                                p * T_OWN + (t4 - 3) * 512],
                            psY1[:, 0:512]))
                        slY1.consumed(sV)
            barrier()
            if g == int(os.environ.get("DBGP", "-1")):
                _db_off = {"ylat": 20608, "kg": 0, "kg2": 4096,
                           "qcg": 16512}[os.environ.get("DBGW", "ylat")]
                for half in range(4):
                    dS.wait(V)
                    sV.inc(V.tensor_copy(
                        fs1[:, 0:1024],
                        arena[:, _db_off + half * 1024:_db_off
                              + (half + 1) * 1024]))
                    ywait(sV)
                    dS.start(SY, 
                        out=dbg[:, half * 1024:(half + 1) * 1024],
                        in_=fs1[:, 0:1024])
                dS.wait(SY)
            if KPH < 3:
                bailout()
                return
            vhat = vg.rearrange("p (j h c) -> p j h c", h=4, c=65)
            for c in range(4):
                for p in range(2):
                    jmax = 19 + 4 * c
                    expv = {}
                    avv = {}
                    slY0.acquire(T)
                    slY1.acquire(T)
                    for j in range(jmax + 1):
                        pS, slp = (psS, slS) if j % 2 == 0 else (psS2, slS2)
                        twait(sV)
                        twait(sSA)
                        slp.acquire(T)
                        T.matmul(pS[:, 0:512],
                                 kg[0:64, p * S_BUF + j * 128:
                                    p * S_BUF + (j + 1) * 128],
                                 qcg[0:64, p * T_OWN + c * 512:
                                     p * T_OWN + (c + 1) * 512],
                                 start=True, stop=True, tile_position=(0, 0))
                        sMM.inc(T.matmul(
                            pS[:, 512:1024],
                            kg[64:128, p * S_BUF + j * 128:
                               p * S_BUF + (j + 1) * 128],
                            qcg[64:128, p * T_OWN + c * 512:
                                p * T_OWN + (c + 1) * 512],
                            start=True, stop=True, tile_position=(64, 0)))
                        qkv = sMM.n
                        d_ = j - 16 - 4 * c
                        if 0 <= d_ <= 3:
                            vwait(sMM, qkv)
                            sV.inc(V.tensor_tensor(
                                pS[:].rearrange("p (h n) -> p h n", n=512),
                                pS[:].rearrange("p (h n) -> p h n", n=512),
                                consts[:, d_ * 512:(d_ + 1) * 512]
                                .unsqueeze(1).broadcast_to([128, 2, 512]),
                                ALU.add))
                            swait(sV)
                        else:
                            swait(sMM, qkv)
                        slot = j % 2
                        if j >= 2:
                            swait(sMM, avv[j - 2])
                        else:
                            swait(sMM)
                        sSA.inc(SCL.activation(
                            pTb[:, slot * 1024:(slot + 1) * 1024], pS[:],
                            AF.Exp, bias=cbias[:, j:j + 1], scale=0.125))
                        expv[j] = sSA.n
                        slp.consumed(sSA)
                        twait(sSA)
                        twait(sV)
                        T.matmul(psY0[0:65, 0:512],
                                 vhat[:, j, 2 * p, :].opt(),
                                 pTb[:, slot * 1024:slot * 1024 + 512],
                                 start=(j == 0), stop=(j == jmax))
                        sMM.inc(T.matmul(
                            psY1[0:65, 0:512],
                            vhat[:, j, 2 * p + 1, :].opt(),
                            pTb[:, slot * 1024 + 512:(slot + 1) * 1024],
                            start=(j == 0), stop=(j == jmax)))
                        avv[j] = sMM.n
                    vwait(sMM)
                    sV.inc(V.tensor_copy(rden[64:65, 0:512], psY0[64:65, :]))
                    sV.inc(V.tensor_copy(rden[64:65, 512:1024],
                                         psY1[64:65, :]))
                    vflush()
                    ywait(sV)
                    dS.start(SY, out=fs1[0:1, 0:1024],
                                         in_=rden[64:65, 0:1024])
                    dS.wait(V)
                    sV.inc(V.reciprocal(rden[0:1, 0:1024],
                                        fs1[0:1, 0:1024]))
                    vwait(sV)
                    twait(sV)
                    twait(sSA)
                    slS.acquire(T)
                    T.matmul(psS[:, 0:512], conesf[0:1, :],
                             rden[0:1, 0:512], start=True, stop=True)
                    sMM.inc(T.matmul(psS[:, 512:1024], conesf[0:1, :],
                                     rden[0:1, 512:1024], start=True,
                                     stop=True))
                    vwait(sMM)
                    sV.inc(V.tensor_copy(rden[:, 0:1024], psS[:]))
                    slS.consumed(sV)
                    yco = p * T_OWN + c * 512
                    sV.inc(V.tensor_mul(ylat[0:64, yco:yco + 512],
                                        psY0[0:64, :], rden[0:64, 0:512]))
                    slY0.consumed(sV)
                    vwait(sMM)
                    sV.inc(V.tensor_mul(bstage[0:64, 3072:3072 + 512],
                                        psY1[0:64, :], rden[0:64, 512:1024]))
                    slY1.consumed(sV)
                    twait(sV)
                    twait(sSA)
                    slS.acquire(T)
                    sMM.inc(T.matmul(psS[:, 0:512], cshift[:],
                                     bstage[0:64, 3072:3072 + 512],
                                     start=True, stop=True))
                    vwait(sMM)
                    sV.inc(V.tensor_copy(ylat[64:128, yco:yco + 512],
                                         psS[64:128, 0:512]))
                    slS.consumed(sV)
                    pump(4)
            barrier()
            if g == int(os.environ.get("DBGG", "-1")):
                _db_off = {"ylat": 20608, "kg": 0, "kg2": 4096,
                           "qcg": 16512}[os.environ.get("DBGW", "ylat")]
                for half in range(4):
                    dS.wait(V)
                    sV.inc(V.tensor_copy(
                        fs1[:, 0:1024],
                        arena[:, _db_off + half * 1024:_db_off + (half + 1) * 1024]))
                    ywait(sV)
                    dS.start(SY, 
                        out=dbg[:, half * 1024:(half + 1) * 1024],
                        in_=fs1[:, 0:1024])
                dS.wait(SY)
            gwait(sMM)
            dS.wait(G)
            dG.start(G, 
                out=wbuf[:, 0:2048].rearrange("p (c n) -> p c n", n=1024),
                in_=Wo[g * 256:(g + 1) * 256, :]
                    .rearrange("(c p) n -> p c n", p=128))
            dG.wait(T)
            for tt in range(NT_OWN):
                for dh in range(2):
                    pk, sl_ = (psY0, slY0) if (tt * 2 + dh) % 2 == 0 \
                        else (psY1, slY1)
                    mm_acc(pk[:, 0:512],
                           [(ylat[:, pq * T_OWN + tt * 128:
                                  pq * T_OWN + (tt + 1) * 128],
                             wbuf[:, pq * 1024 + dh * 512:
                                  pq * 1024 + (dh + 1) * 512])
                            for pq in range(2)], slot=sl_)
                    vwait(sMM)
                    dst = acc[:, tt * D + dh * 512:tt * D + (dh + 1) * 512]
                    if g == 0:
                        sV.inc(V.tensor_add(dst, pk[:, 0:512],
                                            cbo[:, dh * 512:(dh + 1) * 512]))
                    else:
                        sV.inc(V.tensor_add(dst, pk[:, 0:512], dst))
                    sl_.consumed(sV)
            barrier()

        if KPH < 5:
            bailout()
            return
        pump(len(tasks))
        barrier()

        if KPH < 6:
            bailout()
            return
        if int(os.environ.get("DBGF", "0")) == 7:
            for t_ in range(4):
                dG.start(G, out=dbg[:, t_ * 1024:(t_ + 1) * 1024],
                         in_=xq_sp[int(os.environ.get("DBGO", "12"))
                                   + t_, :, :])
            gwait(sV)
            dG.start(G, out=dbg[:, 4096:5120], in_=bstage[:, 0:1024])
            dG.start(G, out=dbg[:, 5120:6144], in_=bstage[:, 1024:2048])
            dG.wait(G)
        hq = bigA[:].rearrange("p (t n) -> p t n", n=3072)
        ioffs = [(0, 512), (512, 512), (1024, 512), (1536, 512),
                 (2048, 512), (2560, 256)]
        for sup in range(4):
            ywait(sV)
            ywait(sMM)
            for t_ in range(4):
                dS.start(SY, out=bstage[:, t_ * D:(t_ + 1) * D],
                         in_=xq_sp[sup * 4 + t_, :, :])
            dS.wait(T)
            dS.wait(V)
            for q in range(4):
                for dc in range(8):
                    transpose128(bstage[:, q * D + dc * 128:
                                        q * D + dc * 128 + 128],
                                 xnt4[:, dc * 512 + q * 128:
                                      dc * 512 + q * 128 + 128],
                                 coarse=(q == 0 and dc == 0))
            xnt = xnt4[:].rearrange("p (c n) -> p c n", n=512)
            if (sup == int(os.environ.get("DBGSUP", "0"))
                    and int(os.environ.get("DBGF", "0")) in (4, 6)):
                _dsrc = bstage if os.environ["DBGF"] == "6" else xnt4
                for half in range(4):
                    dS.wait(V)
                    vwait(sV)
                    sV.inc(V.tensor_copy(
                        xtile[:, 0:1024],
                        _dsrc[:, half * 1024:(half + 1) * 1024]))
                    ywait(sV)
                    dS.start(SY, out=dbg[:, half * 1024:(half + 1) * 1024],
                             in_=xtile[:, 0:1024])
                dS.wait(SY)
                dS.wait(V)
                dS.wait(SCL)
                dS.wait(T)
                dS.wait(G)
            wld = {}
            for ic, (io, il) in enumerate(ioffs):
                sl = ic % 2
                ywait(sMM)
                dS.start(SY,
                         out=bigB[:, sl * 4096:sl * 4096 + 8 * il]
                         .rearrange("p (c n) -> p c n", n=il),
                         in_=wgq_t[:, :, io:io + il]
                         .rearrange("c p n -> p c n"))
                dS.start(SY,
                         out=wbuf[:, sl * 4096:sl * 4096 + 8 * il]
                         .rearrange("p (c n) -> p c n", n=il),
                         in_=wuq_t[:, :, io:io + il]
                         .rearrange("c p n -> p c n"))
                wg_ap = bigB[:, sl * 4096:sl * 4096 + 8 * il] \
                    .rearrange("p (c n) -> p c n", n=il)
                wu_ap = wbuf[:, sl * 4096:sl * 4096 + 8 * il] \
                    .rearrange("p (c n) -> p c n", n=il)
                dS.wait(T)
                for q in range(4):
                    slS.acquire(T)
                    mm_acc(psS[:, 0:il],
                           [(xnt[:, dc, q * 128:(q + 1) * 128],
                             wg_ap[:, dc, :]) for dc in range(8)])
                    mm_acc(psS[:, 512:512 + il],
                           [(xnt[:, dc, q * 128:(q + 1) * 128],
                             wu_ap[:, dc, :]) for dc in range(8)])
                    wld[sl] = sMM.n
                    swait(sMM)
                    swait(sV)
                    sSA.inc(SCL.activation(pTb[:, 0:il], psS[:, 0:il],
                                           AF.Silu))
                    slS.consumed(sSA)
                    vwait(sMM)
                    vwait(sSA)
                    _dbf = int(os.environ.get("DBGF", "0"))
                    if _dbf == 2:
                        sV.inc(V.tensor_copy(fs1[:, 0:il], psS[:, 0:il]))
                    elif _dbf == 3:
                        sV.inc(V.tensor_copy(fs1[:, 0:il],
                                             psS[:, 512:512 + il]))
                    else:
                        sV.inc(V.tensor_tensor(fs1[:, 0:il],
                                               psS[:, 512:512 + il],
                                               pTb[:, 0:il], ALU.mult))
                    slS.consumed(sV)
                    vwait(sV)
                    if _dbf in (2, 3):
                        sV.inc(V.tensor_copy(hq[:, q, io:io + il],
                                             fs1[:, 0:il]))
                    else:
                        quant_chain(fs1[:, 0:il], hq[:, q, io:io + il], il)
            if (sup == int(os.environ.get("DBGSUP", "0"))
                    and int(os.environ.get("DBGF", "0")) == 1):
                for half in range(3):
                    dS.wait(V)
                    vwait(sV)
                    sV.inc(V.tensor_copy(
                        xtile[:, 0:1024],
                        bigA[:, int(os.environ.get("DBGQ", "0")) * 3072
                             + half * 1024:
                             int(os.environ.get("DBGQ", "0")) * 3072
                             + (half + 1) * 1024]))
                    ywait(sV)
                    dS.start(SY, out=dbg[:, half * 1024:(half + 1) * 1024],
                             in_=xtile[:, 0:1024])
                dS.wait(SY)
                dS.wait(V)
                dS.wait(SCL)
                dS.wait(T)
                dS.wait(G)
            for dh in range(2):
                ywait(sMM)
                dS.start(SY, 
                    out=bigB[:, 0:22 * 512].rearrange("p (c n) -> p c n",
                                                      n=512),
                    in_=wdq_t[:, :, dh * 512:(dh + 1) * 512]
                        .rearrange("c p n -> p c n"))
                wd_ap = bigB[:, 0:22 * 512].rearrange("p (c n) -> p c n",
                                                      n=512)
                for q in range(4):
                    tt = sup * 4 + q
                    for icc in range(22):
                        transpose128(hq[:, q, icc * 128:(icc + 1) * 128],
                                     xnt4[:, icc * 128:(icc + 1) * 128],
                                     coarse=(icc == 0))
                    twait(sV)
                    dS.wait(T)
                    pk, sl_ = (psY0, slY0) if q % 2 == 1 else (psY1, slY1)
                    mm_acc(pk[:, 0:512],
                           [(xnt4[:, icc * 128:(icc + 1) * 128],
                             wd_ap[:, icc, :]) for icc in range(22)],
                           slot=sl_)
                    vwait(sMM)
                    dst = acc[:, tt * D + dh * 512:tt * D + (dh + 1) * 512]
                    sV.inc(V.tensor_add(dst, pk[:, 0:512], dst))
                    sl_.consumed(sV)
        barrier()

        if KPH < 7:
            bailout()
            return
        for tt in range(NT_OWN):
            r_ = resid[:, tt * D:(tt + 1) * D]
            sV.inc(V.tensor_add(r_, r_, acc[:, tt * D:(tt + 1) * D]))
            vflush()
            vwait(sV)
            swait(sV)
            sV.inc(V.reduce_sum(smc[:, tt:tt + 1], r_, AX.X))
            swait(sSA)
            sSA.inc(SCL.activation(junk[:], r_, AF.Square,
                                   accum_out=smc[:, 32 + tt:32 + tt + 1]))
        vwait(sV)
        sflush()
        vwait(sSA)
        if int(os.environ.get("DBGF", "0")) == 8:
            sV.inc(V.tensor_copy(qs_u[:, 0:48], smc[:, 0:48]))
            vwait(sV)
        sV.inc(V.tensor_scalar(stats[:, 0:16], smc[:, 0:16], 1.0 / D, None,
                               ALU.mult))
        vwait(sV)
        sV.inc(V.tensor_scalar(smc[:, 64:80], smc[:, 32:48], 1.0 / D, 1e-5,
                               ALU.mult, ALU.add))
        vwait(sV)
        sV.inc(V.tensor_tensor(smc[:, 0:16], stats[:, 0:16], stats[:, 0:16],
                               ALU.mult))
        vwait(sV)
        sV.inc(V.tensor_sub(smc[:, 32:48], smc[:, 64:80], smc[:, 0:16]))
        vwait(sV)
        sV.inc(V.reciprocal(smc[:, 64:80], smc[:, 32:48]))
        vwait(sV)
        vflush()
        vwait(sV)
        swait(sV)
        sSA.inc(SCL.activation(stats[:, 32:48], smc[:, 64:80], AF.Sqrt))
        sflush()
        if KP5 < 2:
            bailout()
            return
        vwait(sSA)
        swait(sSA)
        for tt in range(NT_OWN):
            sV.inc(V.tensor_scalar(acc[:, tt * D:(tt + 1) * D],
                                   resid[:, tt * D:(tt + 1) * D],
                                   stats[:, tt:tt + 1],
                                   stats[:, 32 + tt:32 + tt + 1],
                                   ALU.subtract, ALU.mult))
            vflush()
            swait(sV)
            swait(sSA)
            sSA.inc(SCL.activation(junk[:], acc[:, tt * D:(tt + 1) * D],
                                   AF.Square,
                                   accum_out=smc[:, 96 + tt:96 + tt + 1]))
        sflush()
        vwait(sSA)
        sV.inc(V.tensor_scalar(smc[:, 64:80], smc[:, 96:112], 1.0 / D, 1e-6,
                               ALU.mult, ALU.add))
        vwait(sV)
        sV.inc(V.reciprocal(smc[:, 80:96], smc[:, 64:80]))
        vwait(sV)
        vflush()
        vwait(sV)
        swait(sV)
        sSA.inc(SCL.activation(stats[:, 48:64], smc[:, 80:96], AF.Sqrt))
        sflush()
        vwait(sSA)
        for tt in range(NT_OWN):
            for dc in range(8):
                transpose128(acc[:, tt * D + dc * 128:tt * D + dc * 128
                                 + 128],
                             bstage[:, dc * 128:(dc + 1) * 128],
                             coarse=(dc == 0))
            vflush()
            ywait(sV)
            dS.start(SY,
                out=xn1_sp[:, :, tt * 128:(tt + 1) * 128]
                    .rearrange("c p n -> p c n"),
                in_=bstage[:, 0:1024].rearrange("p (c n) -> p c n", n=128))
            vwait(sMM)
            sV.inc(V.tensor_scalar(bstage[:, 1024:2048],
                                   acc[:, tt * D:(tt + 1) * D],
                                   stats[:, 48 + tt:48 + tt + 1], None,
                                   ALU.mult))
            dT.wait(V)
            for dc in range(8):
                transpose128(bstage[:, 1024 + dc * 128:1024 + dc * 128
                                    + 128], pTb[:, 0:128],
                             coarse=(dc == 0))
                sV.inc(V.tensor_scalar(
                    bstage[:, 2048 + dc * 128:2048 + (dc + 1) * 128],
                    pTb[:, 0:128], cwrm[:, dc:dc + 1], None, ALU.mult))
            vflush()
            ywait(sV)
            dT.start(SY,
                out=xnm_sp[:, :, tt * 128:(tt + 1) * 128]
                    .rearrange("c p n -> p c n"),
                in_=bstage[:, 2048:3072].rearrange("p (c n) -> p c n",
                                                   n=128))
        barrier()

        if int(os.environ.get("DBGF", "0")) == 8:
            vwait(sSA)
            sV.inc(V.tensor_copy(xtile[:, 0:64], stats[:, 0:64]))
            sV.inc(V.tensor_copy(xtile[:, 64:192], smc[:, 0:128]))
            sV.inc(V.tensor_copy(xtile[:, 192:240], qs_u[:, 0:48]))
            vflush()
            ywait(sV)
            dS.start(SY, out=dbg[:, 0:240], in_=xtile[:, 0:240])
            dS.wait(SY)
        if KP5 < 3:
            bailout()
            return
        gvd_ld = {}
        for cc in range(8):
            sl = cc % 2
            if sl in gvd_ld:
                gwait(sMM, gvd_ld[sl])
            wga = wbuf[:, sl * 4096:sl * 4096 + 8 * 384] \
                .rearrange("p (c n) -> p c n", n=384)
            dG.start(G, 
                out=wga[:, :, 0:128],
                in_=Wg1[:, cc * 128:(cc + 1) * 128]
                    .rearrange("(c p) n -> p c n", p=128))
            dG.start(G, 
                out=wga[:, :, 128:256],
                in_=Wv1[:, cc * 128:(cc + 1) * 128]
                    .rearrange("(c p) n -> p c n", p=128))
            dG.start(G, 
                out=wga[:, :, 256:384],
                in_=Wd1[:, cc * 128:(cc + 1) * 128]
                    .rearrange("(c p) n -> p c n", p=128))
            dG.wait(T)
            for st4 in range(4):
                ywait(sMM)
                dS.start(SY, 
                    out=xnt4[:].rearrange("p (c n) -> p c n", n=512),
                    in_=xnm_sp[:, :, st4 * 512:(st4 + 1) * 512]
                        .rearrange("c p n -> p c n"))
                xnt = xnt4[:].rearrange("p (c n) -> p c n", n=512)
                dS.wait(T)
                slS.acquire(T)
                slS2.acquire(T)
                mm_acc(psS[:, 0:512],
                       [(wga[:, dc, 0:128], xnt[:, dc, :])
                        for dc in range(8)])
                mm_acc(psS[:, 512:1024],
                       [(wga[:, dc, 128:256], xnt[:, dc, :])
                        for dc in range(8)])
                mm_acc(psS2[:, 0:512],
                       [(wga[:, dc, 256:384], xnt[:, dc, :])
                        for dc in range(8)])
                gvd_ld[sl] = sMM.n
                swait(sMM)
                swait(sV)
                sSA.inc(SCL.activation(pTb[:, 0:512], psS[:, 0:512],
                                       AF.Sigmoid, bias=cbg[:, cc:cc + 1]))
                sSA.inc(SCL.activation(pTb[:, 512:1024], psS[:, 512:1024],
                                       AF.Tanh, bias=cbv1[:, cc:cc + 1]))
                sSA.inc(SCL.activation(fs1[:, 0:512], psS2[:, 0:512],
                                       AF.Sigmoid, bias=cbd[:, cc:cc + 1]))
                slS.consumed(sSA)
                slS2.consumed(sSA)
                vwait(sSA)
                sV.inc(V.tensor_mul(xs_c[:, st4 * 512:(st4 + 1) * 512],
                                    pTb[:, 0:512], pTb[:, 512:1024]))
                sV.inc(V.tensor_scalar(a_c[:, st4 * 512:(st4 + 1) * 512],
                                       fs1[:, 0:512], 0.998, 0.001,
                                       ALU.mult, ALU.add))
            vwait(sMM)
            sV.inc(V.tensor_tensor_scan(h_c[:], a_c[:], xs_c[:], 0.0,
                                        ALU.mult, ALU.add))
            vwait(sV)
            sV.inc(V.tensor_copy(carry[:, cc:cc + 1], h_c[:, 2047:2048]))
            sV.inc(V.tensor_tensor_scan(
                acum[:, cc * T_OWN:(cc + 1) * T_OWN], a_c[:], a_c[:], 1.0,
                ALU.mult, ALU.bypass))
            vwait(sV)
            sV.inc(V.tensor_copy(xs_c[:], h_c[:]))
            for tt in range(NT_OWN):
                transpose128(xs_c[:, tt * 128:(tt + 1) * 128], pTb[:, 0:128],
                             coarse=(tt == 0))
                dst = acc[:, tt * D + cc * 128:tt * D + cc * 128 + 128]
                sV.inc(V.tensor_add(dst, pTb[:, 0:128], dst))
        barrier()
        if KP5 < 4:
            bailout()
            return
        dG.start(G, out=cin_b, in_=carry[:])
        dG.wait(G)
        sCC.inc(G.collective_compute(
            "AllGather", ALU.bypass, replica_groups=[list(range(8))],
            ins=[cin_b.opt()], outs=[cout_b.opt()]))
        gwait(sCC)
        dG.start(G, 
            out=crecv[:].rearrange("p (r c) -> p r c", c=8),
            in_=cout_b.rearrange("(r p) c -> p r c", p=128))
        dG.wait(V)
        sV.inc(V.tensor_mul(crecv[:], crecv[:], csel[:]))
        vwait(sV)
        sV.inc(V.tensor_reduce(cin_sb[:], crecv[:]
                               .rearrange("p (r c) -> p c r", c=8),
                               AX.X, ALU.add))
        vwait(sV)
        for cc in range(8):
            vwait(sMM)
            sV.inc(V.tensor_scalar(xs_c[:],
                                   acum[:, cc * T_OWN:(cc + 1) * T_OWN],
                                   cin_sb[:, cc:cc + 1], None, ALU.mult))
            for tt in range(NT_OWN):
                transpose128(xs_c[:, tt * 128:(tt + 1) * 128], pTb[:, 0:128],
                             coarse=(tt == 0))
                dst = acc[:, tt * D + cc * 128:tt * D + cc * 128 + 128]
                sV.inc(V.tensor_add(dst, pTb[:, 0:128], dst))
        barrier()
        if KP5 < 5:
            bailout()
            return
        for tt in range(NT_OWN):
            swait(sV)
            swait(sSA)
            sSA.inc(SCL.activation(junk[:], acc[:, tt * D:(tt + 1) * D],
                                   AF.Square, accum_out=smc[:, tt:tt + 1]))
        sflush()
        vwait(sSA)
        sV.inc(V.tensor_scalar(smc[:, 32:48], smc[:, 0:16], 1.0 / D, 1e-6,
                               ALU.mult, ALU.add))
        vwait(sV)
        sV.inc(V.reciprocal(smc[:, 64:80], smc[:, 32:48]))
        vwait(sV)
        vflush()
        vwait(sV)
        swait(sV)
        sSA.inc(SCL.activation(stats[:, 0:16], smc[:, 64:80], AF.Sqrt))
        sflush()
        barrier()

        def emit_ffn(variant):
            ffn_ev = {}
            prod_v = [0]
            for sup in range(4):
                if variant == "A":
                    for q in range(4):
                        tt = sup * 4 + q
                        vwait(sMM)
                        sV.inc(V.tensor_scalar(
                            bstage[:, 0:D], acc[:, tt * D:(tt + 1) * D],
                            stats[:, tt:tt + 1], None, ALU.mult))
                        for dc in range(8):
                            transpose128(bstage[:, dc * 128:(dc + 1) * 128],
                                         pTb[:, 0:128], coarse=(dc == 0))
                            sV.inc(V.tensor_scalar(
                                xnt4[:, dc * 512 + q * 128:
                                     dc * 512 + q * 128 + 128],
                                pTb[:, 0:128], cwrf[:, dc:dc + 1], None,
                                ALU.mult))
                else:
                    ywait(sMM)
                    dS.start(SY, 
                        out=xnt4[:].rearrange("p (c n) -> p c n", n=512),
                        in_=xn1_sp[:, :, sup * 512:(sup + 1) * 512]
                            .rearrange("c p n -> p c n"))
                xnt = xnt4[:].rearrange("p (c n) -> p c n", n=512)
                for fh in range(2):
                    if variant == "A":
                        gwait(sMM)
                    else:
                        ywait(sMM)
                    for f12 in range(12):
                        f = fh * 12 + f12
                        if variant == "A":
                            dG.start(G, 
                                out=bigB[:, f12 * 1024:(f12 + 1) * 1024],
                                in_=Wout1[f * 128:(f + 1) * 128, :])
                        else:
                            dS.start(SY, 
                                out=bigB[:, f12 * 1024:(f12 + 1) * 1024],
                                in_=wdiT[f, :, :])
                    for f12 in range(12):
                        f = fh * 12 + f12
                        sl = f12 % 2
                        wgu = wbuf[:, sl * 4096:sl * 4096 + 8 * 256] \
                            .rearrange("p (c n) -> p c n", n=256)
                        if variant == "A":
                            gwait(sMM)
                            dG.start(G, 
                                out=wgu[:, :, 0:128],
                                in_=Wgate1[:, f * 128:(f + 1) * 128]
                                    .rearrange("(c p) n -> p c n", p=128))
                            dG.start(G, 
                                out=wgu[:, :, 128:256],
                                in_=Wup1[:, f * 128:(f + 1) * 128]
                                    .rearrange("(c p) n -> p c n", p=128))
                            dG.wait(T)
                        else:
                            ywait(sMM)
                            dS.start(SY, 
                                out=wgu[:, :, 0:128],
                                in_=wgiT[:, :, f * 128:(f + 1) * 128]
                                    .rearrange("c p n -> p c n"))
                            dS.start(SY, 
                                out=wgu[:, :, 128:256],
                                in_=wuiT[:, :, f * 128:(f + 1) * 128]
                                    .rearrange("c p n -> p c n"))
                            dS.wait(T)
                        twait(sV, prod_v[0])
                        slS.acquire(T)
                        mm_acc(psS[:, 0:512],
                               [(wgu[:, dc, 0:128], xnt[:, dc, :])
                                for dc in range(8)])
                        mm_acc(psS[:, 512:1024],
                               [(wgu[:, dc, 128:256], xnt[:, dc, :])
                                for dc in range(8)])
                        swait(sMM)
                        swait(sV)
                        sSA.inc(SCL.activation(
                            pTb[:, 0:512], psS[:, 0:512], AF.Silu,
                            scale=(1.0 if variant == "A" else 1.0 / 127.0)))
                        slS.consumed(sSA)
                        vwait(sMM)
                        vwait(sSA)
                        sV.inc(V.scalar_tensor_tensor(
                            bigA[:, f12 * 512:(f12 + 1) * 512],
                            psS[:, 512:1024],
                            (1.0 if variant == "A" else 1.0 / 16129.0),
                            pTb[:, 0:512], ALU.mult, ALU.mult))
                        slS.consumed(sV)
                        prod_v[0] = sV.n
                    prod_all = sV.n
                    if (int(os.environ.get("DBGF", "0")) == 12
                            and variant == "A" and sup == 0
                            and fh == int(os.environ.get("DBGFH", "0"))):
                        dG.wait(T)
                        dS.wait(T)
                        gwait(sMM)
                        dG.start(G, out=dbg[:, 0:8192], in_=bigB[:, 0:8192])
                        dG.wait(V)
                        dG.wait(T)
                    if (int(os.environ.get("DBGF", "0")) in (10, 11)
                            and variant == ("B" if int(os.environ.get(
                                "DBGF", "0")) == 10 else "A") and sup == 0
                            and fh == int(os.environ.get("DBGFH", "0"))):
                        vflush()
                        gwait(sV)
                        dG.start(G, out=dbg[:, 0:6144], in_=bigA[:, 0:6144])
                        dG.wait(V)
                        dG.wait(T)
                    dS.wait(T)
                    dG.wait(T)
                    for tt4 in range(4):
                        tt = sup * 4 + tt4
                        for dh in range(2):
                            pk, sl_ = (psY0, slY0) \
                                if (tt4 * 2 + dh) % 2 == 0 else (psY1, slY1)
                            twait(sV, prod_all)
                            mm_acc(pk[:, 0:512],
                                   [(bigA[:, f12 * 512 + tt4 * 128:
                                          f12 * 512 + tt4 * 128 + 128],
                                     bigB[:, f12 * 1024 + dh * 512:
                                          f12 * 1024 + (dh + 1) * 512])
                                    for f12 in range(12)], slot=sl_)
                            vwait(sMM)
                            if variant == "A":
                                dst = acc[:, tt * D + dh * 512:
                                          tt * D + (dh + 1) * 512]
                                sV.inc(V.tensor_add(dst, pk[:, 0:512], dst))
                            else:
                                dst = arenaF[:, tt4 * D + dh * 512:
                                             tt4 * D + (dh + 1) * 512]
                                if fh == 0:
                                    dT.wait(V)
                                    sV.inc(V.tensor_add(
                                        dst, pk[:, 0:512],
                                        acc[:, tt * D + dh * 512:
                                            tt * D + (dh + 1) * 512]))
                                else:
                                    sV.inc(V.tensor_add(dst, pk[:, 0:512],
                                                        dst))
                                    sV.inc(V.tensor_add(
                                        dst, dst,
                                        resid[:, tt * D + dh * 512:
                                              tt * D + (dh + 1) * 512]))
                                    if dh == 1:
                                        vflush()
                                        ywait(sV)
                                        dT.start(SY,
                                            out=yout[tt * 128:
                                                     (tt + 1) * 128, :],
                                            in_=arenaF[:, tt4 * D:
                                                       (tt4 + 1) * D])
                            sl_.consumed(sV)

        if KPH < 8:
            bailout()
            return
        emit_ffn("A")
        barrier()
        if int(os.environ.get("DBGF", "0")) == 13:
            dG.start(G, out=dbg[:, 0:8192], in_=acc[:, 0:8192])
            dG.wait(V)
            dG.wait(T)
        if KPH < 9:
            bailout()
            return
        emit_ffn("B")
        dT.wait(SY)
        dT.wait(G)
        dT.wait(V)
        dT.wait(SCL)
        dT.wait(T)

    ctx.close()
    return nc


def _prep_inputs(inputs):
    x = np.asarray(inputs["L0_attn_x"])
    f32 = np.float32
    bf16 = ml_dtypes.bfloat16

    def pp(v):
        return np.ascontiguousarray(np.asarray(v, dtype=f32).reshape(8, 128).T)

    def bcast(v):
        return np.ascontiguousarray(
            np.broadcast_to(np.asarray(v, dtype=f32)[None, :], (128, D)))

    wc = np.asarray(inputs["L0_attn_Wc"], dtype=f32)
    w2 = np.zeros((128, 128), f32)
    w2[0:64, 0:64] = wc
    w2[64:128, 64:128] = wc

    ident = np.eye(128, dtype=f32)
    shift_id = np.zeros((64, 128), f32)
    shift_id[np.arange(64), 64 + np.arange(64)] = 1.0

    tm = np.zeros((128, 4, 512), f32)
    for d_ in range(4):
        for kr in range(128):
            lim = d_ * 128 + kr
            tm[kr, d_, :lim] = KILL
    tm = np.ascontiguousarray(tm.reshape(128, 4 * 512))

    base = dict(
        Wq=np.ascontiguousarray(inputs["L0_attn_Wq"], f32),
        Wk=np.ascontiguousarray(inputs["L0_attn_Wk"], f32),
        Wv=np.ascontiguousarray(inputs["L0_attn_Wv"], f32),
        Wo=np.ascontiguousarray(inputs["L0_attn_Wo"], f32),
        w2blk=w2,
        bq_pp=pp(inputs["L0_attn_bq"]),
        bk_pp=pp(inputs["L0_attn_bk"]),
        bo_b=bcast(inputs["L0_attn_bo"]).astype(bf16),
        bv_b=bcast(inputs["L0_attn_bv"]).astype(bf16),
        wgate0=np.ascontiguousarray(inputs["L0_ffn_w_gate"], f32),
        wup0=np.ascontiguousarray(inputs["L0_ffn_w_up"], f32),
        wdown0=np.ascontiguousarray(inputs["L0_ffn_w_down"], f32),
        wrms_mix_pp=pp(inputs["L1_attn_w_rms_mix"]),
        wrms_ffn_pp=pp(inputs["L1_attn_w_rms_ffn"]),
        Wg1=np.ascontiguousarray(inputs["L1_attn_Wg"], f32),
        Wv1=np.ascontiguousarray(inputs["L1_attn_Wv"], f32),
        Wd1=np.ascontiguousarray(inputs["L1_attn_Wd"], f32),
        bg_pp=pp(inputs["L1_attn_bg"]),
        bv1_pp=pp(inputs["L1_attn_bv"]),
        bd_pp=pp(inputs["L1_attn_bd"]),
        Wgate1=np.ascontiguousarray(inputs["L1_attn_W_gate"], f32),
        Wup1=np.ascontiguousarray(inputs["L1_attn_W_up"], f32),
        Wout1=np.ascontiguousarray(inputs["L1_attn_W_out"], f32),
        wgi=np.ascontiguousarray(inputs["L1_ffn_w_gate"], np.int32),
        wui=np.ascontiguousarray(inputs["L1_ffn_w_up"], np.int32),
        wdi=np.ascontiguousarray(inputs["L1_ffn_w_down"], np.int32),
        ident_bf=ident.astype(bf16),
        ident_f32=ident,
        shift_id=shift_id.astype(bf16),
        onesf=np.ones((128, 128), f32),
        trimask=tm.astype(bf16),
    )

    in_maps = []
    for c in range(8):
        b, h = c // 2, c % 2
        if h == 0:
            xb = np.concatenate([np.zeros((T_OWN, D), f32),
                                 np.asarray(x[b, 0:T_OWN], f32)], axis=0)
        else:
            xb = np.ascontiguousarray(np.asarray(x[b], f32))
        bias = np.full((128, NT_BUF), EXP_SHIFT, f32)
        if h == 0:
            bias[:, 0:16] = KILL + EXP_SHIFT
        sel = np.zeros((128, 64), f32)
        if h == 1:
            partner = c - 1
            sel[:, partner * 8:(partner + 1) * 8] = 1.0
        m = dict(base)
        m.update(xin=xb, bias_tbl=bias, sel=sel)
        in_maps.append(m)
    return in_maps


def kernel(**inputs):
    if "nc" not in _CACHE:
        _CACHE["nc"] = build()
    nc = _CACHE["nc"]
    in_maps = _prep_inputs(inputs)
    res = run_bass_kernel_spmd(nc, in_maps, core_ids=list(range(8)))
    out = np.empty((4, S_BUF, D), np.float32)
    for c in range(8):
        b, h = c // 2, c % 2
        out[b, h * T_OWN:(h + 1) * T_OWN, :] = res.results[c]["yout"]
    return out


# version 05
# speedup vs baseline: 1.0558x; 1.0133x over previous
import os
import numpy as np
import ml_dtypes
import concourse.bass as bass
import concourse.mybir as mybir
from concourse.bass_utils import run_bass_kernel_spmd
from contextlib import ExitStack

F32 = mybir.dt.float32
BF16 = mybir.dt.bfloat16
I32 = mybir.dt.int32
ALU = mybir.AluOpType
AF = mybir.ActivationFunctionType
AX = mybir.AxisListType

D = 1024
S_BUF = 4096
T_OWN = 2048
NT_BUF = 32
NT_OWN = 16
I0 = 2816
F1 = 3072
EXP_SHIFT = -20.0
KILL = -10000.0

_CACHE = {}
KPH = int(os.environ.get("KPH", "99"))
KP2 = int(os.environ.get("KP2", "99"))
KP3 = int(os.environ.get("KP3", "99"))
KP4 = int(os.environ.get("KP4", "99"))
KP6 = int(os.environ.get("KP6", "99"))
KP5 = int(os.environ.get("KP5", "99"))


class Sem:
    __slots__ = ("h", "n")

    def __init__(self, handle):
        self.h = handle
        self.n = 0

    def inc(self, instr, k=1):
        instr.then_inc(self.h, k)
        self.n += k
        return self.n


class VDma:
    __slots__ = ("sems", "toks", "marks")

    def __init__(self, sems):
        self.sems = sems
        self.toks = []
        self.marks = {}

    def start(self, eng, out, in_):
        s = self.sems[len(self.toks) % len(self.sems)]
        eng.wait_ge(s.h, s.n)
        s.inc(eng.dma_start(out=out, in_=in_), 16)
        self.toks.append((s, s.n))
        return len(self.toks)

    def pos(self):
        return len(self.toks)

    def wait(self, eng, upto=None):
        v = len(self.toks) if upto is None else upto
        key = id(eng)
        lo = max(self.marks.get(key, 0), v - len(self.sems))
        for (s, n) in self.toks[lo:v]:
            eng.wait_ge(s.h, n)
        if v > self.marks.get(key, 0):
            self.marks[key] = v


class Slot:
    __slots__ = ("cons", "teng")

    def __init__(self, teng=None):
        self.cons = []
        self.teng = teng

    def acquire(self, eng):
        for sm, v in self.cons:
            eng.wait_ge(sm.h, v)
        self.cons = []

    def consumed(self, sm):
        self.cons.append((sm, sm.n))
        if self.teng is not None:
            self.teng.wait_ge(sm.h, sm.n)


def build():
    nc = bass.Bass("TRN2", target_bir_lowering=False, debug=False,
                   detect_race_conditions=False)

    def inp(name, shape, dt=F32):
        return nc.dram_tensor(name, shape, dt, kind="ExternalInput").ap()

    xin = inp("xin", [S_BUF, D])
    Wq = inp("Wq", [D, D]); Wk = inp("Wk", [D, D]); Wv = inp("Wv", [D, D])
    Wo = inp("Wo", [D, D])
    w2blk = inp("w2blk", [128, 128])
    bq_pp = inp("bq_pp", [128, 8]); bk_pp = inp("bk_pp", [128, 8])
    bo_b = inp("bo_b", [128, D], BF16)
    bv_b = inp("bv_b", [128, D], BF16)
    wgate0 = inp("wgate0", [I0, D]); wup0 = inp("wup0", [I0, D])
    wdown0 = inp("wdown0", [D, I0])
    wrms_mix_pp = inp("wrms_mix_pp", [128, 8])
    wrms_ffn_pp = inp("wrms_ffn_pp", [128, 8])
    Wg1 = inp("Wg1", [D, D]); Wv1 = inp("Wv1", [D, D]); Wd1 = inp("Wd1", [D, D])
    bg_pp = inp("bg_pp", [128, 8]); bv1_pp = inp("bv1_pp", [128, 8])
    bd_pp = inp("bd_pp", [128, 8])
    Wgate1 = inp("Wgate1", [D, F1]); Wup1 = inp("Wup1", [D, F1])
    Wout1 = inp("Wout1", [F1, D])
    wgi = inp("wgi", [F1, D], I32); wui = inp("wui", [F1, D], I32)
    wdi = inp("wdi", [D, F1], I32)
    ident_bf_i = inp("ident_bf", [128, 128], BF16)
    ident_f32_i = inp("ident_f32", [128, 128])
    shift_id_i = inp("shift_id", [64, 128], BF16)
    onesf_i = inp("onesf", [128, 128])
    trimask_i = inp("trimask", [128, 4 * 512], BF16)
    bias_tbl_i = inp("bias_tbl", [128, NT_BUF])
    sel_i = inp("sel", [128, 64])

    yout = nc.dram_tensor("yout", [T_OWN, D], F32, kind="ExternalOutput").ap()
    dbg = nc.dram_tensor("dbg", [128, 8192], F32, kind="ExternalOutput").ap()

    wgq_t = nc.dram_tensor("wgq_t", [8, 128, I0], BF16).ap()
    wuq_t = nc.dram_tensor("wuq_t", [8, 128, I0], BF16).ap()
    wdq_t = nc.dram_tensor("wdq_t", [22, 128, D], BF16).ap()
    wgiT = nc.dram_tensor("wgiT", [8, 128, F1], BF16).ap()
    wuiT = nc.dram_tensor("wuiT", [8, 128, F1], BF16).ap()
    wdiT = nc.dram_tensor("wdiT", [24, 128, D], BF16).ap()
    xn1_sp = nc.dram_tensor("xn1_sp", [8, 128, T_OWN], BF16).ap()
    xnm_sp = nc.dram_tensor("xnm_sp", [8, 128, T_OWN], BF16).ap()
    xq_sp = nc.dram_tensor("xq_sp", [NT_OWN, 128, D], BF16).ap()
    cin_b = nc.dram_tensor("cin_b", [128, 8], F32).ap()
    cout_b = nc.dram_tensor("cout_b", [8 * 128, 8], F32).ap()

    ctx = ExitStack()
    ctx.__enter__()

    def sbt(name, shape, dt):
        return ctx.enter_context(nc.sbuf_tensor(name, shape, dt))

    def pst(name, shape):
        return ctx.enter_context(nc.psum_tensor(name, shape, F32))

    def mksem(name):
        return Sem(ctx.enter_context(nc.semaphore(name)))

    resid = sbt("resid", [128, NT_OWN * D], BF16)
    acc = sbt("acc", [128, NT_OWN * D], BF16)
    arena = sbt("arena", [128, 24704], BF16)
    kg = arena[:, 0:8192]
    vg = arena[:, 8192:16512]
    qcg = arena[:, 16512:20608]
    ylat = arena[:, 20608:24704]
    bigA = arena[:, 0:12288]
    bigB = arena[:, 12288:24576]
    acum = arena[:, 0:16384]
    wbuf = sbt("wbuf", [128, 2 * 8 * 512], BF16)
    xnt4 = sbt("xnt4", [128, 8 * 512], BF16)
    arenaF = sbt("arenaF", [128, 4096], F32)
    xtile = arenaF[:, 0:2048]
    wtile = arenaF[:, 0:1024]
    a_c = arenaF[:, 2048:4096]
    h_c = arenaF[:, 0:2048]
    fs1 = sbt("fs1", [128, D], F32)
    bstage = sbt("bstage", [128, 4 * D], BF16)
    pTb = sbt("pTb", [128, 2 * 1024], BF16)
    stats = sbt("stats", [128, 2 * NT_BUF], F32)
    smc = sbt("smc", [128, 128], F32)
    rden = sbt("rden", [128, 1024], F32)
    consts = sbt("consts", [128, 4 * 512], BF16)
    cbias = sbt("cbias", [128, NT_BUF], F32)
    csel = sbt("csel", [128, 64], F32)
    cid_bf = sbt("cid_bf", [128, 128], BF16)
    cid_f = sbt("cid_f", [128, 128], F32)
    cshift = sbt("cshift", [64, 128], BF16)
    conesf = sbt("conesf", [128, 128], F32)
    cw2 = sbt("cw2", [128, 128], BF16)
    cbq = sbt("cbq", [128, 8], F32)
    cbk = sbt("cbk", [128, 8], F32)
    cbo = sbt("cbo", [128, D], BF16)
    cbv = sbt("cbv", [128, D], BF16)
    cwrm = sbt("cwrm", [128, 8], F32)
    cwrf = sbt("cwrf", [128, 8], F32)
    cbg = sbt("cbg", [128, 8], F32)
    cbv1 = sbt("cbv1", [128, 8], F32)
    cbd = sbt("cbd", [128, 8], F32)
    carry = sbt("carry", [128, 8], F32)
    crecv = sbt("crecv", [128, 64], F32)
    cin_sb = sbt("cin_sb", [128, 8], F32)
    xs_c = sbt("xs_c", [128, T_OWN], BF16)
    qs_u = sbt("qs_u", [128, 1024], F32)
    qs_a = sbt("qs_a", [128, 1024], F32)
    qs_b = sbt("qs_b", [128, 1024], F32)
    qs_s = sbt("qs_s", [128, 1024], F32)
    qs_S = sbt("qs_S", [128, 3 * 64], F32)
    junk = fs1

    psS = pst("psS", [128, 1024])
    psS2 = pst("psS2", [128, 1024])
    psY0 = pst("psY0", [128, 512])
    psY1 = pst("psY1", [128, 512])
    psA1 = ctx.enter_context(nc.psum_tensor("psA1", [128, 1024], BF16))
    psA2 = ctx.enter_context(nc.psum_tensor("psA2", [128, 1024], BF16))

    sLD = mksem("sLD"); sGD = mksem("sGD")
    sST = mksem("sST"); sMM = mksem("sMM"); sV = mksem("sV")
    sSA = mksem("sSA"); sG = mksem("sG"); sCC = mksem("sCC")
    PSY = [mksem(f"pS{i}") for i in range(16)]
    PG = [mksem(f"pG{i}") for i in range(6)]
    PT = [mksem(f"pT{i}") for i in range(4)]
    ALL_SEMS = tuple([sLD, sGD, sST, sMM, sV, sSA, sG, sCC] + PSY + PG + PT)

    block = ctx.enter_context(nc.Block())
    V, SCL, T, SY, G = nc.vector, nc.scalar, nc.tensor, nc.sync, nc.gpsimd
    dS = VDma(PSY)
    dG = VDma(PG)
    dT = VDma(PT)


    slS = Slot(nc.tensor); slS2 = Slot(nc.tensor)
    slY0 = Slot(nc.tensor); slY1 = Slot(nc.tensor)

    @block.sync
    def _(_sync):
        def barrier():
            for e, s in ((V, sV), (SCL, sSA), (T, sMM), (SY, sLD), (G, sG)):
                s.inc(e.nop())
            for e in (V, SCL, T, SY, G):
                for s2 in ALL_SEMS:
                    e.wait_ge(s2.h, s2.n)

        def vflush():
            sV.inc(V.tensor_copy(smc[:, 126:127], smc[:, 125:126]))

        def sflush():
            sSA.inc(SCL.copy(smc[:, 124:125], smc[:, 123:124]))

        def vwait(s, n=None):
            V.wait_ge(s.h, s.n if n is None else n)

        def swait(s, n=None):
            SCL.wait_ge(s.h, s.n if n is None else n)

        def twait(s, n=None):
            T.wait_ge(s.h, s.n if n is None else n)

        def ywait(s, n=None):
            SY.wait_ge(s.h, s.n if n is None else n)

        def gwait(s, n=None):
            G.wait_ge(s.h, s.n if n is None else n)

        def mm_acc(pt, pairs, tp=None, slot=None, first=True, last=True):
            if first:
                twait(sV)
                twait(sSA)
            if slot is not None and first:
                slot.acquire(T)
            n = len(pairs)
            ins = None
            for i, (lh, rh) in enumerate(pairs):
                ins = T.matmul(pt, lh, rh, start=(first and i == 0),
                               stop=(last and i == n - 1), tile_position=tp)
            sMM.inc(ins)
            return sMM.n

        trqA = []

        def transpose128(src_ap, dst_ap, ncols=128, coarse=True):
            k = len(trqA)
            pool = psA1 if k % 2 == 0 else psA2
            pos = (k // 2) % 4
            pt = pool[:, pos * 128:pos * 128 + 128]
            if coarse:
                twait(sV)
                twait(sSA)
            elif trqA:
                twait(sV, trqA[-1])
            sMM.inc(T.transpose(pt, src_ap, cid_bf[:]))
            vwait(sMM)
            sV.inc(V.tensor_copy(dst_ap, pt))
            trqA.append(sV.n)

        def quant_chain(src_ap, dst_ap, n):
            nb = n // 16
            blk = src_ap.rearrange("p (b k) -> p b k", k=16)
            S_ = qs_S[:, 0:nb]
            R_ = qs_S[:, 64:64 + nb]
            C_ = qs_S[:, 128:128 + nb]
            sV.inc(V.tensor_reduce(S_, blk, AX.X, ALU.max,
                                   apply_absolute_value=True))
            vwait(sV)
            sV.inc(V.reciprocal(R_, S_))
            vwait(sV)
            V.tensor_scalar(R_, R_, 6.0, 1e12, ALU.mult, ALU.min)
            V.tensor_scalar(C_, S_, 1.0 / 6.0, 1e-12, ALU.mult, ALU.max)
            V.scalar_tensor_tensor(qs_u[:, 0:n], src_ap, -1.0, src_ap,
                                   ALU.mult, ALU.max)
            V.tensor_tensor(qs_u[:, 0:n].rearrange("p (b k) -> p b k", k=16),
                            qs_u[:, 0:n].rearrange("p (b k) -> p b k", k=16),
                            R_.unsqueeze(2).broadcast_to([128, nb, 16]),
                            ALU.mult)
            swait(sV)
            sSA.inc(SCL.activation(qs_s[:, 0:n], src_ap, AF.Sign))
            V.tensor_single_scalar(qs_a[:, 0:n], qs_u[:, 0:n], 0.25,
                                   ALU.is_gt)
            for c_ in (0.75, 1.25, 1.75):
                V.scalar_tensor_tensor(qs_a[:, 0:n], qs_u[:, 0:n], c_,
                                       qs_a[:, 0:n], ALU.is_gt, ALU.add)
            V.tensor_single_scalar(qs_b[:, 0:n], qs_u[:, 0:n], 2.5, ALU.is_gt)
            for c_ in (3.5, 5.0, 5.0):
                V.scalar_tensor_tensor(qs_b[:, 0:n], qs_u[:, 0:n], c_,
                                       qs_b[:, 0:n], ALU.is_gt, ALU.add)
            V.scalar_tensor_tensor(qs_b[:, 0:n], qs_a[:, 0:n], 0.5,
                                   qs_b[:, 0:n], ALU.mult, ALU.add)
            vwait(sSA)
            V.tensor_tensor(qs_s[:, 0:n].rearrange("p (b k) -> p b k", k=16),
                            qs_s[:, 0:n].rearrange("p (b k) -> p b k", k=16),
                            C_.unsqueeze(2).broadcast_to([128, nb, 16]),
                            ALU.mult)
            sV.inc(V.tensor_tensor(dst_ap, qs_b[:, 0:n], qs_s[:, 0:n],
                                   ALU.mult))

        def task_fp4(dst_hbm, src_hbm, r, width):
            for off in range(0, width, 1024):
                nn = min(1024, width - off)
                ywait(sV)
                ywait(sSA)
                dS.start(SY,
                         out=wtile[:, 0:nn],
                         in_=src_hbm[r * 128:(r + 1) * 128, off:off + nn])
                dS.wait(V)
                dS.wait(SCL)
                vwait(sMM)
                quant_chain(wtile[:, 0:nn], wbuf[:, off:off + nn], nn)
            task_state["free_v"] = sV.n
            dS.wait(V, task_state["free_l"])
            nblk = width // 128
            for b_ in range(nblk):
                transpose128(wbuf[:, b_ * 128:(b_ + 1) * 128],
                             wbuf[:, 2816 + b_ * 128:2816 + (b_ + 1) * 128],
                             coarse=(b_ == 0))
            vflush()
            ywait(sV)
            dS.start(SY,
                     out=dst_hbm[:, :, r * 128:(r + 1) * 128]
                     .rearrange("c p n -> p c n"),
                     in_=wbuf[:, 2816:2816 + nblk * 128]
                     .rearrange("p (c n) -> p c n", n=128))
            task_state["free_l"] = dS.pos()

        def task_int(dst_hbm, src_hbm, r, width):
            gwait(sV)
            dS.wait(G)
            dS.wait(G, task_state["free_l"])
            gwait(sMM)
            dG.start(G, out=wbuf[:, 0:width],
                                in_=src_hbm[r * 128:(r + 1) * 128, :])
            tv = dG.pos()
            nblk = width // 128
            dS.wait(V, task_state["free_l"])
            for b_ in range(nblk):
                dG.wait(T, tv)
                transpose128(wbuf[:, b_ * 128:(b_ + 1) * 128],
                             wbuf[:, 3072 + b_ * 128:3072 + (b_ + 1) * 128],
                             coarse=(b_ == 0))
            vflush()
            ywait(sV)
            dS.start(SY,
                     out=dst_hbm[:, :, r * 128:(r + 1) * 128]
                     .rearrange("c p n -> p c n"),
                     in_=wbuf[:, 3072:3072 + nblk * 128]
                     .rearrange("p (c n) -> p c n", n=128))
            task_state["free_l"] = dS.pos()

        task_state = {"free_v": 0, "free_l": 0}
        tasks = []
        for r in range(22):
            tasks.append(("f", wgq_t, wgate0, r, 1024))
        for r in range(22):
            tasks.append(("f", wuq_t, wup0, r, 1024))
        for r in range(8):
            tasks.append(("f", wdq_t, wdown0, r, 2816))
        for r in range(24):
            tasks.append(("i", wgiT, wgi, r, 1024))
        for r in range(24):
            tasks.append(("i", wuiT, wui, r, 1024))
        for r in range(8):
            tasks.append(("i", wdiT, wdi, r, 3072))
        tasks.reverse()

        def pump(nt):
            for _ in range(nt):
                if not tasks:
                    return
                kind, dh_, sh_, r_, w_ = tasks.pop()
                (task_fp4 if kind == "f" else task_int)(dh_, sh_, r_, w_)


        for dst, src in ((consts, trimask_i), (cbias, bias_tbl_i),
                         (csel, sel_i), (cid_bf, ident_bf_i),
                         (cid_f, ident_f32_i), (cshift, shift_id_i),
                         (conesf, onesf_i), (cbq, bq_pp), (cbk, bk_pp),
                         (cbo, bo_b), (cbv, bv_b), (cwrm, wrms_mix_pp),
                         (cwrf, wrms_ffn_pp), (cbg, bg_pp),
                         (cbv1, bv1_pp), (cbd, bd_pp)):
            dS.start(SY, out=dst[:], in_=src)
        dG.start(G, out=cw2[:], in_=w2blk)
        dS.wait(V)
        dG.wait(V)
        sV.inc(V.memset(vg[:].rearrange("p (j h c) -> p j h c", h=8, c=65)
                        [:, :, :, 64:65], 1.0))
        sV.inc(V.memset(smc[:, 120:128], 0.0))
        if int(os.environ.get("DBGF", "0")) == 8:
            sV.inc(V.memset(smc[:, 0:120], 0.0))
            sV.inc(V.memset(stats[:, 0:64], 0.0))
            sV.inc(V.memset(qs_u[:, 0:48], 0.0))

        def bailout():
            dS.wait(V)
            for tt in range(NT_OWN):
                dT.wait(V)
                sV.inc(V.tensor_copy(xtile[:, 0:D],
                                     acc[:, tt * D:(tt + 1) * D]))
                vflush()
                ywait(sV)
                dT.start(SY, out=yout[tt * 128:(tt + 1) * 128, :],
                         in_=xtile[:, 0:D])
            dT.wait(SY)

        pa_done = {}
        for bt in range(NT_BUF):
            sl = bt % 2
            buf = xtile[:, sl * D:sl * D + D]
            if sl in pa_done:
                ywait(sV, pa_done[sl][0])
                ywait(sSA, pa_done[sl][1])
            dS.start(SY, out=buf,
                                 in_=xin[bt * 128:(bt + 1) * 128, :])
            dS.wait(V)
            dS.wait(SCL)
            sV.inc(V.reduce_sum(smc[:, bt:bt + 1], buf, AX.X))
            swait(sSA)
            sSA.inc(SCL.activation(junk[:], buf, AF.Square,
                                   accum_out=smc[:, 32 + bt:32 + bt + 1]))
            pa_done[sl] = (sV.n, sSA.n)
        vwait(sV)
        vwait(sSA)
        sV.inc(V.tensor_scalar(stats[:, 0:32], smc[:, 0:32], 1.0 / D, None,
                               ALU.mult))
        vwait(sV)
        sV.inc(V.tensor_scalar(smc[:, 64:96], smc[:, 32:64], 1.0 / D, 1e-5,
                               ALU.mult, ALU.add))
        vwait(sV)
        sV.inc(V.tensor_tensor(smc[:, 0:32], stats[:, 0:32], stats[:, 0:32],
                               ALU.mult))
        vwait(sV)
        sV.inc(V.tensor_sub(smc[:, 32:64], smc[:, 64:96], smc[:, 0:32]))
        vwait(sV)
        sV.inc(V.reciprocal(smc[:, 64:96], smc[:, 32:64]))
        vwait(sV)
        vflush()
        vwait(sV)
        swait(sV)
        sSA.inc(SCL.activation(stats[:, 32:64], smc[:, 64:96], AF.Sqrt))
        sflush()
        vwait(sSA)
        swait(sSA)
        pc_done = {}
        xq_st = {}
        _otorder = list(range(NT_OWN))
        if os.environ.get("KREV"):
            _otorder = _otorder[::-1]
        for ot in _otorder:
            bt = 16 + ot
            sl = ot % 2
            buf = xtile[:, sl * D:sl * D + D]
            if sl in pc_done:
                ywait(sV, pc_done[sl][0])
                ywait(sSA, pc_done[sl][1])
            else:
                ywait(sV)
                ywait(sSA)
            dS.start(SY, out=buf,
                                 in_=xin[bt * 128:(bt + 1) * 128, :])
            dS.wait(V)
            dS.wait(SCL)
            sV.inc(V.tensor_copy(resid[:, ot * D:(ot + 1) * D], buf))
            V.tensor_scalar(fs1[:], buf, stats[:, bt:bt + 1],
                            stats[:, 32 + bt:32 + bt + 1],
                            ALU.subtract, ALU.mult)
            if KP6 >= 1:
                dT.wait(V, xq_st.get(sl, 0))
                quant_chain(fs1[:], bstage[:, sl * D:(sl + 1) * D], D)
                vflush()
                ywait(sV)
                dT.start(SY, out=xq_sp[ot, :, :],
                                     in_=bstage[:, sl * D:(sl + 1) * D])
                xq_st[sl] = dT.pos()
            pc_done[sl] = (sV.n, sSA.n)
        barrier()

        if KPH < 2:
            bailout()
            return
        for g in range(4 if KPH >= 4 else 1):
            dG.start(G, 
                out=wbuf[:, 0:2048].rearrange("p (c n) -> p c n", n=256),
                in_=Wk[:, g * 256:(g + 1) * 256]
                    .rearrange("(c p) n -> p c n", p=128))
            dG.start(G, 
                out=wbuf[:, 2048:4096].rearrange("p (c n) -> p c n", n=256),
                in_=Wv[:, g * 256:(g + 1) * 256]
                    .rearrange("(c p) n -> p c n", p=128))
            dG.start(G, 
                out=wbuf[:, 4096:6144].rearrange("p (c n) -> p c n", n=256),
                in_=Wq[:, g * 256:(g + 1) * 256]
                    .rearrange("(c p) n -> p c n", p=128))
            wk_ap = wbuf[:, 0:2048].rearrange("p (c n) -> p c n", n=256)
            wv_ap = wbuf[:, 2048:4096].rearrange("p (c n) -> p c n", n=256)
            wq_ap = wbuf[:, 4096:6144].rearrange("p (c n) -> p c n", n=256)
            bst = {}
            for t4 in range(8):
                for q in range(4):
                    bt = t4 * 4 + q
                    sl = q % 2
                    buf = xtile[:, sl * D:sl * D + D]
                    if sl in bst:
                        ywait(sV, bst[sl])
                    else:
                        ywait(sV)
                        ywait(sSA)
                    dS.start(SY, 
                        out=buf, in_=xin[bt * 128:(bt + 1) * 128, :])
                    dS.wait(V)
                    vwait(sMM)
                    sV.inc(V.tensor_scalar(
                        bstage[:, sl * D:sl * D + D], buf,
                        stats[:, bt:bt + 1], stats[:, 32 + bt:32 + bt + 1],
                        ALU.subtract, ALU.mult))
                    bst[sl] = sV.n
                    for dc in range(8):
                        transpose128(
                            bstage[:, sl * D + dc * 128:sl * D + dc * 128
                                   + 128],
                            xnt4[:, dc * 512 + q * 128:dc * 512 + q * 128
                                 + 128], coarse=(dc == 0))
                xnt = xnt4[:].rearrange("p (c n) -> p c n", n=512)
                dG.wait(T)
                for p in range(min(2 if KP2 >= 2 else 0,
                                   KP3 - t4 if KP3 < 99 else 2)):
                    pk, sl_ = (psY0, slY0) if p % 2 == 0 else (psY1, slY1)
                    mm_acc(pk[:, 0:512],
                           [(wk_ap[:, dc, p * 128:(p + 1) * 128],
                             xnt[:, dc, :]) for dc in range(8)], slot=sl_)
                    vwait(sMM)
                    if KP4 >= 1:
                        if KP4 == 2:
                            sV.inc(V.tensor_copy(
                                kg[:, p * S_BUF + t4 * 512:p * S_BUF
                                   + (t4 + 1) * 512], pk[:, 0:512]))
                        elif KP4 == 3:
                            sV.inc(V.tensor_copy(kg[:, 0:512],
                                                 pk[:, 0:512]))
                        else:
                            sV.inc(V.tensor_scalar(
                                kg[:, p * S_BUF + t4 * 512:p * S_BUF
                                   + (t4 + 1) * 512],
                                pk[:, 0:512],
                                cbk[:, g * 2 + p:g * 2 + p + 1],
                                None, ALU.add))
                        sl_.consumed(sV)
                        if KP4 == 4:
                            twait(sV)
                for q in range(4 if KP2 >= 3 else 0):
                    j = t4 * 4 + q
                    pk, sl_ = (psY0, slY0) if q % 2 == 0 else (psY1, slY1)
                    mm_acc(pk[:, 0:256],
                           [(xnt[:, dc, q * 128:(q + 1) * 128],
                             wv_ap[:, dc, :]) for dc in range(8)], slot=sl_)
                    vwait(sMM)
                    dst = vg.rearrange("p (j h c) -> p j h c", h=4, c=65)[
                        :, j, :, 0:64]
                    sV.inc(V.tensor_add(
                        dst,
                        pk[:, 0:256].rearrange("p (h c) -> p h c", c=64),
                        cbv[:, g * 256:(g + 1) * 256]
                            .rearrange("p (h c) -> p h c", c=64)))
                    sl_.consumed(sV)
                if t4 >= 4 and KP2 >= 4:
                    for p in range(2):
                        mm_acc(psY0[:, 0:512],
                               [(wq_ap[:, dc, p * 128:(p + 1) * 128],
                                 xnt[:, dc, :]) for dc in range(8)],
                               slot=slY0)
                        vwait(sMM)
                        sV.inc(V.tensor_scalar(
                            bstage[:, 3072:3072 + 512], psY0[:, 0:512],
                            cbq[:, g * 2 + p:g * 2 + p + 1], None, ALU.add))
                        slY0.consumed(sV)
                        twait(sV)
                        mm_acc(psY1[:, 0:512],
                               [(cw2[:], bstage[:, 3072:3072 + 512])],
                               slot=slY1)
                        vwait(sMM)
                        sV.inc(V.tensor_copy(
                            qcg[:, p * T_OWN + (t4 - 4) * 512:
                                p * T_OWN + (t4 - 3) * 512],
                            psY1[:, 0:512]))
                        slY1.consumed(sV)
            barrier()
            if g == int(os.environ.get("DBGP", "-1")):
                _db_off = {"ylat": 20608, "kg": 0, "kg2": 4096,
                           "qcg": 16512}[os.environ.get("DBGW", "ylat")]
                for half in range(4):
                    dS.wait(V)
                    sV.inc(V.tensor_copy(
                        fs1[:, 0:1024],
                        arena[:, _db_off + half * 1024:_db_off
                              + (half + 1) * 1024]))
                    ywait(sV)
                    dS.start(SY, 
                        out=dbg[:, half * 1024:(half + 1) * 1024],
                        in_=fs1[:, 0:1024])
                dS.wait(SY)
            if KPH < 3:
                bailout()
                return
            vhat = vg.rearrange("p (j h c) -> p j h c", h=4, c=65)
            for c in range(4):
                for p in range(2):
                    jmax = 19 + 4 * c
                    expv = {}
                    avv = {}
                    slY0.acquire(T)
                    slY1.acquire(T)
                    lvp = sV.n
                    exp0 = sSA.n
                    for j in range(jmax + 1):
                        pS, slp = (psS, slS) if j % 2 == 0 else (psS2, slS2)
                        twait(sV, lvp)
                        twait(sSA, expv[j - 1] if j >= 1 else exp0)
                        slp.acquire(T)
                        T.matmul(pS[:, 0:512],
                                 kg[0:64, p * S_BUF + j * 128:
                                    p * S_BUF + (j + 1) * 128],
                                 qcg[0:64, p * T_OWN + c * 512:
                                     p * T_OWN + (c + 1) * 512],
                                 start=True, stop=True, tile_position=(0, 0))
                        sMM.inc(T.matmul(
                            pS[:, 512:1024],
                            kg[64:128, p * S_BUF + j * 128:
                               p * S_BUF + (j + 1) * 128],
                            qcg[64:128, p * T_OWN + c * 512:
                                p * T_OWN + (c + 1) * 512],
                            start=True, stop=True, tile_position=(64, 0)))
                        qkv = sMM.n
                        d_ = j - 16 - 4 * c
                        if 0 <= d_ <= 3:
                            vwait(sMM, qkv)
                            sV.inc(V.tensor_tensor(
                                pS[:].rearrange("p (h n) -> p h n", n=512),
                                pS[:].rearrange("p (h n) -> p h n", n=512),
                                consts[:, d_ * 512:(d_ + 1) * 512]
                                .unsqueeze(1).broadcast_to([128, 2, 512]),
                                ALU.add))
                            lvp = sV.n
                            swait(sV)
                        else:
                            swait(sMM, qkv)
                        slot = j % 2
                        if j >= 2:
                            swait(sMM, avv[j - 2])
                        else:
                            swait(sMM)
                        sSA.inc(SCL.activation(
                            pTb[:, slot * 1024:(slot + 1) * 1024], pS[:],
                            AF.Exp, bias=cbias[:, j:j + 1], scale=0.125))
                        expv[j] = sSA.n
                        slp.consumed(sSA)
                        twait(sSA, expv[j])
                        twait(sV, lvp)
                        T.matmul(psY0[0:65, 0:512],
                                 vhat[:, j, 2 * p, :].opt(),
                                 pTb[:, slot * 1024:slot * 1024 + 512],
                                 start=(j == 0), stop=(j == jmax))
                        sMM.inc(T.matmul(
                            psY1[0:65, 0:512],
                            vhat[:, j, 2 * p + 1, :].opt(),
                            pTb[:, slot * 1024 + 512:(slot + 1) * 1024],
                            start=(j == 0), stop=(j == jmax)))
                        avv[j] = sMM.n
                    vwait(sMM)
                    sV.inc(V.tensor_copy(rden[64:65, 0:512], psY0[64:65, :]))
                    sV.inc(V.tensor_copy(rden[64:65, 512:1024],
                                         psY1[64:65, :]))
                    vflush()
                    ywait(sV)
                    dS.start(SY, out=fs1[0:1, 0:1024],
                                         in_=rden[64:65, 0:1024])
                    dS.wait(V)
                    sV.inc(V.reciprocal(rden[0:1, 0:1024],
                                        fs1[0:1, 0:1024]))
                    vwait(sV)
                    twait(sV)
                    twait(sSA)
                    slS.acquire(T)
                    T.matmul(psS[:, 0:512], conesf[0:1, :],
                             rden[0:1, 0:512], start=True, stop=True)
                    sMM.inc(T.matmul(psS[:, 512:1024], conesf[0:1, :],
                                     rden[0:1, 512:1024], start=True,
                                     stop=True))
                    vwait(sMM)
                    sV.inc(V.tensor_copy(rden[:, 0:1024], psS[:]))
                    slS.consumed(sV)
                    yco = p * T_OWN + c * 512
                    sV.inc(V.tensor_mul(ylat[0:64, yco:yco + 512],
                                        psY0[0:64, :], rden[0:64, 0:512]))
                    slY0.consumed(sV)
                    vwait(sMM)
                    sV.inc(V.tensor_mul(bstage[0:64, 3072:3072 + 512],
                                        psY1[0:64, :], rden[0:64, 512:1024]))
                    slY1.consumed(sV)
                    twait(sV)
                    twait(sSA)
                    slS.acquire(T)
                    sMM.inc(T.matmul(psS[:, 0:512], cshift[:],
                                     bstage[0:64, 3072:3072 + 512],
                                     start=True, stop=True))
                    vwait(sMM)
                    sV.inc(V.tensor_copy(ylat[64:128, yco:yco + 512],
                                         psS[64:128, 0:512]))
                    slS.consumed(sV)
                    pump(4)
            barrier()
            if g == int(os.environ.get("DBGG", "-1")):
                _db_off = {"ylat": 20608, "kg": 0, "kg2": 4096,
                           "qcg": 16512}[os.environ.get("DBGW", "ylat")]
                for half in range(4):
                    dS.wait(V)
                    sV.inc(V.tensor_copy(
                        fs1[:, 0:1024],
                        arena[:, _db_off + half * 1024:_db_off + (half + 1) * 1024]))
                    ywait(sV)
                    dS.start(SY, 
                        out=dbg[:, half * 1024:(half + 1) * 1024],
                        in_=fs1[:, 0:1024])
                dS.wait(SY)
            gwait(sMM)
            dS.wait(G)
            dG.start(G, 
                out=wbuf[:, 0:2048].rearrange("p (c n) -> p c n", n=1024),
                in_=Wo[g * 256:(g + 1) * 256, :]
                    .rearrange("(c p) n -> p c n", p=128))
            dG.wait(T)
            for tt in range(NT_OWN):
                for dh in range(2):
                    pk, sl_ = (psY0, slY0) if (tt * 2 + dh) % 2 == 0 \
                        else (psY1, slY1)
                    mm_acc(pk[:, 0:512],
                           [(ylat[:, pq * T_OWN + tt * 128:
                                  pq * T_OWN + (tt + 1) * 128],
                             wbuf[:, pq * 1024 + dh * 512:
                                  pq * 1024 + (dh + 1) * 512])
                            for pq in range(2)], slot=sl_)
                    vwait(sMM)
                    dst = acc[:, tt * D + dh * 512:tt * D + (dh + 1) * 512]
                    if g == 0:
                        sV.inc(V.tensor_add(dst, pk[:, 0:512],
                                            cbo[:, dh * 512:(dh + 1) * 512]))
                    else:
                        sV.inc(V.tensor_add(dst, pk[:, 0:512], dst))
                    sl_.consumed(sV)
            barrier()

        if KPH < 5:
            bailout()
            return
        pump(len(tasks))
        barrier()

        if KPH < 6:
            bailout()
            return
        if int(os.environ.get("DBGF", "0")) == 7:
            for t_ in range(4):
                dG.start(G, out=dbg[:, t_ * 1024:(t_ + 1) * 1024],
                         in_=xq_sp[int(os.environ.get("DBGO", "12"))
                                   + t_, :, :])
            gwait(sV)
            dG.start(G, out=dbg[:, 4096:5120], in_=bstage[:, 0:1024])
            dG.start(G, out=dbg[:, 5120:6144], in_=bstage[:, 1024:2048])
            dG.wait(G)
        hq = bigA[:].rearrange("p (t n) -> p t n", n=3072)
        ioffs = [(0, 512), (512, 512), (1024, 512), (1536, 512),
                 (2048, 512), (2560, 256)]
        for sup in range(4):
            ywait(sV)
            ywait(sMM)
            for t_ in range(4):
                dS.start(SY, out=bstage[:, t_ * D:(t_ + 1) * D],
                         in_=xq_sp[sup * 4 + t_, :, :])
            dS.wait(T)
            dS.wait(V)
            for q in range(4):
                for dc in range(8):
                    transpose128(bstage[:, q * D + dc * 128:
                                        q * D + dc * 128 + 128],
                                 xnt4[:, dc * 512 + q * 128:
                                      dc * 512 + q * 128 + 128],
                                 coarse=(q == 0 and dc == 0))
            xnt = xnt4[:].rearrange("p (c n) -> p c n", n=512)
            if (sup == int(os.environ.get("DBGSUP", "0"))
                    and int(os.environ.get("DBGF", "0")) in (4, 6)):
                _dsrc = bstage if os.environ["DBGF"] == "6" else xnt4
                for half in range(4):
                    dS.wait(V)
                    vwait(sV)
                    sV.inc(V.tensor_copy(
                        xtile[:, 0:1024],
                        _dsrc[:, half * 1024:(half + 1) * 1024]))
                    ywait(sV)
                    dS.start(SY, out=dbg[:, half * 1024:(half + 1) * 1024],
                             in_=xtile[:, 0:1024])
                dS.wait(SY)
                dS.wait(V)
                dS.wait(SCL)
                dS.wait(T)
                dS.wait(G)
            wld = {}
            for ic, (io, il) in enumerate(ioffs):
                sl = ic % 2
                ywait(sMM)
                dS.start(SY,
                         out=bigB[:, sl * 4096:sl * 4096 + 8 * il]
                         .rearrange("p (c n) -> p c n", n=il),
                         in_=wgq_t[:, :, io:io + il]
                         .rearrange("c p n -> p c n"))
                dS.start(SY,
                         out=wbuf[:, sl * 4096:sl * 4096 + 8 * il]
                         .rearrange("p (c n) -> p c n", n=il),
                         in_=wuq_t[:, :, io:io + il]
                         .rearrange("c p n -> p c n"))
                wg_ap = bigB[:, sl * 4096:sl * 4096 + 8 * il] \
                    .rearrange("p (c n) -> p c n", n=il)
                wu_ap = wbuf[:, sl * 4096:sl * 4096 + 8 * il] \
                    .rearrange("p (c n) -> p c n", n=il)
                dS.wait(T)
                for q in range(4):
                    slS.acquire(T)
                    mm_acc(psS[:, 0:il],
                           [(xnt[:, dc, q * 128:(q + 1) * 128],
                             wg_ap[:, dc, :]) for dc in range(8)])
                    mm_acc(psS[:, 512:512 + il],
                           [(xnt[:, dc, q * 128:(q + 1) * 128],
                             wu_ap[:, dc, :]) for dc in range(8)])
                    wld[sl] = sMM.n
                    swait(sMM)
                    swait(sV)
                    sSA.inc(SCL.activation(pTb[:, 0:il], psS[:, 0:il],
                                           AF.Silu))
                    slS.consumed(sSA)
                    vwait(sMM)
                    vwait(sSA)
                    _dbf = int(os.environ.get("DBGF", "0"))
                    if _dbf == 2:
                        sV.inc(V.tensor_copy(fs1[:, 0:il], psS[:, 0:il]))
                    elif _dbf == 3:
                        sV.inc(V.tensor_copy(fs1[:, 0:il],
                                             psS[:, 512:512 + il]))
                    else:
                        sV.inc(V.tensor_tensor(fs1[:, 0:il],
                                               psS[:, 512:512 + il],
                                               pTb[:, 0:il], ALU.mult))
                    slS.consumed(sV)
                    vwait(sV)
                    if _dbf in (2, 3):
                        sV.inc(V.tensor_copy(hq[:, q, io:io + il],
                                             fs1[:, 0:il]))
                    else:
                        quant_chain(fs1[:, 0:il], hq[:, q, io:io + il], il)
            if (sup == int(os.environ.get("DBGSUP", "0"))
                    and int(os.environ.get("DBGF", "0")) == 1):
                for half in range(3):
                    dS.wait(V)
                    vwait(sV)
                    sV.inc(V.tensor_copy(
                        xtile[:, 0:1024],
                        bigA[:, int(os.environ.get("DBGQ", "0")) * 3072
                             + half * 1024:
                             int(os.environ.get("DBGQ", "0")) * 3072
                             + (half + 1) * 1024]))
                    ywait(sV)
                    dS.start(SY, out=dbg[:, half * 1024:(half + 1) * 1024],
                             in_=xtile[:, 0:1024])
                dS.wait(SY)
                dS.wait(V)
                dS.wait(SCL)
                dS.wait(T)
                dS.wait(G)
            for dh in range(2):
                ywait(sMM)
                dS.start(SY, 
                    out=bigB[:, 0:22 * 512].rearrange("p (c n) -> p c n",
                                                      n=512),
                    in_=wdq_t[:, :, dh * 512:(dh + 1) * 512]
                        .rearrange("c p n -> p c n"))
                wd_ap = bigB[:, 0:22 * 512].rearrange("p (c n) -> p c n",
                                                      n=512)
                for q in range(4):
                    tt = sup * 4 + q
                    for icc in range(22):
                        transpose128(hq[:, q, icc * 128:(icc + 1) * 128],
                                     xnt4[:, icc * 128:(icc + 1) * 128],
                                     coarse=(icc == 0))
                    twait(sV)
                    dS.wait(T)
                    pk, sl_ = (psY0, slY0) if q % 2 == 1 else (psY1, slY1)
                    mm_acc(pk[:, 0:512],
                           [(xnt4[:, icc * 128:(icc + 1) * 128],
                             wd_ap[:, icc, :]) for icc in range(22)],
                           slot=sl_)
                    vwait(sMM)
                    dst = acc[:, tt * D + dh * 512:tt * D + (dh + 1) * 512]
                    sV.inc(V.tensor_add(dst, pk[:, 0:512], dst))
                    sl_.consumed(sV)
        barrier()

        if KPH < 7:
            bailout()
            return
        for tt in range(NT_OWN):
            r_ = resid[:, tt * D:(tt + 1) * D]
            sV.inc(V.tensor_add(r_, r_, acc[:, tt * D:(tt + 1) * D]))
            vflush()
            vwait(sV)
            swait(sV)
            sV.inc(V.reduce_sum(smc[:, tt:tt + 1], r_, AX.X))
            swait(sSA)
            sSA.inc(SCL.activation(junk[:], r_, AF.Square,
                                   accum_out=smc[:, 32 + tt:32 + tt + 1]))
        vwait(sV)
        sflush()
        vwait(sSA)
        if int(os.environ.get("DBGF", "0")) == 8:
            sV.inc(V.tensor_copy(qs_u[:, 0:48], smc[:, 0:48]))
            vwait(sV)
        sV.inc(V.tensor_scalar(stats[:, 0:16], smc[:, 0:16], 1.0 / D, None,
                               ALU.mult))
        vwait(sV)
        sV.inc(V.tensor_scalar(smc[:, 64:80], smc[:, 32:48], 1.0 / D, 1e-5,
                               ALU.mult, ALU.add))
        vwait(sV)
        sV.inc(V.tensor_tensor(smc[:, 0:16], stats[:, 0:16], stats[:, 0:16],
                               ALU.mult))
        vwait(sV)
        sV.inc(V.tensor_sub(smc[:, 32:48], smc[:, 64:80], smc[:, 0:16]))
        vwait(sV)
        sV.inc(V.reciprocal(smc[:, 64:80], smc[:, 32:48]))
        vwait(sV)
        vflush()
        vwait(sV)
        swait(sV)
        sSA.inc(SCL.activation(stats[:, 32:48], smc[:, 64:80], AF.Sqrt))
        sflush()
        if KP5 < 2:
            bailout()
            return
        vwait(sSA)
        swait(sSA)
        for tt in range(NT_OWN):
            sV.inc(V.tensor_scalar(acc[:, tt * D:(tt + 1) * D],
                                   resid[:, tt * D:(tt + 1) * D],
                                   stats[:, tt:tt + 1],
                                   stats[:, 32 + tt:32 + tt + 1],
                                   ALU.subtract, ALU.mult))
            vflush()
            swait(sV)
            swait(sSA)
            sSA.inc(SCL.activation(junk[:], acc[:, tt * D:(tt + 1) * D],
                                   AF.Square,
                                   accum_out=smc[:, 96 + tt:96 + tt + 1]))
        sflush()
        vwait(sSA)
        sV.inc(V.tensor_scalar(smc[:, 64:80], smc[:, 96:112], 1.0 / D, 1e-6,
                               ALU.mult, ALU.add))
        vwait(sV)
        sV.inc(V.reciprocal(smc[:, 80:96], smc[:, 64:80]))
        vwait(sV)
        vflush()
        vwait(sV)
        swait(sV)
        sSA.inc(SCL.activation(stats[:, 48:64], smc[:, 80:96], AF.Sqrt))
        sflush()
        vwait(sSA)
        for tt in range(NT_OWN):
            for dc in range(8):
                transpose128(acc[:, tt * D + dc * 128:tt * D + dc * 128
                                 + 128],
                             bstage[:, dc * 128:(dc + 1) * 128],
                             coarse=(dc == 0))
            vflush()
            ywait(sV)
            dS.start(SY,
                out=xn1_sp[:, :, tt * 128:(tt + 1) * 128]
                    .rearrange("c p n -> p c n"),
                in_=bstage[:, 0:1024].rearrange("p (c n) -> p c n", n=128))
            vwait(sMM)
            sV.inc(V.tensor_scalar(bstage[:, 1024:2048],
                                   acc[:, tt * D:(tt + 1) * D],
                                   stats[:, 48 + tt:48 + tt + 1], None,
                                   ALU.mult))
            dT.wait(V)
            for dc in range(8):
                transpose128(bstage[:, 1024 + dc * 128:1024 + dc * 128
                                    + 128], pTb[:, 0:128],
                             coarse=(dc == 0))
                sV.inc(V.tensor_scalar(
                    bstage[:, 2048 + dc * 128:2048 + (dc + 1) * 128],
                    pTb[:, 0:128], cwrm[:, dc:dc + 1], None, ALU.mult))
            vflush()
            ywait(sV)
            dT.start(SY,
                out=xnm_sp[:, :, tt * 128:(tt + 1) * 128]
                    .rearrange("c p n -> p c n"),
                in_=bstage[:, 2048:3072].rearrange("p (c n) -> p c n",
                                                   n=128))
        barrier()

        if int(os.environ.get("DBGF", "0")) == 8:
            vwait(sSA)
            sV.inc(V.tensor_copy(xtile[:, 0:64], stats[:, 0:64]))
            sV.inc(V.tensor_copy(xtile[:, 64:192], smc[:, 0:128]))
            sV.inc(V.tensor_copy(xtile[:, 192:240], qs_u[:, 0:48]))
            vflush()
            ywait(sV)
            dS.start(SY, out=dbg[:, 0:240], in_=xtile[:, 0:240])
            dS.wait(SY)
        if KP5 < 3:
            bailout()
            return
        gvd_ld = {}
        for cc in range(8):
            sl = cc % 2
            if sl in gvd_ld:
                gwait(sMM, gvd_ld[sl])
            wga = wbuf[:, sl * 4096:sl * 4096 + 8 * 384] \
                .rearrange("p (c n) -> p c n", n=384)
            dG.start(G, 
                out=wga[:, :, 0:128],
                in_=Wg1[:, cc * 128:(cc + 1) * 128]
                    .rearrange("(c p) n -> p c n", p=128))
            dG.start(G, 
                out=wga[:, :, 128:256],
                in_=Wv1[:, cc * 128:(cc + 1) * 128]
                    .rearrange("(c p) n -> p c n", p=128))
            dG.start(G, 
                out=wga[:, :, 256:384],
                in_=Wd1[:, cc * 128:(cc + 1) * 128]
                    .rearrange("(c p) n -> p c n", p=128))
            dG.wait(T)
            for st4 in range(4):
                ywait(sMM)
                dS.start(SY, 
                    out=xnt4[:].rearrange("p (c n) -> p c n", n=512),
                    in_=xnm_sp[:, :, st4 * 512:(st4 + 1) * 512]
                        .rearrange("c p n -> p c n"))
                xnt = xnt4[:].rearrange("p (c n) -> p c n", n=512)
                dS.wait(T)
                slS.acquire(T)
                slS2.acquire(T)
                mm_acc(psS[:, 0:512],
                       [(wga[:, dc, 0:128], xnt[:, dc, :])
                        for dc in range(8)])
                mm_acc(psS[:, 512:1024],
                       [(wga[:, dc, 128:256], xnt[:, dc, :])
                        for dc in range(8)])
                mm_acc(psS2[:, 0:512],
                       [(wga[:, dc, 256:384], xnt[:, dc, :])
                        for dc in range(8)])
                gvd_ld[sl] = sMM.n
                swait(sMM)
                swait(sV)
                sSA.inc(SCL.activation(pTb[:, 0:512], psS[:, 0:512],
                                       AF.Sigmoid, bias=cbg[:, cc:cc + 1]))
                sSA.inc(SCL.activation(pTb[:, 512:1024], psS[:, 512:1024],
                                       AF.Tanh, bias=cbv1[:, cc:cc + 1]))
                sSA.inc(SCL.activation(fs1[:, 0:512], psS2[:, 0:512],
                                       AF.Sigmoid, bias=cbd[:, cc:cc + 1]))
                slS.consumed(sSA)
                slS2.consumed(sSA)
                vwait(sSA)
                sV.inc(V.tensor_mul(xs_c[:, st4 * 512:(st4 + 1) * 512],
                                    pTb[:, 0:512], pTb[:, 512:1024]))
                sV.inc(V.tensor_scalar(a_c[:, st4 * 512:(st4 + 1) * 512],
                                       fs1[:, 0:512], 0.998, 0.001,
                                       ALU.mult, ALU.add))
            vwait(sMM)
            sV.inc(V.tensor_tensor_scan(h_c[:], a_c[:], xs_c[:], 0.0,
                                        ALU.mult, ALU.add))
            vwait(sV)
            sV.inc(V.tensor_copy(carry[:, cc:cc + 1], h_c[:, 2047:2048]))
            sV.inc(V.tensor_tensor_scan(
                acum[:, cc * T_OWN:(cc + 1) * T_OWN], a_c[:], a_c[:], 1.0,
                ALU.mult, ALU.bypass))
            vwait(sV)
            sV.inc(V.tensor_copy(xs_c[:], h_c[:]))
            for tt in range(NT_OWN):
                transpose128(xs_c[:, tt * 128:(tt + 1) * 128], pTb[:, 0:128],
                             coarse=(tt == 0))
                dst = acc[:, tt * D + cc * 128:tt * D + cc * 128 + 128]
                sV.inc(V.tensor_add(dst, pTb[:, 0:128], dst))
        barrier()
        if KP5 < 4:
            bailout()
            return
        dG.start(G, out=cin_b, in_=carry[:])
        dG.wait(G)
        sCC.inc(G.collective_compute(
            "AllGather", ALU.bypass, replica_groups=[list(range(8))],
            ins=[cin_b.opt()], outs=[cout_b.opt()]))
        gwait(sCC)
        dG.start(G, 
            out=crecv[:].rearrange("p (r c) -> p r c", c=8),
            in_=cout_b.rearrange("(r p) c -> p r c", p=128))
        dG.wait(V)
        sV.inc(V.tensor_mul(crecv[:], crecv[:], csel[:]))
        vwait(sV)
        sV.inc(V.tensor_reduce(cin_sb[:], crecv[:]
                               .rearrange("p (r c) -> p c r", c=8),
                               AX.X, ALU.add))
        vwait(sV)
        for cc in range(8):
            vwait(sMM)
            sV.inc(V.tensor_scalar(xs_c[:],
                                   acum[:, cc * T_OWN:(cc + 1) * T_OWN],
                                   cin_sb[:, cc:cc + 1], None, ALU.mult))
            for tt in range(NT_OWN):
                transpose128(xs_c[:, tt * 128:(tt + 1) * 128], pTb[:, 0:128],
                             coarse=(tt == 0))
                dst = acc[:, tt * D + cc * 128:tt * D + cc * 128 + 128]
                sV.inc(V.tensor_add(dst, pTb[:, 0:128], dst))
        barrier()
        if KP5 < 5:
            bailout()
            return
        for tt in range(NT_OWN):
            swait(sV)
            swait(sSA)
            sSA.inc(SCL.activation(junk[:], acc[:, tt * D:(tt + 1) * D],
                                   AF.Square, accum_out=smc[:, tt:tt + 1]))
        sflush()
        vwait(sSA)
        sV.inc(V.tensor_scalar(smc[:, 32:48], smc[:, 0:16], 1.0 / D, 1e-6,
                               ALU.mult, ALU.add))
        vwait(sV)
        sV.inc(V.reciprocal(smc[:, 64:80], smc[:, 32:48]))
        vwait(sV)
        vflush()
        vwait(sV)
        swait(sV)
        sSA.inc(SCL.activation(stats[:, 0:16], smc[:, 64:80], AF.Sqrt))
        sflush()
        barrier()

        def emit_ffn(variant):
            ffn_ev = {}
            prod_v = [0]
            for sup in range(4):
                if variant == "A":
                    for q in range(4):
                        tt = sup * 4 + q
                        vwait(sMM)
                        sV.inc(V.tensor_scalar(
                            bstage[:, 0:D], acc[:, tt * D:(tt + 1) * D],
                            stats[:, tt:tt + 1], None, ALU.mult))
                        for dc in range(8):
                            transpose128(bstage[:, dc * 128:(dc + 1) * 128],
                                         pTb[:, 0:128], coarse=(dc == 0))
                            sV.inc(V.tensor_scalar(
                                xnt4[:, dc * 512 + q * 128:
                                     dc * 512 + q * 128 + 128],
                                pTb[:, 0:128], cwrf[:, dc:dc + 1], None,
                                ALU.mult))
                else:
                    ywait(sMM)
                    dS.start(SY, 
                        out=xnt4[:].rearrange("p (c n) -> p c n", n=512),
                        in_=xn1_sp[:, :, sup * 512:(sup + 1) * 512]
                            .rearrange("c p n -> p c n"))
                xnt = xnt4[:].rearrange("p (c n) -> p c n", n=512)
                for fh in range(2):
                    if variant == "A":
                        gwait(sMM)
                    else:
                        ywait(sMM)
                    for f12 in range(12):
                        f = fh * 12 + f12
                        if variant == "A":
                            dG.start(G, 
                                out=bigB[:, f12 * 1024:(f12 + 1) * 1024],
                                in_=Wout1[f * 128:(f + 1) * 128, :])
                        else:
                            dS.start(SY, 
                                out=bigB[:, f12 * 1024:(f12 + 1) * 1024],
                                in_=wdiT[f, :, :])
                    for f12 in range(12):
                        f = fh * 12 + f12
                        sl = f12 % 2
                        wgu = wbuf[:, sl * 4096:sl * 4096 + 8 * 256] \
                            .rearrange("p (c n) -> p c n", n=256)
                        if variant == "A":
                            gwait(sMM)
                            dG.start(G, 
                                out=wgu[:, :, 0:128],
                                in_=Wgate1[:, f * 128:(f + 1) * 128]
                                    .rearrange("(c p) n -> p c n", p=128))
                            dG.start(G, 
                                out=wgu[:, :, 128:256],
                                in_=Wup1[:, f * 128:(f + 1) * 128]
                                    .rearrange("(c p) n -> p c n", p=128))
                            dG.wait(T)
                        else:
                            ywait(sMM)
                            dS.start(SY, 
                                out=wgu[:, :, 0:128],
                                in_=wgiT[:, :, f * 128:(f + 1) * 128]
                                    .rearrange("c p n -> p c n"))
                            dS.start(SY, 
                                out=wgu[:, :, 128:256],
                                in_=wuiT[:, :, f * 128:(f + 1) * 128]
                                    .rearrange("c p n -> p c n"))
                            dS.wait(T)
                        twait(sV, prod_v[0])
                        slS.acquire(T)
                        mm_acc(psS[:, 0:512],
                               [(wgu[:, dc, 0:128], xnt[:, dc, :])
                                for dc in range(8)])
                        mm_acc(psS[:, 512:1024],
                               [(wgu[:, dc, 128:256], xnt[:, dc, :])
                                for dc in range(8)])
                        swait(sMM)
                        swait(sV)
                        sSA.inc(SCL.activation(
                            pTb[:, 0:512], psS[:, 0:512], AF.Silu,
                            scale=(1.0 if variant == "A" else 1.0 / 127.0)))
                        slS.consumed(sSA)
                        vwait(sMM)
                        vwait(sSA)
                        sV.inc(V.scalar_tensor_tensor(
                            bigA[:, f12 * 512:(f12 + 1) * 512],
                            psS[:, 512:1024],
                            (1.0 if variant == "A" else 1.0 / 16129.0),
                            pTb[:, 0:512], ALU.mult, ALU.mult))
                        slS.consumed(sV)
                        prod_v[0] = sV.n
                    prod_all = sV.n
                    if (int(os.environ.get("DBGF", "0")) == 12
                            and variant == "A" and sup == 0
                            and fh == int(os.environ.get("DBGFH", "0"))):
                        dG.wait(T)
                        dS.wait(T)
                        gwait(sMM)
                        dG.start(G, out=dbg[:, 0:8192], in_=bigB[:, 0:8192])
                        dG.wait(V)
                        dG.wait(T)
                    if (int(os.environ.get("DBGF", "0")) in (10, 11)
                            and variant == ("B" if int(os.environ.get(
                                "DBGF", "0")) == 10 else "A") and sup == 0
                            and fh == int(os.environ.get("DBGFH", "0"))):
                        vflush()
                        gwait(sV)
                        dG.start(G, out=dbg[:, 0:6144], in_=bigA[:, 0:6144])
                        dG.wait(V)
                        dG.wait(T)
                    dS.wait(T)
                    dG.wait(T)
                    for tt4 in range(4):
                        tt = sup * 4 + tt4
                        for dh in range(2):
                            pk, sl_ = (psY0, slY0) \
                                if (tt4 * 2 + dh) % 2 == 0 else (psY1, slY1)
                            twait(sV, prod_all)
                            mm_acc(pk[:, 0:512],
                                   [(bigA[:, f12 * 512 + tt4 * 128:
                                          f12 * 512 + tt4 * 128 + 128],
                                     bigB[:, f12 * 1024 + dh * 512:
                                          f12 * 1024 + (dh + 1) * 512])
                                    for f12 in range(12)], slot=sl_)
                            vwait(sMM)
                            if variant == "A":
                                dst = acc[:, tt * D + dh * 512:
                                          tt * D + (dh + 1) * 512]
                                sV.inc(V.tensor_add(dst, pk[:, 0:512], dst))
                            else:
                                dst = arenaF[:, tt4 * D + dh * 512:
                                             tt4 * D + (dh + 1) * 512]
                                if fh == 0:
                                    dT.wait(V)
                                    sV.inc(V.tensor_add(
                                        dst, pk[:, 0:512],
                                        acc[:, tt * D + dh * 512:
                                            tt * D + (dh + 1) * 512]))
                                else:
                                    sV.inc(V.tensor_add(dst, pk[:, 0:512],
                                                        dst))
                                    sV.inc(V.tensor_add(
                                        dst, dst,
                                        resid[:, tt * D + dh * 512:
                                              tt * D + (dh + 1) * 512]))
                                    if dh == 1:
                                        vflush()
                                        ywait(sV)
                                        dT.start(SY,
                                            out=yout[tt * 128:
                                                     (tt + 1) * 128, :],
                                            in_=arenaF[:, tt4 * D:
                                                       (tt4 + 1) * D])
                            sl_.consumed(sV)

        if KPH < 8:
            bailout()
            return
        emit_ffn("A")
        barrier()
        if int(os.environ.get("DBGF", "0")) == 13:
            dG.start(G, out=dbg[:, 0:8192], in_=acc[:, 0:8192])
            dG.wait(V)
            dG.wait(T)
        if KPH < 9:
            bailout()
            return
        emit_ffn("B")
        dT.wait(SY)
        dT.wait(G)
        dT.wait(V)
        dT.wait(SCL)
        dT.wait(T)

    ctx.close()
    return nc


def _prep_inputs(inputs):
    x = np.asarray(inputs["L0_attn_x"])
    f32 = np.float32
    bf16 = ml_dtypes.bfloat16

    def pp(v):
        return np.ascontiguousarray(np.asarray(v, dtype=f32).reshape(8, 128).T)

    def bcast(v):
        return np.ascontiguousarray(
            np.broadcast_to(np.asarray(v, dtype=f32)[None, :], (128, D)))

    wc = np.asarray(inputs["L0_attn_Wc"], dtype=f32)
    w2 = np.zeros((128, 128), f32)
    w2[0:64, 0:64] = wc
    w2[64:128, 64:128] = wc

    ident = np.eye(128, dtype=f32)
    shift_id = np.zeros((64, 128), f32)
    shift_id[np.arange(64), 64 + np.arange(64)] = 1.0

    tm = np.zeros((128, 4, 512), f32)
    for d_ in range(4):
        for kr in range(128):
            lim = d_ * 128 + kr
            tm[kr, d_, :lim] = KILL
    tm = np.ascontiguousarray(tm.reshape(128, 4 * 512))

    base = dict(
        Wq=np.ascontiguousarray(inputs["L0_attn_Wq"], f32),
        Wk=np.ascontiguousarray(inputs["L0_attn_Wk"], f32),
        Wv=np.ascontiguousarray(inputs["L0_attn_Wv"], f32),
        Wo=np.ascontiguousarray(inputs["L0_attn_Wo"], f32),
        w2blk=w2,
        bq_pp=pp(inputs["L0_attn_bq"]),
        bk_pp=pp(inputs["L0_attn_bk"]),
        bo_b=bcast(inputs["L0_attn_bo"]).astype(bf16),
        bv_b=bcast(inputs["L0_attn_bv"]).astype(bf16),
        wgate0=np.ascontiguousarray(inputs["L0_ffn_w_gate"], f32),
        wup0=np.ascontiguousarray(inputs["L0_ffn_w_up"], f32),
        wdown0=np.ascontiguousarray(inputs["L0_ffn_w_down"], f32),
        wrms_mix_pp=pp(inputs["L1_attn_w_rms_mix"]),
        wrms_ffn_pp=pp(inputs["L1_attn_w_rms_ffn"]),
        Wg1=np.ascontiguousarray(inputs["L1_attn_Wg"], f32),
        Wv1=np.ascontiguousarray(inputs["L1_attn_Wv"], f32),
        Wd1=np.ascontiguousarray(inputs["L1_attn_Wd"], f32),
        bg_pp=pp(inputs["L1_attn_bg"]),
        bv1_pp=pp(inputs["L1_attn_bv"]),
        bd_pp=pp(inputs["L1_attn_bd"]),
        Wgate1=np.ascontiguousarray(inputs["L1_attn_W_gate"], f32),
        Wup1=np.ascontiguousarray(inputs["L1_attn_W_up"], f32),
        Wout1=np.ascontiguousarray(inputs["L1_attn_W_out"], f32),
        wgi=np.ascontiguousarray(inputs["L1_ffn_w_gate"], np.int32),
        wui=np.ascontiguousarray(inputs["L1_ffn_w_up"], np.int32),
        wdi=np.ascontiguousarray(inputs["L1_ffn_w_down"], np.int32),
        ident_bf=ident.astype(bf16),
        ident_f32=ident,
        shift_id=shift_id.astype(bf16),
        onesf=np.ones((128, 128), f32),
        trimask=tm.astype(bf16),
    )

    in_maps = []
    for c in range(8):
        b, h = c // 2, c % 2
        if h == 0:
            xb = np.concatenate([np.zeros((T_OWN, D), f32),
                                 np.asarray(x[b, 0:T_OWN], f32)], axis=0)
        else:
            xb = np.ascontiguousarray(np.asarray(x[b], f32))
        bias = np.full((128, NT_BUF), EXP_SHIFT, f32)
        if h == 0:
            bias[:, 0:16] = KILL + EXP_SHIFT
        sel = np.zeros((128, 64), f32)
        if h == 1:
            partner = c - 1
            sel[:, partner * 8:(partner + 1) * 8] = 1.0
        m = dict(base)
        m.update(xin=xb, bias_tbl=bias, sel=sel)
        in_maps.append(m)
    return in_maps


def kernel(**inputs):
    if "nc" not in _CACHE:
        _CACHE["nc"] = build()
    nc = _CACHE["nc"]
    in_maps = _prep_inputs(inputs)
    res = run_bass_kernel_spmd(nc, in_maps, core_ids=list(range(8)))
    out = np.empty((4, S_BUF, D), np.float32)
    for c in range(8):
        b, h = c // 2, c % 2
        out[b, h * T_OWN:(h + 1) * T_OWN, :] = res.results[c]["yout"]
    return out


# version 06 (final)
# speedup vs baseline: 1.0925x; 1.0347x over previous
import os
import numpy as np
import ml_dtypes
import concourse.bass as bass
import concourse.mybir as mybir
from concourse.bass_utils import run_bass_kernel_spmd
from contextlib import ExitStack

F32 = mybir.dt.float32
BF16 = mybir.dt.bfloat16
I32 = mybir.dt.int32
ALU = mybir.AluOpType
AF = mybir.ActivationFunctionType
AX = mybir.AxisListType

D = 1024
S_BUF = 4096
T_OWN = 2048
NT_BUF = 32
NT_OWN = 16
I0 = 2816
F1 = 3072
EXP_SHIFT = -20.0
KILL = -10000.0

_CACHE = {}
KPH = int(os.environ.get("KPH", "99"))
KP2 = int(os.environ.get("KP2", "99"))
KP3 = int(os.environ.get("KP3", "99"))
KP4 = int(os.environ.get("KP4", "99"))
KP6 = int(os.environ.get("KP6", "99"))
KP5 = int(os.environ.get("KP5", "99"))


class Sem:
    __slots__ = ("h", "n")

    def __init__(self, handle):
        self.h = handle
        self.n = 0

    def inc(self, instr, k=1):
        instr.then_inc(self.h, k)
        self.n += k
        return self.n


class VDma:
    __slots__ = ("sems", "toks", "marks")

    def __init__(self, sems):
        self.sems = sems
        self.toks = []
        self.marks = {}

    def start(self, eng, out, in_):
        s = self.sems[len(self.toks) % len(self.sems)]
        eng.wait_ge(s.h, s.n)
        s.inc(eng.dma_start(out=out, in_=in_), 16)
        self.toks.append((s, s.n))
        return len(self.toks)

    def pos(self):
        return len(self.toks)

    def wait(self, eng, upto=None):
        v = len(self.toks) if upto is None else upto
        key = id(eng)
        lo = max(self.marks.get(key, 0), v - len(self.sems))
        for (s, n) in self.toks[lo:v]:
            eng.wait_ge(s.h, n)
        if v > self.marks.get(key, 0):
            self.marks[key] = v


class Slot:
    __slots__ = ("cons", "teng")
    sv_ref = None
    last_v = [0]

    def __init__(self, teng=None):
        self.cons = []
        self.teng = teng

    def acquire(self, eng):
        for sm, v in self.cons:
            eng.wait_ge(sm.h, v)
        self.cons = []

    def consumed(self, sm):
        self.cons.append((sm, sm.n))
        if sm is Slot.sv_ref:
            Slot.last_v[0] = sm.n
        if self.teng is not None:
            self.teng.wait_ge(sm.h, sm.n)


def build():
    nc = bass.Bass("TRN2", target_bir_lowering=False, debug=False,
                   detect_race_conditions=False)

    def inp(name, shape, dt=F32):
        return nc.dram_tensor(name, shape, dt, kind="ExternalInput").ap()

    xin = inp("xin", [S_BUF, D])
    Wq = inp("Wq", [D, D]); Wk = inp("Wk", [D, D]); Wv = inp("Wv", [D, D])
    Wo = inp("Wo", [D, D])
    w2blk = inp("w2blk", [128, 128])
    bq_pp = inp("bq_pp", [128, 8]); bk_pp = inp("bk_pp", [128, 8])
    bo_b = inp("bo_b", [128, D], BF16)
    bv_b = inp("bv_b", [128, D], BF16)
    wgate0 = inp("wgate0", [I0, D]); wup0 = inp("wup0", [I0, D])
    wdown0 = inp("wdown0", [D, I0])
    wrms_mix_pp = inp("wrms_mix_pp", [128, 8])
    wrms_ffn_pp = inp("wrms_ffn_pp", [128, 8])
    Wg1 = inp("Wg1", [D, D]); Wv1 = inp("Wv1", [D, D]); Wd1 = inp("Wd1", [D, D])
    bg_pp = inp("bg_pp", [128, 8]); bv1_pp = inp("bv1_pp", [128, 8])
    bd_pp = inp("bd_pp", [128, 8])
    Wgate1 = inp("Wgate1", [D, F1]); Wup1 = inp("Wup1", [D, F1])
    Wout1 = inp("Wout1", [F1, D])
    wgi = inp("wgi", [F1, D], I32); wui = inp("wui", [F1, D], I32)
    wdi = inp("wdi", [D, F1], I32)
    ident_bf_i = inp("ident_bf", [128, 128], BF16)
    ident_f32_i = inp("ident_f32", [128, 128])
    shift_id_i = inp("shift_id", [64, 128], BF16)
    onesf_i = inp("onesf", [128, 128])
    trimask_i = inp("trimask", [128, 4 * 512], BF16)
    bias_tbl_i = inp("bias_tbl", [128, NT_BUF])
    sel_i = inp("sel", [128, 64])

    yout = nc.dram_tensor("yout", [T_OWN, D], F32, kind="ExternalOutput").ap()
    dbg = nc.dram_tensor("dbg", [128, 8192], F32, kind="ExternalOutput").ap()

    wgq_t = nc.dram_tensor("wgq_t", [8, 128, I0], BF16).ap()
    wuq_t = nc.dram_tensor("wuq_t", [8, 128, I0], BF16).ap()
    wdq_t = nc.dram_tensor("wdq_t", [22, 128, D], BF16).ap()
    wgiT = nc.dram_tensor("wgiT", [8, 128, F1], BF16).ap()
    wuiT = nc.dram_tensor("wuiT", [8, 128, F1], BF16).ap()
    wdiT = nc.dram_tensor("wdiT", [24, 128, D], BF16).ap()
    xn1_sp = nc.dram_tensor("xn1_sp", [8, 128, T_OWN], BF16).ap()
    xnm_sp = nc.dram_tensor("xnm_sp", [8, 128, T_OWN], BF16).ap()
    xq_sp = nc.dram_tensor("xq_sp", [NT_OWN, 128, D], BF16).ap()
    cin_b = nc.dram_tensor("cin_b", [128, 8], F32).ap()
    cout_b = nc.dram_tensor("cout_b", [8 * 128, 8], F32).ap()

    ctx = ExitStack()
    ctx.__enter__()

    def sbt(name, shape, dt):
        return ctx.enter_context(nc.sbuf_tensor(name, shape, dt))

    def pst(name, shape):
        return ctx.enter_context(nc.psum_tensor(name, shape, F32))

    def mksem(name):
        return Sem(ctx.enter_context(nc.semaphore(name)))

    resid = sbt("resid", [128, NT_OWN * D], BF16)
    acc = sbt("acc", [128, NT_OWN * D], BF16)
    arena = sbt("arena", [128, 24704], BF16)
    kg = arena[:, 0:8192]
    vg = arena[:, 8192:16512]
    qcg = arena[:, 16512:20608]
    ylat = arena[:, 20608:24704]
    bigA = arena[:, 0:12288]
    bigB = arena[:, 12288:24576]
    acum = arena[:, 0:16384]
    wbuf = sbt("wbuf", [128, 2 * 8 * 512], BF16)
    xnt4 = sbt("xnt4", [128, 8 * 512], BF16)
    arenaF = sbt("arenaF", [128, 4096], F32)
    xtile = arenaF[:, 0:2048]
    wtile = arenaF[:, 0:1024]
    a_c = arenaF[:, 2048:4096]
    h_c = arenaF[:, 0:2048]
    fs1 = sbt("fs1", [128, D], F32)
    bstage = sbt("bstage", [128, 4 * D], BF16)
    pTb = sbt("pTb", [128, 2 * 1024], BF16)
    stats = sbt("stats", [128, 2 * NT_BUF], F32)
    smc = sbt("smc", [128, 128], F32)
    rden = sbt("rden", [128, 1024], F32)
    consts = sbt("consts", [128, 4 * 512], BF16)
    cbias = sbt("cbias", [128, NT_BUF], F32)
    csel = sbt("csel", [128, 64], F32)
    cid_bf = sbt("cid_bf", [128, 128], BF16)
    cid_f = sbt("cid_f", [128, 128], F32)
    cshift = sbt("cshift", [64, 128], BF16)
    conesf = sbt("conesf", [128, 128], F32)
    cw2 = sbt("cw2", [128, 128], BF16)
    cbq = sbt("cbq", [128, 8], F32)
    cbk = sbt("cbk", [128, 8], F32)
    cbo = sbt("cbo", [128, D], BF16)
    cbv = sbt("cbv", [128, D], BF16)
    cwrm = sbt("cwrm", [128, 8], F32)
    cwrf = sbt("cwrf", [128, 8], F32)
    cbg = sbt("cbg", [128, 8], F32)
    cbv1 = sbt("cbv1", [128, 8], F32)
    cbd = sbt("cbd", [128, 8], F32)
    carry = sbt("carry", [128, 8], F32)
    crecv = sbt("crecv", [128, 64], F32)
    cin_sb = sbt("cin_sb", [128, 8], F32)
    xs_c = sbt("xs_c", [128, T_OWN], BF16)
    qs_u = sbt("qs_u", [128, 1024], F32)
    qs_a = sbt("qs_a", [128, 1024], F32)
    qs_b = sbt("qs_b", [128, 1024], F32)
    qs_s = sbt("qs_s", [128, 1024], F32)
    qs_S = sbt("qs_S", [128, 3 * 64], F32)
    junk = fs1

    psS = pst("psS", [128, 1024])
    psS2 = pst("psS2", [128, 1024])
    psY0 = pst("psY0", [128, 512])
    psY1 = pst("psY1", [128, 512])
    psA1 = ctx.enter_context(nc.psum_tensor("psA1", [128, 1024], BF16))
    psA2 = ctx.enter_context(nc.psum_tensor("psA2", [128, 1024], BF16))

    sLD = mksem("sLD"); sGD = mksem("sGD")
    sST = mksem("sST"); sMM = mksem("sMM"); sV = mksem("sV")
    sSA = mksem("sSA"); sG = mksem("sG"); sCC = mksem("sCC")
    Slot.sv_ref = sV
    Slot.last_v[0] = 0
    PSY = [mksem(f"pS{i}") for i in range(16)]
    PG = [mksem(f"pG{i}") for i in range(6)]
    PT = [mksem(f"pT{i}") for i in range(4)]
    ALL_SEMS = tuple([sLD, sGD, sST, sMM, sV, sSA, sG, sCC] + PSY + PG + PT)

    block = ctx.enter_context(nc.Block())
    V, SCL, T, SY, G = nc.vector, nc.scalar, nc.tensor, nc.sync, nc.gpsimd
    dS = VDma(PSY)
    dG = VDma(PG)
    dT = VDma(PT)


    slS = Slot(nc.tensor); slS2 = Slot(nc.tensor)
    slY0 = Slot(nc.tensor); slY1 = Slot(nc.tensor)

    @block.sync
    def _(_sync):
        def barrier():
            for e, s in ((V, sV), (SCL, sSA), (T, sMM), (SY, sLD), (G, sG)):
                s.inc(e.nop())
            for e in (V, SCL, T, SY, G):
                for s2 in ALL_SEMS:
                    e.wait_ge(s2.h, s2.n)

        def vflush():
            sV.inc(V.tensor_copy(smc[:, 126:127], smc[:, 125:126]))

        def sflush():
            sSA.inc(SCL.copy(smc[:, 124:125], smc[:, 123:124]))

        def vwait(s, n=None):
            V.wait_ge(s.h, s.n if n is None else n)

        def swait(s, n=None):
            SCL.wait_ge(s.h, s.n if n is None else n)

        def twait(s, n=None):
            T.wait_ge(s.h, s.n if n is None else n)

        def ywait(s, n=None):
            SY.wait_ge(s.h, s.n if n is None else n)

        def gwait(s, n=None):
            G.wait_ge(s.h, s.n if n is None else n)

        def mm_acc(pt, pairs, tp=None, slot=None, first=True, last=True):
            if first:
                tok = Slot.last_v[0]
                if trqA and trqA[-1] > tok:
                    tok = trqA[-1]
                twait(sV, tok)
                twait(sSA)
            if slot is not None and first:
                slot.acquire(T)
            n = len(pairs)
            ins = None
            for i, (lh, rh) in enumerate(pairs):
                ins = T.matmul(pt, lh, rh, start=(first and i == 0),
                               stop=(last and i == n - 1), tile_position=tp)
            sMM.inc(ins)
            return sMM.n

        trqA = []

        def transpose128(src_ap, dst_ap, ncols=128, coarse=True):
            k = len(trqA)
            pool = psA1 if k % 2 == 0 else psA2
            pos = (k // 2) % 4
            pt = pool[:, pos * 128:pos * 128 + 128]
            if coarse:
                twait(sV)
                twait(sSA)
            elif trqA:
                twait(sV, trqA[-1])
            sMM.inc(T.transpose(pt, src_ap, cid_bf[:]))
            vwait(sMM)
            sV.inc(V.tensor_copy(dst_ap, pt))
            trqA.append(sV.n)

        def quant_chain(src_ap, dst_ap, n):
            nb = n // 16
            blk = src_ap.rearrange("p (b k) -> p b k", k=16)
            S_ = qs_S[:, 0:nb]
            R_ = qs_S[:, 64:64 + nb]
            C_ = qs_S[:, 128:128 + nb]
            sV.inc(V.tensor_reduce(S_, blk, AX.X, ALU.max,
                                   apply_absolute_value=True))
            vwait(sV)
            sV.inc(V.reciprocal(R_, S_))
            vwait(sV)
            V.tensor_scalar(R_, R_, 6.0, 1e12, ALU.mult, ALU.min)
            V.tensor_scalar(C_, S_, 1.0 / 6.0, 1e-12, ALU.mult, ALU.max)
            V.scalar_tensor_tensor(qs_u[:, 0:n], src_ap, -1.0, src_ap,
                                   ALU.mult, ALU.max)
            V.tensor_tensor(qs_u[:, 0:n].rearrange("p (b k) -> p b k", k=16),
                            qs_u[:, 0:n].rearrange("p (b k) -> p b k", k=16),
                            R_.unsqueeze(2).broadcast_to([128, nb, 16]),
                            ALU.mult)
            swait(sV)
            sSA.inc(SCL.activation(qs_s[:, 0:n], src_ap, AF.Sign))
            V.tensor_single_scalar(qs_a[:, 0:n], qs_u[:, 0:n], 0.25,
                                   ALU.is_gt)
            for c_ in (0.75, 1.25, 1.75):
                V.scalar_tensor_tensor(qs_a[:, 0:n], qs_u[:, 0:n], c_,
                                       qs_a[:, 0:n], ALU.is_gt, ALU.add)
            V.tensor_single_scalar(qs_b[:, 0:n], qs_u[:, 0:n], 2.5, ALU.is_gt)
            for c_ in (3.5, 5.0, 5.0):
                V.scalar_tensor_tensor(qs_b[:, 0:n], qs_u[:, 0:n], c_,
                                       qs_b[:, 0:n], ALU.is_gt, ALU.add)
            V.scalar_tensor_tensor(qs_b[:, 0:n], qs_a[:, 0:n], 0.5,
                                   qs_b[:, 0:n], ALU.mult, ALU.add)
            vwait(sSA)
            V.tensor_tensor(qs_s[:, 0:n].rearrange("p (b k) -> p b k", k=16),
                            qs_s[:, 0:n].rearrange("p (b k) -> p b k", k=16),
                            C_.unsqueeze(2).broadcast_to([128, nb, 16]),
                            ALU.mult)
            sV.inc(V.tensor_tensor(dst_ap, qs_b[:, 0:n], qs_s[:, 0:n],
                                   ALU.mult))

        def task_fp4(dst_hbm, src_hbm, r, width):
            for off in range(0, width, 1024):
                nn = min(1024, width - off)
                ywait(sV)
                ywait(sSA)
                dS.start(SY,
                         out=wtile[:, 0:nn],
                         in_=src_hbm[r * 128:(r + 1) * 128, off:off + nn])
                dS.wait(V)
                dS.wait(SCL)
                vwait(sMM)
                quant_chain(wtile[:, 0:nn], wbuf[:, off:off + nn], nn)
            task_state["free_v"] = sV.n
            dS.wait(V, task_state["free_l"])
            nblk = width // 128
            for b_ in range(nblk):
                transpose128(wbuf[:, b_ * 128:(b_ + 1) * 128],
                             wbuf[:, 2816 + b_ * 128:2816 + (b_ + 1) * 128],
                             coarse=(b_ == 0))
            vflush()
            ywait(sV)
            dS.start(SY,
                     out=dst_hbm[:, :, r * 128:(r + 1) * 128]
                     .rearrange("c p n -> p c n"),
                     in_=wbuf[:, 2816:2816 + nblk * 128]
                     .rearrange("p (c n) -> p c n", n=128))
            task_state["free_l"] = dS.pos()

        def task_int(dst_hbm, src_hbm, r, width):
            gwait(sV)
            dS.wait(G)
            dS.wait(G, task_state["free_l"])
            gwait(sMM)
            dG.start(G, out=wbuf[:, 0:width],
                                in_=src_hbm[r * 128:(r + 1) * 128, :])
            tv = dG.pos()
            nblk = width // 128
            dS.wait(V, task_state["free_l"])
            for b_ in range(nblk):
                dG.wait(T, tv)
                transpose128(wbuf[:, b_ * 128:(b_ + 1) * 128],
                             wbuf[:, 3072 + b_ * 128:3072 + (b_ + 1) * 128],
                             coarse=(b_ == 0))
            vflush()
            ywait(sV)
            dS.start(SY,
                     out=dst_hbm[:, :, r * 128:(r + 1) * 128]
                     .rearrange("c p n -> p c n"),
                     in_=wbuf[:, 3072:3072 + nblk * 128]
                     .rearrange("p (c n) -> p c n", n=128))
            task_state["free_l"] = dS.pos()

        task_state = {"free_v": 0, "free_l": 0}
        tasks = []
        for r in range(22):
            tasks.append(("f", wgq_t, wgate0, r, 1024))
        for r in range(22):
            tasks.append(("f", wuq_t, wup0, r, 1024))
        for r in range(8):
            tasks.append(("f", wdq_t, wdown0, r, 2816))
        for r in range(24):
            tasks.append(("i", wgiT, wgi, r, 1024))
        for r in range(24):
            tasks.append(("i", wuiT, wui, r, 1024))
        for r in range(8):
            tasks.append(("i", wdiT, wdi, r, 3072))
        tasks.reverse()

        def pump(nt):
            for _ in range(nt):
                if not tasks:
                    return
                kind, dh_, sh_, r_, w_ = tasks.pop()
                (task_fp4 if kind == "f" else task_int)(dh_, sh_, r_, w_)


        for dst, src in ((consts, trimask_i), (cbias, bias_tbl_i),
                         (csel, sel_i), (cid_bf, ident_bf_i),
                         (cid_f, ident_f32_i), (cshift, shift_id_i),
                         (conesf, onesf_i), (cbq, bq_pp), (cbk, bk_pp),
                         (cbo, bo_b), (cbv, bv_b), (cwrm, wrms_mix_pp),
                         (cwrf, wrms_ffn_pp), (cbg, bg_pp),
                         (cbv1, bv1_pp), (cbd, bd_pp)):
            dS.start(SY, out=dst[:], in_=src)
        dG.start(G, out=cw2[:], in_=w2blk)
        dS.wait(V)
        dG.wait(V)
        sV.inc(V.memset(vg[:].rearrange("p (j h c) -> p j h c", h=8, c=65)
                        [:, :, :, 64:65], 1.0))
        sV.inc(V.memset(smc[:, 120:128], 0.0))
        if int(os.environ.get("DBGF", "0")) == 8:
            sV.inc(V.memset(smc[:, 0:120], 0.0))
            sV.inc(V.memset(stats[:, 0:64], 0.0))
            sV.inc(V.memset(qs_u[:, 0:48], 0.0))

        def bailout():
            dS.wait(V)
            for tt in range(NT_OWN):
                dT.wait(V)
                sV.inc(V.tensor_copy(xtile[:, 0:D],
                                     acc[:, tt * D:(tt + 1) * D]))
                vflush()
                ywait(sV)
                dT.start(SY, out=yout[tt * 128:(tt + 1) * 128, :],
                         in_=xtile[:, 0:D])
            dT.wait(SY)

        pa_done = {}
        for bt in range(NT_BUF):
            sl = bt % 2
            buf = xtile[:, sl * D:sl * D + D]
            if sl in pa_done:
                ywait(sV, pa_done[sl][0])
                ywait(sSA, pa_done[sl][1])
            dS.start(SY, out=buf,
                                 in_=xin[bt * 128:(bt + 1) * 128, :])
            dS.wait(V)
            dS.wait(SCL)
            sV.inc(V.reduce_sum(smc[:, bt:bt + 1], buf, AX.X))
            swait(sSA)
            sSA.inc(SCL.activation(junk[:], buf, AF.Square,
                                   accum_out=smc[:, 32 + bt:32 + bt + 1]))
            pa_done[sl] = (sV.n, sSA.n)
        vwait(sV)
        vwait(sSA)
        sV.inc(V.tensor_scalar(stats[:, 0:32], smc[:, 0:32], 1.0 / D, None,
                               ALU.mult))
        vwait(sV)
        sV.inc(V.tensor_scalar(smc[:, 64:96], smc[:, 32:64], 1.0 / D, 1e-5,
                               ALU.mult, ALU.add))
        vwait(sV)
        sV.inc(V.tensor_tensor(smc[:, 0:32], stats[:, 0:32], stats[:, 0:32],
                               ALU.mult))
        vwait(sV)
        sV.inc(V.tensor_sub(smc[:, 32:64], smc[:, 64:96], smc[:, 0:32]))
        vwait(sV)
        sV.inc(V.reciprocal(smc[:, 64:96], smc[:, 32:64]))
        vwait(sV)
        vflush()
        vwait(sV)
        swait(sV)
        sSA.inc(SCL.activation(stats[:, 32:64], smc[:, 64:96], AF.Sqrt))
        sflush()
        vwait(sSA)
        swait(sSA)
        pc_done = {}
        xq_st = {}
        _otorder = list(range(NT_OWN))
        if os.environ.get("KREV"):
            _otorder = _otorder[::-1]
        for ot in _otorder:
            bt = 16 + ot
            sl = ot % 2
            buf = xtile[:, sl * D:sl * D + D]
            if sl in pc_done:
                ywait(sV, pc_done[sl][0])
                ywait(sSA, pc_done[sl][1])
            else:
                ywait(sV)
                ywait(sSA)
            dS.start(SY, out=buf,
                                 in_=xin[bt * 128:(bt + 1) * 128, :])
            dS.wait(V)
            dS.wait(SCL)
            sV.inc(V.tensor_copy(resid[:, ot * D:(ot + 1) * D], buf))
            V.tensor_scalar(fs1[:], buf, stats[:, bt:bt + 1],
                            stats[:, 32 + bt:32 + bt + 1],
                            ALU.subtract, ALU.mult)
            if KP6 >= 1:
                dT.wait(V, xq_st.get(sl, 0))
                quant_chain(fs1[:], bstage[:, sl * D:(sl + 1) * D], D)
                vflush()
                ywait(sV)
                dT.start(SY, out=xq_sp[ot, :, :],
                                     in_=bstage[:, sl * D:(sl + 1) * D])
                xq_st[sl] = dT.pos()
            pc_done[sl] = (sV.n, sSA.n)
        barrier()

        if KPH < 2:
            bailout()
            return
        for g in range(4 if KPH >= 4 else 1):
            dG.start(G, 
                out=wbuf[:, 0:2048].rearrange("p (c n) -> p c n", n=256),
                in_=Wk[:, g * 256:(g + 1) * 256]
                    .rearrange("(c p) n -> p c n", p=128))
            dG.start(G, 
                out=wbuf[:, 2048:4096].rearrange("p (c n) -> p c n", n=256),
                in_=Wv[:, g * 256:(g + 1) * 256]
                    .rearrange("(c p) n -> p c n", p=128))
            dG.start(G, 
                out=wbuf[:, 4096:6144].rearrange("p (c n) -> p c n", n=256),
                in_=Wq[:, g * 256:(g + 1) * 256]
                    .rearrange("(c p) n -> p c n", p=128))
            wk_ap = wbuf[:, 0:2048].rearrange("p (c n) -> p c n", n=256)
            wv_ap = wbuf[:, 2048:4096].rearrange("p (c n) -> p c n", n=256)
            wq_ap = wbuf[:, 4096:6144].rearrange("p (c n) -> p c n", n=256)
            bst = {}
            for t4 in range(8):
                for q in range(4):
                    bt = t4 * 4 + q
                    sl = q % 2
                    buf = xtile[:, sl * D:sl * D + D]
                    if sl in bst:
                        ywait(sV, bst[sl])
                    else:
                        ywait(sV)
                        ywait(sSA)
                    dS.start(SY, 
                        out=buf, in_=xin[bt * 128:(bt + 1) * 128, :])
                    dS.wait(V)
                    vwait(sMM)
                    sV.inc(V.tensor_scalar(
                        bstage[:, sl * D:sl * D + D], buf,
                        stats[:, bt:bt + 1], stats[:, 32 + bt:32 + bt + 1],
                        ALU.subtract, ALU.mult))
                    bst[sl] = sV.n
                    for dc in range(8):
                        transpose128(
                            bstage[:, sl * D + dc * 128:sl * D + dc * 128
                                   + 128],
                            xnt4[:, dc * 512 + q * 128:dc * 512 + q * 128
                                 + 128], coarse=(dc == 0))
                xnt = xnt4[:].rearrange("p (c n) -> p c n", n=512)
                dG.wait(T)
                for p in range(min(2 if KP2 >= 2 else 0,
                                   KP3 - t4 if KP3 < 99 else 2)):
                    pk, sl_ = (psY0, slY0) if p % 2 == 0 else (psY1, slY1)
                    mm_acc(pk[:, 0:512],
                           [(wk_ap[:, dc, p * 128:(p + 1) * 128],
                             xnt[:, dc, :]) for dc in range(8)], slot=sl_)
                    vwait(sMM)
                    if KP4 >= 1:
                        if KP4 == 2:
                            sV.inc(V.tensor_copy(
                                kg[:, p * S_BUF + t4 * 512:p * S_BUF
                                   + (t4 + 1) * 512], pk[:, 0:512]))
                        elif KP4 == 3:
                            sV.inc(V.tensor_copy(kg[:, 0:512],
                                                 pk[:, 0:512]))
                        else:
                            sV.inc(V.tensor_scalar(
                                kg[:, p * S_BUF + t4 * 512:p * S_BUF
                                   + (t4 + 1) * 512],
                                pk[:, 0:512],
                                cbk[:, g * 2 + p:g * 2 + p + 1],
                                None, ALU.add))
                        sl_.consumed(sV)
                        if KP4 == 4:
                            twait(sV)
                for q in range(4 if KP2 >= 3 else 0):
                    j = t4 * 4 + q
                    pk, sl_ = (psY0, slY0) if q % 2 == 0 else (psY1, slY1)
                    mm_acc(pk[:, 0:256],
                           [(xnt[:, dc, q * 128:(q + 1) * 128],
                             wv_ap[:, dc, :]) for dc in range(8)], slot=sl_)
                    vwait(sMM)
                    dst = vg.rearrange("p (j h c) -> p j h c", h=4, c=65)[
                        :, j, :, 0:64]
                    sV.inc(V.tensor_add(
                        dst,
                        pk[:, 0:256].rearrange("p (h c) -> p h c", c=64),
                        cbv[:, g * 256:(g + 1) * 256]
                            .rearrange("p (h c) -> p h c", c=64)))
                    sl_.consumed(sV)
                if t4 >= 4 and KP2 >= 4:
                    for p in range(2):
                        mm_acc(psY0[:, 0:512],
                               [(wq_ap[:, dc, p * 128:(p + 1) * 128],
                                 xnt[:, dc, :]) for dc in range(8)],
                               slot=slY0)
                        vwait(sMM)
                        sV.inc(V.tensor_scalar(
                            bstage[:, 3072:3072 + 512], psY0[:, 0:512],
                            cbq[:, g * 2 + p:g * 2 + p + 1], None, ALU.add))
                        slY0.consumed(sV)
                        twait(sV)
                        mm_acc(psY1[:, 0:512],
                               [(cw2[:], bstage[:, 3072:3072 + 512])],
                               slot=slY1)
                        vwait(sMM)
                        sV.inc(V.tensor_copy(
                            qcg[:, p * T_OWN + (t4 - 4) * 512:
                                p * T_OWN + (t4 - 3) * 512],
                            psY1[:, 0:512]))
                        slY1.consumed(sV)
            barrier()
            if g == int(os.environ.get("DBGP", "-1")):
                _db_off = {"ylat": 20608, "kg": 0, "kg2": 4096,
                           "qcg": 16512}[os.environ.get("DBGW", "ylat")]
                for half in range(4):
                    dS.wait(V)
                    sV.inc(V.tensor_copy(
                        fs1[:, 0:1024],
                        arena[:, _db_off + half * 1024:_db_off
                              + (half + 1) * 1024]))
                    ywait(sV)
                    dS.start(SY, 
                        out=dbg[:, half * 1024:(half + 1) * 1024],
                        in_=fs1[:, 0:1024])
                dS.wait(SY)
            if KPH < 3:
                bailout()
                return
            vhat = vg.rearrange("p (j h c) -> p j h c", h=4, c=65)
            for c in range(4):
                for p in range(2):
                    jmax = 19 + 4 * c
                    expv = {}
                    avv = {}
                    slY0.acquire(T)
                    slY1.acquire(T)
                    for j in range(jmax + 1):
                        pS, slp = (psS, slS) if j % 2 == 0 else (psS2, slS2)
                        twait(sV)
                        twait(sSA)
                        slp.acquire(T)
                        T.matmul(pS[:, 0:512],
                                 kg[0:64, p * S_BUF + j * 128:
                                    p * S_BUF + (j + 1) * 128],
                                 qcg[0:64, p * T_OWN + c * 512:
                                     p * T_OWN + (c + 1) * 512],
                                 start=True, stop=True, tile_position=(0, 0))
                        sMM.inc(T.matmul(
                            pS[:, 512:1024],
                            kg[64:128, p * S_BUF + j * 128:
                               p * S_BUF + (j + 1) * 128],
                            qcg[64:128, p * T_OWN + c * 512:
                                p * T_OWN + (c + 1) * 512],
                            start=True, stop=True, tile_position=(64, 0)))
                        qkv = sMM.n
                        d_ = j - 16 - 4 * c
                        if 0 <= d_ <= 3:
                            vwait(sMM, qkv)
                            sV.inc(V.tensor_tensor(
                                pS[:].rearrange("p (h n) -> p h n", n=512),
                                pS[:].rearrange("p (h n) -> p h n", n=512),
                                consts[:, d_ * 512:(d_ + 1) * 512]
                                .unsqueeze(1).broadcast_to([128, 2, 512]),
                                ALU.add))
                            swait(sV)
                        else:
                            swait(sMM, qkv)
                        slot = j % 2
                        if j >= 2:
                            swait(sMM, avv[j - 2])
                        else:
                            swait(sMM)
                        sSA.inc(SCL.activation(
                            pTb[:, slot * 1024:(slot + 1) * 1024], pS[:],
                            AF.Exp, bias=cbias[:, j:j + 1], scale=0.125))
                        expv[j] = sSA.n
                        slp.consumed(sSA)
                        twait(sSA)
                        twait(sV)
                        T.matmul(psY0[0:65, 0:512],
                                 vhat[:, j, 2 * p, :].opt(),
                                 pTb[:, slot * 1024:slot * 1024 + 512],
                                 start=(j == 0), stop=(j == jmax))
                        sMM.inc(T.matmul(
                            psY1[0:65, 0:512],
                            vhat[:, j, 2 * p + 1, :].opt(),
                            pTb[:, slot * 1024 + 512:(slot + 1) * 1024],
                            start=(j == 0), stop=(j == jmax)))
                        avv[j] = sMM.n
                    vwait(sMM)
                    sV.inc(V.tensor_copy(rden[64:65, 0:512], psY0[64:65, :]))
                    sV.inc(V.tensor_copy(rden[64:65, 512:1024],
                                         psY1[64:65, :]))
                    vflush()
                    ywait(sV)
                    dS.start(SY, out=fs1[0:1, 0:1024],
                                         in_=rden[64:65, 0:1024])
                    dS.wait(V)
                    sV.inc(V.reciprocal(rden[0:1, 0:1024],
                                        fs1[0:1, 0:1024]))
                    vwait(sV)
                    twait(sV)
                    twait(sSA)
                    slS.acquire(T)
                    T.matmul(psS[:, 0:512], conesf[0:1, :],
                             rden[0:1, 0:512], start=True, stop=True)
                    sMM.inc(T.matmul(psS[:, 512:1024], conesf[0:1, :],
                                     rden[0:1, 512:1024], start=True,
                                     stop=True))
                    vwait(sMM)
                    sV.inc(V.tensor_copy(rden[:, 0:1024], psS[:]))
                    slS.consumed(sV)
                    yco = p * T_OWN + c * 512
                    sV.inc(V.tensor_mul(ylat[0:64, yco:yco + 512],
                                        psY0[0:64, :], rden[0:64, 0:512]))
                    slY0.consumed(sV)
                    vwait(sMM)
                    sV.inc(V.tensor_mul(bstage[0:64, 3072:3072 + 512],
                                        psY1[0:64, :], rden[0:64, 512:1024]))
                    slY1.consumed(sV)
                    twait(sV)
                    twait(sSA)
                    slS.acquire(T)
                    sMM.inc(T.matmul(psS[:, 0:512], cshift[:],
                                     bstage[0:64, 3072:3072 + 512],
                                     start=True, stop=True))
                    vwait(sMM)
                    sV.inc(V.tensor_copy(ylat[64:128, yco:yco + 512],
                                         psS[64:128, 0:512]))
                    slS.consumed(sV)
                    pump(4)
            barrier()
            if g == int(os.environ.get("DBGG", "-1")):
                _db_off = {"ylat": 20608, "kg": 0, "kg2": 4096,
                           "qcg": 16512}[os.environ.get("DBGW", "ylat")]
                for half in range(4):
                    dS.wait(V)
                    sV.inc(V.tensor_copy(
                        fs1[:, 0:1024],
                        arena[:, _db_off + half * 1024:_db_off + (half + 1) * 1024]))
                    ywait(sV)
                    dS.start(SY, 
                        out=dbg[:, half * 1024:(half + 1) * 1024],
                        in_=fs1[:, 0:1024])
                dS.wait(SY)
            gwait(sMM)
            dS.wait(G)
            dG.start(G, 
                out=wbuf[:, 0:2048].rearrange("p (c n) -> p c n", n=1024),
                in_=Wo[g * 256:(g + 1) * 256, :]
                    .rearrange("(c p) n -> p c n", p=128))
            dG.wait(T)
            for tt in range(NT_OWN):
                for dh in range(2):
                    pk, sl_ = (psY0, slY0) if (tt * 2 + dh) % 2 == 0 \
                        else (psY1, slY1)
                    mm_acc(pk[:, 0:512],
                           [(ylat[:, pq * T_OWN + tt * 128:
                                  pq * T_OWN + (tt + 1) * 128],
                             wbuf[:, pq * 1024 + dh * 512:
                                  pq * 1024 + (dh + 1) * 512])
                            for pq in range(2)], slot=sl_)
                    vwait(sMM)
                    dst = acc[:, tt * D + dh * 512:tt * D + (dh + 1) * 512]
                    if g == 0:
                        sV.inc(V.tensor_add(dst, pk[:, 0:512],
                                            cbo[:, dh * 512:(dh + 1) * 512]))
                    else:
                        sV.inc(V.tensor_add(dst, pk[:, 0:512], dst))
                    sl_.consumed(sV)
            barrier()

        if KPH < 5:
            bailout()
            return
        pump(len(tasks))
        barrier()

        if KPH < 6:
            bailout()
            return
        if int(os.environ.get("DBGF", "0")) == 7:
            for t_ in range(4):
                dG.start(G, out=dbg[:, t_ * 1024:(t_ + 1) * 1024],
                         in_=xq_sp[int(os.environ.get("DBGO", "12"))
                                   + t_, :, :])
            gwait(sV)
            dG.start(G, out=dbg[:, 4096:5120], in_=bstage[:, 0:1024])
            dG.start(G, out=dbg[:, 5120:6144], in_=bstage[:, 1024:2048])
            dG.wait(G)
        hq = bigA[:].rearrange("p (t n) -> p t n", n=3072)
        ioffs = [(0, 512), (512, 512), (1024, 512), (1536, 512),
                 (2048, 512), (2560, 256)]
        for sup in range(4):
            ywait(sV)
            ywait(sMM)
            for t_ in range(4):
                dS.start(SY, out=bstage[:, t_ * D:(t_ + 1) * D],
                         in_=xq_sp[sup * 4 + t_, :, :])
            dS.wait(T)
            dS.wait(V)
            for q in range(4):
                for dc in range(8):
                    transpose128(bstage[:, q * D + dc * 128:
                                        q * D + dc * 128 + 128],
                                 xnt4[:, dc * 512 + q * 128:
                                      dc * 512 + q * 128 + 128],
                                 coarse=(q == 0 and dc == 0))
            xnt = xnt4[:].rearrange("p (c n) -> p c n", n=512)
            if (sup == int(os.environ.get("DBGSUP", "0"))
                    and int(os.environ.get("DBGF", "0")) in (4, 6)):
                _dsrc = bstage if os.environ["DBGF"] == "6" else xnt4
                for half in range(4):
                    dS.wait(V)
                    vwait(sV)
                    sV.inc(V.tensor_copy(
                        xtile[:, 0:1024],
                        _dsrc[:, half * 1024:(half + 1) * 1024]))
                    ywait(sV)
                    dS.start(SY, out=dbg[:, half * 1024:(half + 1) * 1024],
                             in_=xtile[:, 0:1024])
                dS.wait(SY)
                dS.wait(V)
                dS.wait(SCL)
                dS.wait(T)
                dS.wait(G)
            wld = {}
            for ic, (io, il) in enumerate(ioffs):
                sl = ic % 2
                ywait(sMM)
                dS.start(SY,
                         out=bigB[:, sl * 4096:sl * 4096 + 8 * il]
                         .rearrange("p (c n) -> p c n", n=il),
                         in_=wgq_t[:, :, io:io + il]
                         .rearrange("c p n -> p c n"))
                dS.start(SY,
                         out=wbuf[:, sl * 4096:sl * 4096 + 8 * il]
                         .rearrange("p (c n) -> p c n", n=il),
                         in_=wuq_t[:, :, io:io + il]
                         .rearrange("c p n -> p c n"))
                wg_ap = bigB[:, sl * 4096:sl * 4096 + 8 * il] \
                    .rearrange("p (c n) -> p c n", n=il)
                wu_ap = wbuf[:, sl * 4096:sl * 4096 + 8 * il] \
                    .rearrange("p (c n) -> p c n", n=il)
                dS.wait(T)
                for q in range(4):
                    slS.acquire(T)
                    mm_acc(psS[:, 0:il],
                           [(xnt[:, dc, q * 128:(q + 1) * 128],
                             wg_ap[:, dc, :]) for dc in range(8)])
                    mm_acc(psS[:, 512:512 + il],
                           [(xnt[:, dc, q * 128:(q + 1) * 128],
                             wu_ap[:, dc, :]) for dc in range(8)])
                    wld[sl] = sMM.n
                    swait(sMM)
                    swait(sV)
                    sSA.inc(SCL.activation(pTb[:, 0:il], psS[:, 0:il],
                                           AF.Silu))
                    slS.consumed(sSA)
                    vwait(sMM)
                    vwait(sSA)
                    _dbf = int(os.environ.get("DBGF", "0"))
                    if _dbf == 2:
                        sV.inc(V.tensor_copy(fs1[:, 0:il], psS[:, 0:il]))
                    elif _dbf == 3:
                        sV.inc(V.tensor_copy(fs1[:, 0:il],
                                             psS[:, 512:512 + il]))
                    else:
                        sV.inc(V.tensor_tensor(fs1[:, 0:il],
                                               psS[:, 512:512 + il],
                                               pTb[:, 0:il], ALU.mult))
                    slS.consumed(sV)
                    vwait(sV)
                    if _dbf in (2, 3):
                        sV.inc(V.tensor_copy(hq[:, q, io:io + il],
                                             fs1[:, 0:il]))
                    else:
                        quant_chain(fs1[:, 0:il], hq[:, q, io:io + il], il)
            if (sup == int(os.environ.get("DBGSUP", "0"))
                    and int(os.environ.get("DBGF", "0")) == 1):
                for half in range(3):
                    dS.wait(V)
                    vwait(sV)
                    sV.inc(V.tensor_copy(
                        xtile[:, 0:1024],
                        bigA[:, int(os.environ.get("DBGQ", "0")) * 3072
                             + half * 1024:
                             int(os.environ.get("DBGQ", "0")) * 3072
                             + (half + 1) * 1024]))
                    ywait(sV)
                    dS.start(SY, out=dbg[:, half * 1024:(half + 1) * 1024],
                             in_=xtile[:, 0:1024])
                dS.wait(SY)
                dS.wait(V)
                dS.wait(SCL)
                dS.wait(T)
                dS.wait(G)
            for dh in range(2):
                ywait(sMM)
                dS.start(SY, 
                    out=bigB[:, 0:22 * 512].rearrange("p (c n) -> p c n",
                                                      n=512),
                    in_=wdq_t[:, :, dh * 512:(dh + 1) * 512]
                        .rearrange("c p n -> p c n"))
                wd_ap = bigB[:, 0:22 * 512].rearrange("p (c n) -> p c n",
                                                      n=512)
                for q in range(4):
                    tt = sup * 4 + q
                    for icc in range(22):
                        transpose128(hq[:, q, icc * 128:(icc + 1) * 128],
                                     xnt4[:, icc * 128:(icc + 1) * 128],
                                     coarse=(icc == 0))
                    twait(sV)
                    dS.wait(T)
                    pk, sl_ = (psY0, slY0) if q % 2 == 1 else (psY1, slY1)
                    mm_acc(pk[:, 0:512],
                           [(xnt4[:, icc * 128:(icc + 1) * 128],
                             wd_ap[:, icc, :]) for icc in range(22)],
                           slot=sl_)
                    vwait(sMM)
                    dst = acc[:, tt * D + dh * 512:tt * D + (dh + 1) * 512]
                    sV.inc(V.tensor_add(dst, pk[:, 0:512], dst))
                    sl_.consumed(sV)
        barrier()

        if KPH < 7:
            bailout()
            return
        for tt in range(NT_OWN):
            r_ = resid[:, tt * D:(tt + 1) * D]
            sV.inc(V.tensor_add(r_, r_, acc[:, tt * D:(tt + 1) * D]))
            vflush()
            vwait(sV)
            swait(sV)
            sV.inc(V.reduce_sum(smc[:, tt:tt + 1], r_, AX.X))
            swait(sSA)
            sSA.inc(SCL.activation(junk[:], r_, AF.Square,
                                   accum_out=smc[:, 32 + tt:32 + tt + 1]))
        vwait(sV)
        sflush()
        vwait(sSA)
        if int(os.environ.get("DBGF", "0")) == 8:
            sV.inc(V.tensor_copy(qs_u[:, 0:48], smc[:, 0:48]))
            vwait(sV)
        sV.inc(V.tensor_scalar(stats[:, 0:16], smc[:, 0:16], 1.0 / D, None,
                               ALU.mult))
        vwait(sV)
        sV.inc(V.tensor_scalar(smc[:, 64:80], smc[:, 32:48], 1.0 / D, 1e-5,
                               ALU.mult, ALU.add))
        vwait(sV)
        sV.inc(V.tensor_tensor(smc[:, 0:16], stats[:, 0:16], stats[:, 0:16],
                               ALU.mult))
        vwait(sV)
        sV.inc(V.tensor_sub(smc[:, 32:48], smc[:, 64:80], smc[:, 0:16]))
        vwait(sV)
        sV.inc(V.reciprocal(smc[:, 64:80], smc[:, 32:48]))
        vwait(sV)
        vflush()
        vwait(sV)
        swait(sV)
        sSA.inc(SCL.activation(stats[:, 32:48], smc[:, 64:80], AF.Sqrt))
        sflush()
        if KP5 < 2:
            bailout()
            return
        vwait(sSA)
        swait(sSA)
        for tt in range(NT_OWN):
            sV.inc(V.tensor_scalar(acc[:, tt * D:(tt + 1) * D],
                                   resid[:, tt * D:(tt + 1) * D],
                                   stats[:, tt:tt + 1],
                                   stats[:, 32 + tt:32 + tt + 1],
                                   ALU.subtract, ALU.mult))
            vflush()
            swait(sV)
            swait(sSA)
            sSA.inc(SCL.activation(junk[:], acc[:, tt * D:(tt + 1) * D],
                                   AF.Square,
                                   accum_out=smc[:, 96 + tt:96 + tt + 1]))
        sflush()
        vwait(sSA)
        sV.inc(V.tensor_scalar(smc[:, 64:80], smc[:, 96:112], 1.0 / D, 1e-6,
                               ALU.mult, ALU.add))
        vwait(sV)
        sV.inc(V.reciprocal(smc[:, 80:96], smc[:, 64:80]))
        vwait(sV)
        vflush()
        vwait(sV)
        swait(sV)
        sSA.inc(SCL.activation(stats[:, 48:64], smc[:, 80:96], AF.Sqrt))
        sflush()
        vwait(sSA)
        for tt in range(NT_OWN):
            for dc in range(8):
                transpose128(acc[:, tt * D + dc * 128:tt * D + dc * 128
                                 + 128],
                             bstage[:, dc * 128:(dc + 1) * 128],
                             coarse=(dc == 0))
            vflush()
            ywait(sV)
            dS.start(SY,
                out=xn1_sp[:, :, tt * 128:(tt + 1) * 128]
                    .rearrange("c p n -> p c n"),
                in_=bstage[:, 0:1024].rearrange("p (c n) -> p c n", n=128))
            vwait(sMM)
            sV.inc(V.tensor_scalar(bstage[:, 1024:2048],
                                   acc[:, tt * D:(tt + 1) * D],
                                   stats[:, 48 + tt:48 + tt + 1], None,
                                   ALU.mult))
            dT.wait(V)
            for dc in range(8):
                transpose128(bstage[:, 1024 + dc * 128:1024 + dc * 128
                                    + 128], pTb[:, 0:128],
                             coarse=(dc == 0))
                sV.inc(V.tensor_scalar(
                    bstage[:, 2048 + dc * 128:2048 + (dc + 1) * 128],
                    pTb[:, 0:128], cwrm[:, dc:dc + 1], None, ALU.mult))
            vflush()
            ywait(sV)
            dT.start(SY,
                out=xnm_sp[:, :, tt * 128:(tt + 1) * 128]
                    .rearrange("c p n -> p c n"),
                in_=bstage[:, 2048:3072].rearrange("p (c n) -> p c n",
                                                   n=128))
        barrier()

        if int(os.environ.get("DBGF", "0")) == 8:
            vwait(sSA)
            sV.inc(V.tensor_copy(xtile[:, 0:64], stats[:, 0:64]))
            sV.inc(V.tensor_copy(xtile[:, 64:192], smc[:, 0:128]))
            sV.inc(V.tensor_copy(xtile[:, 192:240], qs_u[:, 0:48]))
            vflush()
            ywait(sV)
            dS.start(SY, out=dbg[:, 0:240], in_=xtile[:, 0:240])
            dS.wait(SY)
        if KP5 < 3:
            bailout()
            return
        gvd_ld = {}
        for cc in range(8):
            sl = cc % 2
            if sl in gvd_ld:
                gwait(sMM, gvd_ld[sl])
            wga = wbuf[:, sl * 4096:sl * 4096 + 8 * 384] \
                .rearrange("p (c n) -> p c n", n=384)
            dG.start(G, 
                out=wga[:, :, 0:128],
                in_=Wg1[:, cc * 128:(cc + 1) * 128]
                    .rearrange("(c p) n -> p c n", p=128))
            dG.start(G, 
                out=wga[:, :, 128:256],
                in_=Wv1[:, cc * 128:(cc + 1) * 128]
                    .rearrange("(c p) n -> p c n", p=128))
            dG.start(G, 
                out=wga[:, :, 256:384],
                in_=Wd1[:, cc * 128:(cc + 1) * 128]
                    .rearrange("(c p) n -> p c n", p=128))
            dG.wait(T)
            for st4 in range(4):
                ywait(sMM)
                dS.start(SY, 
                    out=xnt4[:].rearrange("p (c n) -> p c n", n=512),
                    in_=xnm_sp[:, :, st4 * 512:(st4 + 1) * 512]
                        .rearrange("c p n -> p c n"))
                xnt = xnt4[:].rearrange("p (c n) -> p c n", n=512)
                dS.wait(T)
                slS.acquire(T)
                slS2.acquire(T)
                mm_acc(psS[:, 0:512],
                       [(wga[:, dc, 0:128], xnt[:, dc, :])
                        for dc in range(8)])
                mm_acc(psS[:, 512:1024],
                       [(wga[:, dc, 128:256], xnt[:, dc, :])
                        for dc in range(8)])
                mm_acc(psS2[:, 0:512],
                       [(wga[:, dc, 256:384], xnt[:, dc, :])
                        for dc in range(8)])
                gvd_ld[sl] = sMM.n
                swait(sMM)
                swait(sV)
                sSA.inc(SCL.activation(pTb[:, 0:512], psS[:, 0:512],
                                       AF.Sigmoid, bias=cbg[:, cc:cc + 1]))
                sSA.inc(SCL.activation(pTb[:, 512:1024], psS[:, 512:1024],
                                       AF.Tanh, bias=cbv1[:, cc:cc + 1]))
                sSA.inc(SCL.activation(fs1[:, 0:512], psS2[:, 0:512],
                                       AF.Sigmoid, bias=cbd[:, cc:cc + 1]))
                slS.consumed(sSA)
                slS2.consumed(sSA)
                vwait(sSA)
                sV.inc(V.tensor_mul(xs_c[:, st4 * 512:(st4 + 1) * 512],
                                    pTb[:, 0:512], pTb[:, 512:1024]))
                sV.inc(V.tensor_scalar(a_c[:, st4 * 512:(st4 + 1) * 512],
                                       fs1[:, 0:512], 0.998, 0.001,
                                       ALU.mult, ALU.add))
            vwait(sMM)
            sV.inc(V.tensor_tensor_scan(h_c[:], a_c[:], xs_c[:], 0.0,
                                        ALU.mult, ALU.add))
            vwait(sV)
            sV.inc(V.tensor_copy(carry[:, cc:cc + 1], h_c[:, 2047:2048]))
            sV.inc(V.tensor_tensor_scan(
                acum[:, cc * T_OWN:(cc + 1) * T_OWN], a_c[:], a_c[:], 1.0,
                ALU.mult, ALU.bypass))
            vwait(sV)
            sV.inc(V.tensor_copy(xs_c[:], h_c[:]))
            for tt in range(NT_OWN):
                transpose128(xs_c[:, tt * 128:(tt + 1) * 128], pTb[:, 0:128],
                             coarse=(tt == 0))
                dst = acc[:, tt * D + cc * 128:tt * D + cc * 128 + 128]
                sV.inc(V.tensor_add(dst, pTb[:, 0:128], dst))
        barrier()
        if KP5 < 4:
            bailout()
            return
        dG.start(G, out=cin_b, in_=carry[:])
        dG.wait(G)
        sCC.inc(G.collective_compute(
            "AllGather", ALU.bypass, replica_groups=[list(range(8))],
            ins=[cin_b.opt()], outs=[cout_b.opt()]))
        gwait(sCC)
        dG.start(G, 
            out=crecv[:].rearrange("p (r c) -> p r c", c=8),
            in_=cout_b.rearrange("(r p) c -> p r c", p=128))
        dG.wait(V)
        sV.inc(V.tensor_mul(crecv[:], crecv[:], csel[:]))
        vwait(sV)
        sV.inc(V.tensor_reduce(cin_sb[:], crecv[:]
                               .rearrange("p (r c) -> p c r", c=8),
                               AX.X, ALU.add))
        vwait(sV)
        for cc in range(8):
            vwait(sMM)
            sV.inc(V.tensor_scalar(xs_c[:],
                                   acum[:, cc * T_OWN:(cc + 1) * T_OWN],
                                   cin_sb[:, cc:cc + 1], None, ALU.mult))
            for tt in range(NT_OWN):
                transpose128(xs_c[:, tt * 128:(tt + 1) * 128], pTb[:, 0:128],
                             coarse=(tt == 0))
                dst = acc[:, tt * D + cc * 128:tt * D + cc * 128 + 128]
                sV.inc(V.tensor_add(dst, pTb[:, 0:128], dst))
        barrier()
        if KP5 < 5:
            bailout()
            return
        for tt in range(NT_OWN):
            swait(sV)
            swait(sSA)
            sSA.inc(SCL.activation(junk[:], acc[:, tt * D:(tt + 1) * D],
                                   AF.Square, accum_out=smc[:, tt:tt + 1]))
        sflush()
        vwait(sSA)
        sV.inc(V.tensor_scalar(smc[:, 32:48], smc[:, 0:16], 1.0 / D, 1e-6,
                               ALU.mult, ALU.add))
        vwait(sV)
        sV.inc(V.reciprocal(smc[:, 64:80], smc[:, 32:48]))
        vwait(sV)
        vflush()
        vwait(sV)
        swait(sV)
        sSA.inc(SCL.activation(stats[:, 0:16], smc[:, 64:80], AF.Sqrt))
        sflush()
        barrier()

        def emit_ffn(variant):
            ffn_ev = {}
            prod_v = [0]
            for sup in range(4):
                if variant == "A":
                    for q in range(4):
                        tt = sup * 4 + q
                        vwait(sMM)
                        sV.inc(V.tensor_scalar(
                            bstage[:, 0:D], acc[:, tt * D:(tt + 1) * D],
                            stats[:, tt:tt + 1], None, ALU.mult))
                        for dc in range(8):
                            transpose128(bstage[:, dc * 128:(dc + 1) * 128],
                                         pTb[:, 0:128], coarse=(dc == 0))
                            sV.inc(V.tensor_scalar(
                                xnt4[:, dc * 512 + q * 128:
                                     dc * 512 + q * 128 + 128],
                                pTb[:, 0:128], cwrf[:, dc:dc + 1], None,
                                ALU.mult))
                        Slot.last_v[0] = sV.n
                else:
                    ywait(sMM)
                    dS.start(SY, 
                        out=xnt4[:].rearrange("p (c n) -> p c n", n=512),
                        in_=xn1_sp[:, :, sup * 512:(sup + 1) * 512]
                            .rearrange("c p n -> p c n"))
                xnt = xnt4[:].rearrange("p (c n) -> p c n", n=512)
                for fh in range(2):
                    if variant == "A":
                        gwait(sMM)
                    else:
                        ywait(sMM)
                    for f12 in range(12):
                        f = fh * 12 + f12
                        if variant == "A":
                            dG.start(G, 
                                out=bigB[:, f12 * 1024:(f12 + 1) * 1024],
                                in_=Wout1[f * 128:(f + 1) * 128, :])
                        else:
                            dS.start(SY, 
                                out=bigB[:, f12 * 1024:(f12 + 1) * 1024],
                                in_=wdiT[f, :, :])
                    for f12 in range(12):
                        f = fh * 12 + f12
                        sl = f12 % 2
                        wgu = wbuf[:, sl * 4096:sl * 4096 + 8 * 256] \
                            .rearrange("p (c n) -> p c n", n=256)
                        if variant == "A":
                            gwait(sMM)
                            dG.start(G, 
                                out=wgu[:, :, 0:128],
                                in_=Wgate1[:, f * 128:(f + 1) * 128]
                                    .rearrange("(c p) n -> p c n", p=128))
                            dG.start(G, 
                                out=wgu[:, :, 128:256],
                                in_=Wup1[:, f * 128:(f + 1) * 128]
                                    .rearrange("(c p) n -> p c n", p=128))
                            dG.wait(T)
                        else:
                            ywait(sMM)
                            dS.start(SY, 
                                out=wgu[:, :, 0:128],
                                in_=wgiT[:, :, f * 128:(f + 1) * 128]
                                    .rearrange("c p n -> p c n"))
                            dS.start(SY, 
                                out=wgu[:, :, 128:256],
                                in_=wuiT[:, :, f * 128:(f + 1) * 128]
                                    .rearrange("c p n -> p c n"))
                            dS.wait(T)
                        twait(sV, prod_v[0])
                        slS.acquire(T)
                        mm_acc(psS[:, 0:512],
                               [(wgu[:, dc, 0:128], xnt[:, dc, :])
                                for dc in range(8)])
                        mm_acc(psS[:, 512:1024],
                               [(wgu[:, dc, 128:256], xnt[:, dc, :])
                                for dc in range(8)])
                        swait(sMM)
                        swait(sV)
                        sSA.inc(SCL.activation(
                            pTb[:, 0:512], psS[:, 0:512], AF.Silu,
                            scale=(1.0 if variant == "A" else 1.0 / 127.0)))
                        slS.consumed(sSA)
                        vwait(sMM)
                        vwait(sSA)
                        sV.inc(V.scalar_tensor_tensor(
                            bigA[:, f12 * 512:(f12 + 1) * 512],
                            psS[:, 512:1024],
                            (1.0 if variant == "A" else 1.0 / 16129.0),
                            pTb[:, 0:512], ALU.mult, ALU.mult))
                        slS.consumed(sV)
                        prod_v[0] = sV.n
                    prod_all = sV.n
                    if (int(os.environ.get("DBGF", "0")) == 12
                            and variant == "A" and sup == 0
                            and fh == int(os.environ.get("DBGFH", "0"))):
                        dG.wait(T)
                        dS.wait(T)
                        gwait(sMM)
                        dG.start(G, out=dbg[:, 0:8192], in_=bigB[:, 0:8192])
                        dG.wait(V)
                        dG.wait(T)
                    if (int(os.environ.get("DBGF", "0")) in (10, 11)
                            and variant == ("B" if int(os.environ.get(
                                "DBGF", "0")) == 10 else "A") and sup == 0
                            and fh == int(os.environ.get("DBGFH", "0"))):
                        vflush()
                        gwait(sV)
                        dG.start(G, out=dbg[:, 0:6144], in_=bigA[:, 0:6144])
                        dG.wait(V)
                        dG.wait(T)
                    dS.wait(T)
                    dG.wait(T)
                    for tt4 in range(4):
                        tt = sup * 4 + tt4
                        for dh in range(2):
                            pk, sl_ = (psY0, slY0) \
                                if (tt4 * 2 + dh) % 2 == 0 else (psY1, slY1)
                            twait(sV, prod_all)
                            mm_acc(pk[:, 0:512],
                                   [(bigA[:, f12 * 512 + tt4 * 128:
                                          f12 * 512 + tt4 * 128 + 128],
                                     bigB[:, f12 * 1024 + dh * 512:
                                          f12 * 1024 + (dh + 1) * 512])
                                    for f12 in range(12)], slot=sl_)
                            vwait(sMM)
                            if variant == "A":
                                dst = acc[:, tt * D + dh * 512:
                                          tt * D + (dh + 1) * 512]
                                sV.inc(V.tensor_add(dst, pk[:, 0:512], dst))
                            else:
                                dst = arenaF[:, tt4 * D + dh * 512:
                                             tt4 * D + (dh + 1) * 512]
                                if fh == 0:
                                    dT.wait(V)
                                    sV.inc(V.tensor_add(
                                        dst, pk[:, 0:512],
                                        acc[:, tt * D + dh * 512:
                                            tt * D + (dh + 1) * 512]))
                                else:
                                    sV.inc(V.tensor_add(dst, pk[:, 0:512],
                                                        dst))
                                    sV.inc(V.tensor_add(
                                        dst, dst,
                                        resid[:, tt * D + dh * 512:
                                              tt * D + (dh + 1) * 512]))
                                    if dh == 1:
                                        vflush()
                                        ywait(sV)
                                        dT.start(SY,
                                            out=yout[tt * 128:
                                                     (tt + 1) * 128, :],
                                            in_=arenaF[:, tt4 * D:
                                                       (tt4 + 1) * D])
                            sl_.consumed(sV)

        if KPH < 8:
            bailout()
            return
        emit_ffn("A")
        barrier()
        if int(os.environ.get("DBGF", "0")) == 13:
            dG.start(G, out=dbg[:, 0:8192], in_=acc[:, 0:8192])
            dG.wait(V)
            dG.wait(T)
        if KPH < 9:
            bailout()
            return
        emit_ffn("B")
        dT.wait(SY)
        dT.wait(G)
        dT.wait(V)
        dT.wait(SCL)
        dT.wait(T)

    ctx.close()
    return nc


def _prep_inputs(inputs):
    x = np.asarray(inputs["L0_attn_x"])
    f32 = np.float32
    bf16 = ml_dtypes.bfloat16

    def pp(v):
        return np.ascontiguousarray(np.asarray(v, dtype=f32).reshape(8, 128).T)

    def bcast(v):
        return np.ascontiguousarray(
            np.broadcast_to(np.asarray(v, dtype=f32)[None, :], (128, D)))

    wc = np.asarray(inputs["L0_attn_Wc"], dtype=f32)
    w2 = np.zeros((128, 128), f32)
    w2[0:64, 0:64] = wc
    w2[64:128, 64:128] = wc

    ident = np.eye(128, dtype=f32)
    shift_id = np.zeros((64, 128), f32)
    shift_id[np.arange(64), 64 + np.arange(64)] = 1.0

    tm = np.zeros((128, 4, 512), f32)
    for d_ in range(4):
        for kr in range(128):
            lim = d_ * 128 + kr
            tm[kr, d_, :lim] = KILL
    tm = np.ascontiguousarray(tm.reshape(128, 4 * 512))

    base = dict(
        Wq=np.ascontiguousarray(inputs["L0_attn_Wq"], f32),
        Wk=np.ascontiguousarray(inputs["L0_attn_Wk"], f32),
        Wv=np.ascontiguousarray(inputs["L0_attn_Wv"], f32),
        Wo=np.ascontiguousarray(inputs["L0_attn_Wo"], f32),
        w2blk=w2,
        bq_pp=pp(inputs["L0_attn_bq"]),
        bk_pp=pp(inputs["L0_attn_bk"]),
        bo_b=bcast(inputs["L0_attn_bo"]).astype(bf16),
        bv_b=bcast(inputs["L0_attn_bv"]).astype(bf16),
        wgate0=np.ascontiguousarray(inputs["L0_ffn_w_gate"], f32),
        wup0=np.ascontiguousarray(inputs["L0_ffn_w_up"], f32),
        wdown0=np.ascontiguousarray(inputs["L0_ffn_w_down"], f32),
        wrms_mix_pp=pp(inputs["L1_attn_w_rms_mix"]),
        wrms_ffn_pp=pp(inputs["L1_attn_w_rms_ffn"]),
        Wg1=np.ascontiguousarray(inputs["L1_attn_Wg"], f32),
        Wv1=np.ascontiguousarray(inputs["L1_attn_Wv"], f32),
        Wd1=np.ascontiguousarray(inputs["L1_attn_Wd"], f32),
        bg_pp=pp(inputs["L1_attn_bg"]),
        bv1_pp=pp(inputs["L1_attn_bv"]),
        bd_pp=pp(inputs["L1_attn_bd"]),
        Wgate1=np.ascontiguousarray(inputs["L1_attn_W_gate"], f32),
        Wup1=np.ascontiguousarray(inputs["L1_attn_W_up"], f32),
        Wout1=np.ascontiguousarray(inputs["L1_attn_W_out"], f32),
        wgi=np.ascontiguousarray(inputs["L1_ffn_w_gate"], np.int32),
        wui=np.ascontiguousarray(inputs["L1_ffn_w_up"], np.int32),
        wdi=np.ascontiguousarray(inputs["L1_ffn_w_down"], np.int32),
        ident_bf=ident.astype(bf16),
        ident_f32=ident,
        shift_id=shift_id.astype(bf16),
        onesf=np.ones((128, 128), f32),
        trimask=tm.astype(bf16),
    )

    in_maps = []
    for c in range(8):
        b, h = c // 2, c % 2
        if h == 0:
            xb = np.concatenate([np.zeros((T_OWN, D), f32),
                                 np.asarray(x[b, 0:T_OWN], f32)], axis=0)
        else:
            xb = np.ascontiguousarray(np.asarray(x[b], f32))
        bias = np.full((128, NT_BUF), EXP_SHIFT, f32)
        if h == 0:
            bias[:, 0:16] = KILL + EXP_SHIFT
        sel = np.zeros((128, 64), f32)
        if h == 1:
            partner = c - 1
            sel[:, partner * 8:(partner + 1) * 8] = 1.0
        m = dict(base)
        m.update(xin=xb, bias_tbl=bias, sel=sel)
        in_maps.append(m)
    return in_maps


def kernel(**inputs):
    if "nc" not in _CACHE:
        _CACHE["nc"] = build()
    nc = _CACHE["nc"]
    in_maps = _prep_inputs(inputs)
    res = run_bass_kernel_spmd(nc, in_maps, core_ids=list(range(8)))
    out = np.empty((4, S_BUF, D), np.float32)
    for c in range(8):
        b, h = c // 2, c % 2
        out[b, h * T_OWN:(h + 1) * T_OWN, :] = res.results[c]["yout"]
    return out
